# version 10
# baseline (speedup 1.0000x reference)
import numpy as np
from contextlib import ExitStack

import concourse.bass as bass
import concourse.mybir as mybir
from concourse.bass_utils import run_bass_kernel_spmd

F32 = mybir.dt.float32
F16 = mybir.dt.float16
AF = mybir.ActivationFunctionType
ALU = mybir.AluOpType
AXX = mybir.AxisListType.X

MAGIC = float(1.5 * 2 ** 23)
T, D, I3 = 2048, 1024, 3072
NT, NCH, NIC = T // 128, D // 128, I3 // 128
NCORE = 8

_DEV = {"taps": False, "stage": 99}


def build_program():
    taps = _DEV["taps"]
    SPLIT = _DEV.get("split", False)
    nc = bass.Bass("TRN2", target_bir_lowering=False, debug=False,
                   detect_race_conditions=False)

    x_h = nc.dram_tensor("x", [T, D], F32, kind="ExternalInput").ap()
    w_h = {}
    for nm in ("Wq", "Wk", "Wv", "Wo"):
        w_h[nm] = nc.dram_tensor(nm + "_w", [D, D], F32, kind="ExternalInput").ap()
        w_h[nm + "_dW"] = nc.dram_tensor(nm + "_dW", [32, 16], F32, kind="ExternalInput").ap()
        w_h[nm + "_dB"] = nc.dram_tensor(nm + "_dB", [D, 32], F32, kind="ExternalInput").ap()
    wup_h = nc.dram_tensor("w_up", [I3, D], F32, kind="ExternalInput").ap()
    wg_h = nc.dram_tensor("w_gate", [I3, D], F32, kind="ExternalInput").ap()
    wd_h = nc.dram_tensor("w_down", [D, I3], F32, kind="ExternalInput").ap()
    hsel_h = nc.dram_tensor("c_hsel", [16, D], F16, kind="ExternalInput").ap()
    out_h = nc.dram_tensor("out", [T, D], F32, kind="ExternalOutput").ap()

    def scratch(name, shape, dt):
        if taps:
            return nc.dram_tensor(name, shape, dt, kind="ExternalOutput").ap()
        return nc.dram_tensor(name, shape, dt).ap()

    q_hbm = scratch("q_sp", [NCH, 128, T], F16)
    k_hbm = scratch("k_sp", [NCH, 128, T], F16)
    xmid_hbm = scratch("xmid_sp", [NT, 128, D], F32)
    up_hbm = scratch("up_sp", [NIC, 128, T], F16)
    attnfm_hbm = scratch("attnfm_sp", [NCH, 128, T], F16) if taps else None
    gate_hbm = scratch("gate_sp", [NIC, 128, T], F16) if taps else None
    lscr = nc.dram_tensor("lscr", [128, 2], F32).ap()
    lscr2 = nc.dram_tensor("lscr2", [2, 1], F32).ap()
    cc_in = [nc.dram_tensor(f"cc_in{r}", [1, 8], F32).ap() for r in range(4)]
    cc_out = [nc.dram_tensor(f"cc_out{r}", [1, 8], F32).ap() for r in range(4)]

    ctx = ExitStack()
    sub = {}
    alloc_order = []

    class _Stop(Exception):
        pass

    TAP = {}

    def sb(name, shape, dt, scoped=False, side=None):
        if scoped:
            es = ExitStack()
            t = es.enter_context(nc.sbuf_tensor(name, shape, dt, side=side))
            sub[name] = es
            alloc_order.append(name)
            TAP[name] = t
            return t
        t = ctx.enter_context(nc.sbuf_tensor(name, shape, dt, side=side))
        TAP[name] = t
        return t

    def freesb(*names):
        for n in names:
            sub.pop(n).close()
            alloc_order.remove(n)

    hsel = sb("hsel", [16, D], F16)
    ones1 = sb("ones1", [1, 128], F32)
    scal = sb("scal", [128, 12], F32)
    s_cst = sb("s_cst", [128, 2], F32)
    nc.const_aps.register(F32, MAGIC, s_cst[:, 0:1])
    nc.const_aps.register(F32, 1e-5, s_cst[:, 1:2])
    s_sum = sb("s_sum", [128, NT], F32)
    s_ssq = sb("s_ssq", [128, NT], F32)
    s_m = sb("s_m", [128, NT], F32)
    s_a = sb("s_a", [128, NT], F32)
    s_b = sb("s_b", [128, NT], F32)
    s_t0 = sb("s_t0", [128, NT], F32)
    lmax = sb("lmax", [128, 2], F32)
    s_mx = sb("s_mx", [128, 24], F32)
    junk16 = sb("junk16", [128, 1024], F16)
    s_h2j = sb("s_h2j", [128, 1024], F16)
    ccsb = sb("ccsb", [1, 16], F32)
    cc128 = sb("cc128", [2, 128], F32)
    s_dst = sb("s_dst", [1, 4, 512], F32)
    dwraw = sb("dwraw", [32, 16], F32)
    dwq = sb("dwq", [32, 16], F16)
    dbraw = sb("dbraw", [128, 8, 32], F32)
    dbq = sb("dbq", [128, 8, 32], F16)
    s_tb1 = sb("s_tb1", [128, 128], F16)
    s_tb2 = sb("s_tb2", [128, 128], F16)
    dbT = sb("dbT", [32, D], F16)
    deltaT = sb("deltaT", [16, D], F16)
    corr_sb = sb("corr_sb", [128, NCH, D], F16)

    psum = [ctx.enter_context(nc.psum_tensor(f"pb{i}", [128, 512], F32)) for i in range(8)]

    SEM = {}
    for s in ("lc", "ls", "A0", "A1", "B0", "B1", "D0", "D1", "D2", "D3",
              "G0", "G1", "GC", "GH", "T0", "T1", "tn", "vv", "sc", "bar", "ccs"):
        SEM[s] = [ctx.enter_context(nc.semaphore(s)), 0]

    def inc(instr, name, n=1):
        SEM[name][1] += n
        instr.then_inc(SEM[name][0], n)
        return SEM[name][1]

    def val(name):
        return SEM[name][1]

    def wait(e, name, v=None):
        e.wait_ge(SEM[name][0], SEM[name][1] if v is None else v)

    def sw(e, instr, sem):
        i = inc(instr, sem)
        wait(e, sem, i)
        return i

    blk = ctx.enter_context(nc.Block())

    def maybe_stop(n):
        if _DEV.get("stage", 99) != n:
            return
        dmp = _DEV.get("dump")

        @blk.gpsimd
        def _(e):
            if dmp:
                outv = out_h[:, :].rearrange("(a b) d -> a (b d)", a=128)
                off = 0
                i = None
                for name in dmp.split(","):
                    t = TAP[name]
                    sh = list(t.shape)
                    p = sh[0]
                    fl = 1
                    for d in sh[1:]:
                        fl *= d
                    src = t[:] if len(sh) == 2 else t[:].rearrange("p a b -> p (a b)")
                    i = inc(e.dma_start(out=outv[0:p, off:off + fl], in_=src), "G0", 16)
                    off += fl
            else:
                i = inc(e.dma_start(out=out_h[0:128, 0:16], in_=s_sum[:, :]), "G0", 16)
            wait(e, "G0", i)
        for nm in reversed(list(alloc_order)):
            sub.pop(nm).close()
        alloc_order.clear()
        raise _Stop()

    ENG = ("sync", "gpsimd", "vector", "scalar", "tensor")
    barn = [0]

    def barrier():
        barn[0] += 1
        tgt = 5 * barn[0]
        for en in ENG:
            @getattr(blk, en)
            def _(e, en=en):
                if en == "tensor":
                    e.drain().then_inc(SEM["bar"][0], 1)
                else:
                    e.nop().then_inc(SEM["bar"][0], 1)
        for en in ENG:
            @getattr(blk, en)
            def _(e):
                e.wait_ge(SEM["bar"][0], tgt)

    def drainT():
        @blk.sync
        def _(e):
            wait(e, "T0", val("T0"))
            wait(e, "T1", val("T1"))

    with ctx:
      try:
        @blk.sync
        def _(e):
            inc(e.dma_start(out=hsel[:], in_=hsel_h[:, :]), "lc", 16)
        ld_consts = val("lc")

        @blk.vector
        def _(e):
            e.memset(ones1[:], 1.0)
            e.memset(s_cst[:, 0:1], MAGIC)
            e.memset(s_cst[:, 1:2], 1e-5)
            e.memset(s_tb1[:, :], 0.0)
            inc(e.memset(lmax[:], 0.0), "vv")
        maybe_stop(0)

        s_xt = sb("s_xt", [128, 2, 1024], F32, scoped=True)
        s_wtB = sb("s_wtB", [128, NCH, D], F16, scoped=True)
        s_wraw = sb("s_wraw", [128, 2, 1024], F32, scoped=True)
        s_wq16 = sb("s_wq16", [128, 2, 1024], F16, scoped=True)
        s_wtA = sb("s_wtA", [128, NCH, D], F16, scoped=True)
        s_h = sb("s_h", [128, NCH, T], F16, scoped=True)
        s_htm = sb("s_htm", [128, 2, 1024], F16, scoped=True)

        ln_ld, ln_sc1, ln_v1, ln_sc2, ln_v2, ln_tt = [], [], [], [], [], []
        for tc in range(NT):
            @blk.scalar
            def _(e, tc=tc):
                if tc >= 2:
                    wait(e, "vv", ln_v2[tc - 2])
                ln_ld.append(inc(e.dma_start(out=s_xt[:, tc % 2, :],
                                             in_=x_h[tc * 128:(tc + 1) * 128, :]),
                                 f"A{tc % 2}", 16))

            @blk.scalar
            def _(e, tc=tc):
                wait(e, f"A{tc % 2}", ln_ld[tc])
                ln_sc1.append(inc(e.activation(out=junk16[:, :], in_=s_xt[:, tc % 2, :],
                                               func=AF.Square, accum_out=s_ssq[:, tc:tc + 1]),
                                  "sc"))

            @blk.vector
            def _(e, tc=tc):
                wait(e, f"A{tc % 2}", ln_ld[tc])
                sw(e, e.reduce_sum(s_sum[:, tc:tc + 1], s_xt[:, tc % 2, :], AXX), "vv")
                sw(e, e.tensor_scalar(out=s_m[:, tc:tc + 1], in0=s_sum[:, tc:tc + 1],
                                      scalar1=1.0 / D, scalar2=None, op0=ALU.mult), "vv")
                sw(e, e.tensor_mul(s_t0[:, tc:tc + 1], s_m[:, tc:tc + 1],
                                   s_m[:, tc:tc + 1]), "vv")
                wait(e, "sc", ln_sc1[tc])
                ln_v1.append(inc(e.scalar_tensor_tensor(out=s_t0[:, tc:tc + 1],
                                                        in0=s_ssq[:, tc:tc + 1],
                                                        scalar=1.0 / D, in1=s_t0[:, tc:tc + 1],
                                                        op0=ALU.mult, op1=ALU.subtract), "vv"))

            @blk.scalar
            def _(e, tc=tc):
                wait(e, "vv", ln_v1[tc])
                ln_sc2.append(inc(e.activation(out=s_t0[:, tc:tc + 1], in_=s_t0[:, tc:tc + 1],
                                               func=AF.Sqrt, bias=1e-5, scale=1.0), "sc"))

            @blk.vector
            def _(e, tc=tc):
                wait(e, "sc", ln_sc2[tc])
                sw(e, e.reciprocal(out=s_a[:, tc:tc + 1], in_=s_t0[:, tc:tc + 1]), "vv")
                sw(e, e.scalar_tensor_tensor(out=s_b[:, tc:tc + 1], in0=s_m[:, tc:tc + 1],
                                             scalar=-1.0, in1=s_a[:, tc:tc + 1],
                                             op0=ALU.mult, op1=ALU.mult), "vv")
                if tc >= 2:
                    wait(e, f"T{tc % 2}", ln_tt[tc - 2])
                ln_v2.append(inc(e.tensor_scalar(out=s_htm[:, tc % 2, :],
                                                 in0=s_xt[:, tc % 2, :],
                                                 scalar1=s_a[:, tc:tc + 1],
                                                 scalar2=s_b[:, tc:tc + 1],
                                                 op0=ALU.mult, op1=ALU.add), "vv"))

            @blk.sync
            def _(e, tc=tc):
                wait(e, "vv", ln_v2[tc])
                for cc in range(0, NCH, 2) if SPLIT else range(NCH):
                    inc(e.dma_start_transpose(s_h[:, cc, tc * 128:(tc + 1) * 128],
                                              s_htm[:, tc % 2, cc * 128:(cc + 1) * 128]),
                        f"T{tc % 2}", 16)

            if SPLIT:
                @blk.scalar
                def _(e, tc=tc):
                    wait(e, "vv", ln_v2[tc])
                    for cc in range(1, NCH, 2):
                        inc(e.dma_start_transpose(s_h[:, cc, tc * 128:(tc + 1) * 128],
                                                  s_htm[:, tc % 2, cc * 128:(cc + 1) * 128]),
                            f"T{tc % 2}", 16)
            ln_tt.append(val(f"T{tc % 2}"))
        drainT()
        barrier()
        maybe_stop(1)

        wpcut = [_DEV.get("wprep_cut")]

        def wprep(nm, wt):
            m = {}

            def cut(tag):
                if wpcut[0] == tag:
                    wpcut[0] = None
                    maybe_stop(2)

            @blk.sync
            def _(e):
                inc(e.dma_start(out=dwraw[:], in_=w_h[nm + "_dW"][:, :]), "ls", 16)
                m["lds"] = inc(e.dma_start(
                    out=dbraw[:, :, :],
                    in_=w_h[nm + "_dB"][:, :].rearrange("(a p) b -> p a b", p=128)), "ls", 16)

            @blk.vector
            def _(e):
                wait(e, "ls", m["lds"])
                sw(e, e.tensor_scalar(out=dwraw[:], in0=dwraw[:], scalar1=10.0, scalar2=MAGIC,
                                      op0=ALU.mult, op1=ALU.add), "vv")
                e.tensor_scalar(out=dwq[:], in0=dwraw[:], scalar1=MAGIC, scalar2=0.1,
                                op0=ALU.subtract, op1=ALU.mult)
                sw(e, e.tensor_scalar(out=dbraw[:].rearrange("p a b -> p (a b)"),
                                      in0=dbraw[:].rearrange("p a b -> p (a b)"),
                                      scalar1=10.0, scalar2=MAGIC,
                                      op0=ALU.mult, op1=ALU.add), "vv")
                m["vq"] = inc(e.tensor_scalar(out=dbq[:].rearrange("p a b -> p (a b)"),
                                              in0=dbraw[:].rearrange("p a b -> p (a b)"),
                                              scalar1=MAGIC, scalar2=0.1,
                                              op0=ALU.subtract, op1=ALU.mult), "vv")

            vbase = val("vv")
            for g in range(8):
                @blk.vector
                def _(e, g=g, vbase=vbase):
                    if g == 0:
                        wait(e, "vv", m["vq"])
                    else:
                        wait(e, "T0", m[f"tb{g - 1}"])
                    inc(e.tensor_copy(s_tb1[:, 0:32], dbq[:, g, :]), "vv")

                @blk.sync
                def _(e, g=g, vbase=vbase):
                    wait(e, "vv", vbase + 2 * g + 1)
                    m[f"tb{g}"] = inc(e.dma_start_transpose(s_tb2[:, :], s_tb1[:, :]),
                                      "T0", 16)

                @blk.vector
                def _(e, g=g):
                    wait(e, "T0", m[f"tb{g}"])
                    inc(e.tensor_copy(dbT[:, g * 128:(g + 1) * 128], s_tb2[0:32, :]), "vv")
            m["vq2"] = val("vv")
            cut("smalls")

            @blk.tensor
            def _(e):
                wait(e, "vv", m["vq2"])
                e.matmul(psum[6][0:16, :], dwq[:], dbT[:, 0:512], start=True, stop=True)
                m["t2"] = inc(e.matmul(psum[7][0:16, :], dwq[:], dbT[:, 512:1024],
                                       start=True, stop=True), "tn")

            @blk.scalar
            def _(e):
                wait(e, "tn", m["t2"])
                e.activation(out=deltaT[:, 0:512], in_=psum[6][0:16, :], func=AF.Copy,
                             bias=0.0, scale=1.0 / 32.0)
                m["s1"] = inc(e.activation(out=deltaT[:, 512:1024], in_=psum[7][0:16, :],
                                           func=AF.Copy, bias=0.0, scale=1.0 / 32.0), "sc")
            cut("delta")

            vv_corr0 = val("vv")
            for cc in range(NCH):
                m[f"corr{cc}"] = vv_corr0 + cc + 1

            @blk.tensor
            def _(e):
                wait(e, "lc", ld_consts)
                wait(e, "sc", m["s1"])
                for cc in range(NCH):
                    b0, b1 = (0, 1) if cc % 2 == 0 else (2, 3)
                    if cc >= 2:
                        wait(e, "vv", m[f"corr{cc - 2}"])
                    e.matmul(psum[b0][:, :], hsel[:, cc * 128:(cc + 1) * 128],
                             deltaT[:, 0:512], start=True, stop=True)
                    m[f"corrt{cc}"] = inc(e.matmul(psum[b1][:, :],
                                                   hsel[:, cc * 128:(cc + 1) * 128],
                                                   deltaT[:, 512:1024],
                                                   start=True, stop=True), "tn")

            @blk.vector
            def _(e):
                for cc in range(NCH):
                    b0, b1 = (0, 1) if cc % 2 == 0 else (2, 3)
                    wait(e, "tn", m[f"corrt{cc}"])
                    e.tensor_copy(corr_sb[:, cc, 0:512], psum[b0][:, :])
                    assert inc(e.tensor_copy(corr_sb[:, cc, 512:1024],
                                             psum[b1][:, :]), "vv") == m[f"corr{cc}"]
            cut("corr")

            ld_w, vq_w, tt_w, ev_w = [], [], [], []
            for oc in range(NCH):
                @blk.sync
                def _(e, oc=oc):
                    if oc >= 2:
                        wait(e, "vv", vq_w[oc - 2])
                    ld_w.append(inc(e.dma_start(out=s_wraw[:, oc % 2, :],
                                                in_=w_h[nm][oc * 128:(oc + 1) * 128, :]),
                                    f"B{oc % 2}", 16))

                @blk.vector
                def _(e, oc=oc):
                    wait(e, f"B{oc % 2}", ld_w[oc])
                    if oc >= 2:
                        wait(e, f"T{oc % 2}", tt_w[oc - 2])
                    e.tensor_scalar(out=s_wraw[:, oc % 2, :], in0=s_wraw[:, oc % 2, :],
                                    scalar1=10.0, scalar2=MAGIC, op0=ALU.mult, op1=ALU.add)
                    vq_w.append(inc(e.tensor_scalar(out=s_wq16[:, oc % 2, :],
                                                    in0=s_wraw[:, oc % 2, :],
                                                    scalar1=MAGIC, scalar2=0.1,
                                                    op0=ALU.subtract, op1=ALU.mult), "vv"))

                @blk.sync
                def _(e, oc=oc):
                    wait(e, "vv", vq_w[oc])
                    if oc >= 2:
                        wait(e, "vv", ev_w[oc - 2])
                    for cc in range(0, NCH, 2) if SPLIT else range(NCH):
                        inc(e.dma_start_transpose(wt[:, cc, oc * 128:(oc + 1) * 128],
                                                  s_wq16[:, oc % 2, cc * 128:(cc + 1) * 128]),
                            f"T{oc % 2}", 16)

                if SPLIT:
                    @blk.scalar
                    def _(e, oc=oc):
                        wait(e, "vv", vq_w[oc])
                        if oc >= 2:
                            wait(e, "vv", ev_w[oc - 2])
                        for cc in range(1, NCH, 2):
                            inc(e.dma_start_transpose(wt[:, cc, oc * 128:(oc + 1) * 128],
                                                      s_wq16[:, oc % 2,
                                                             cc * 128:(cc + 1) * 128]),
                                f"T{oc % 2}", 16)
                tt_w.append(val(f"T{oc % 2}"))

                @blk.vector
                def _(e, oc=oc):
                    wait(e, f"T{oc % 2}", tt_w[oc])
                    if oc == 0:
                        wait(e, "vv", m["corr7"])
                    last = None
                    for cc in range(NCH):
                        last = e.tensor_add(wt[:, cc, oc * 128:(oc + 1) * 128],
                                            wt[:, cc, oc * 128:(oc + 1) * 128],
                                            corr_sb[:, cc, oc * 128:(oc + 1) * 128])
                    ev_w.append(inc(last, "vv"))
            drainT()
            barrier()

        _pqn = [0]

        def proj_spill(wt, dst_hbm):
            _pqn[0] += 1
            pqname = f"s_pq{_pqn[0]}"
            s_pq = sb(pqname, [128, 2, T], F16, scoped=True, side="right")
            tn_p, ev_p, st_p = [], [], []
            for oc in range(NCH):
                @blk.tensor
                def _(e, oc=oc):
                    if oc >= 2:
                        wait(e, "sc", ev_p[oc - 2])
                    banks = [psum[4 * (oc % 2) + t] for t in range(4)]
                    mm = None
                    for cc in range(NCH):
                        for t in range(4):
                            mm = e.matmul(banks[t][:, :], wt[:, cc, oc * 128:(oc + 1) * 128],
                                          s_h[:, cc, t * 512:(t + 1) * 512],
                                          start=(cc == 0), stop=(cc == NCH - 1))
                    tn_p.append(inc(mm, "tn"))

                @blk.scalar
                def _(e, oc=oc):
                    wait(e, "tn", tn_p[oc])
                    if oc >= 2:
                        wait(e, f"G{oc % 2}", st_p[oc - 2])
                    banks = [psum[4 * (oc % 2) + t] for t in range(4)]
                    last = None
                    for t in range(4):
                        last = e.copy(s_pq[:, oc % 2, t * 512:(t + 1) * 512], banks[t][:, :])
                    ev_p.append(inc(last, "sc"))

                @blk.gpsimd
                def _(e, oc=oc):
                    wait(e, "sc", ev_p[oc])
                    st_p.append(inc(e.dma_start(out=dst_hbm[oc, :, :],
                                                in_=s_pq[:, oc % 2, :]), f"G{oc % 2}", 16))

            @blk.gpsimd
            def _(e):
                wait(e, "G0", val("G0"))
                wait(e, "G1", val("G1"))
            barrier()
            freesb(pqname)

        wprep("Wq", s_wtA)
        maybe_stop(2)
        proj_spill(s_wtA, q_hbm)
        maybe_stop(3)
        wprep("Wk", s_wtB)
        proj_spill(s_wtB, k_hbm)
        wprep("Wv", s_wtA)

        s_v = sb("s_v", [128, NT, 1040], F16, scoped=True, side="right")

        @blk.vector
        def _(e):
            last = None
            for tc in range(NT):
                last = e.memset(s_v[:, tc, :].rearrange("p (h c) -> p h c", c=65)[:, :, 64:65],
                                1.0)
            inc(last, "vv")
        v_ones = val("vv")

        tn_v, ev_v = [], []
        for tc in range(NT):
            @blk.tensor
            def _(e, tc=tc):
                if tc == 0:
                    wait(e, "vv", v_ones)
                if tc >= 2:
                    wait(e, "vv", ev_v[tc - 2])
                banks = [psum[4 * (tc % 2)], psum[4 * (tc % 2) + 1]]
                mm = None
                for cc in range(NCH):
                    for t in range(2):
                        mm = e.matmul(banks[t][:, :], s_h[:, cc, tc * 128:(tc + 1) * 128],
                                      s_wtA[:, cc, t * 512:(t + 1) * 512],
                                      start=(cc == 0), stop=(cc == NCH - 1))
                tn_v.append(inc(mm, "tn"))

            @blk.vector
            def _(e, tc=tc):
                wait(e, "tn", tn_v[tc])
                banks = [psum[4 * (tc % 2)], psum[4 * (tc % 2) + 1]]
                last = None
                for hh in range(16):
                    bank = banks[hh // 8]
                    last = e.tensor_copy(s_v[:, tc, hh * 65:hh * 65 + 64],
                                         bank[:, (hh % 8) * 64:(hh % 8) * 64 + 64])
                ev_v.append(inc(last, "vv"))
        barrier()
        maybe_stop(4)
        freesb("s_htm", "s_h", "s_wtA")
        wprep("Wo", s_wtB)
        maybe_stop(5)

        s_o = sb("s_o", [128, NCH, T], F16, scoped=True)
        s_den = sb("s_den", [16, T], F32, scoped=True)
        s_qh = sb("s_qh", [128, 2, T], F16, scoped=True, side="right")
        s_kh = sb("s_kh", [128, 2, T], F16, scoped=True, side="right")
        s_pt = sb("s_pt", [128, 4, 512], F16, scoped=True, side="right")

        tn0, sc0, vv0 = val("tn"), val("sc"), val("vv")
        tn_qk, tn_pv = {}, {}
        tn_c = tn0
        att_done = []
        for p in range(8):
            for ic in range(4):
                for jc in range(16):
                    tn_c += 1
                    tn_qk[(p, ic, jc)] = tn_c
                    if jc >= 1:
                        tn_c += 1
                        tn_pv[(p, ic, jc - 1)] = tn_c
                tn_c += 1
                tn_pv[(p, ic, 15)] = tn_c
            att_done.append(tn_c)
        exp_sc = {}
        sc_c = sc0
        for p in range(8):
            for ic in range(4):
                for jc in range(16):
                    sc_c += 1
                    exp_sc[(p, ic, jc)] = sc_c
        pv_evac = {}
        vv_c = vv0
        for p in range(8):
            for ic in range(4):
                vv_c += 1
                pv_evac[(p, ic)] = vv_c
        ldA0, ldA1 = val("A0"), val("A1")
        ld_qk = [(ldA0 if p % 2 == 0 else ldA1) + 32 * (p // 2 + 1) for p in range(8)]
        flat_ids = {(p, ic, jc): (p * 4 + ic) * 16 + jc
                    for p in range(8) for ic in range(4) for jc in range(16)}
        inv_flat = {v: k for k, v in flat_ids.items()}

        @blk.sync
        def _(e):
            for p in range(-2, 8):
                if p + 2 < 8:
                    pp = p + 2
                    if pp >= 2:
                        wait(e, "tn", att_done[pp - 2])
                    inc(e.dma_start(out=s_qh[:, pp % 2, :], in_=q_hbm[pp, :, :]),
                        f"A{pp % 2}", 16)
                    inc(e.dma_start(out=s_kh[:, pp % 2, :], in_=k_hbm[pp, :, :]),
                        f"A{pp % 2}", 16)
                if p >= 0:
                    for ic in range(4):
                        fi = p * 4 + ic
                        wait(e, "vv", vv0 + fi + 1)
                        for b in range(2):
                            hh = 2 * p + b
                            d = fi * 2 + b
                            inc(e.dma_start(out=s_den[hh:hh + 1, ic * 512:(ic + 1) * 512],
                                            in_=s_dst[0:1, d % 4, :]), f"D{d % 4}", 16)
        assert val("A0") == ld_qk[6] and val("A1") == ld_qk[7]

        @blk.tensor
        def _(e):
            for p in range(8):
                wait(e, f"A{p % 2}", ld_qk[p])
                hA, hB = 2 * p, 2 * p + 1
                for ic in range(4):
                    fi = p * 4 + ic
                    pvA, pvB = psum[4 + fi % 2], psum[6 + fi % 2]
                    if fi >= 2:
                        wait(e, "vv", pv_evac[((fi - 2) // 4, (fi - 2) % 4)])
                    for jc in range(16):
                        g = flat_ids[(p, ic, jc)]
                        if g >= 2:
                            wait(e, "sc", exp_sc[inv_flat[g - 2]])
                        sA, sB = psum[jc % 2], psum[2 + jc % 2]
                        e.matmul(sA[:, :], s_kh[0:64, p % 2, jc * 128:(jc + 1) * 128],
                                 s_qh[0:64, p % 2, ic * 512:(ic + 1) * 512],
                                 start=True, stop=True, tile_position=(0, 0))
                        inc(e.matmul(sB[:, :], s_kh[64:128, p % 2, jc * 128:(jc + 1) * 128],
                                     s_qh[64:128, p % 2, ic * 512:(ic + 1) * 512],
                                     start=True, stop=True, tile_position=(64, 0)), "tn")
                        if jc >= 1:
                            wait(e, "sc", exp_sc[(p, ic, jc - 1)])
                            e.matmul(pvA[0:65, :], s_v[:, jc - 1, hA * 65:hA * 65 + 65],
                                     s_pt[:, 2 * ((jc - 1) % 2), :],
                                     start=(jc == 1), stop=False)
                            inc(e.matmul(pvB[0:65, :], s_v[:, jc - 1, hB * 65:hB * 65 + 65],
                                         s_pt[:, 2 * ((jc - 1) % 2) + 1, :],
                                         start=(jc == 1), stop=False), "tn")
                    wait(e, "sc", exp_sc[(p, ic, 15)])
                    e.matmul(pvA[0:65, :], s_v[:, 15, hA * 65:hA * 65 + 65],
                             s_pt[:, 2 * (15 % 2), :], start=False, stop=True)
                    inc(e.matmul(pvB[0:65, :], s_v[:, 15, hB * 65:hB * 65 + 65],
                                 s_pt[:, 2 * (15 % 2) + 1, :], start=False, stop=True), "tn")
        assert val("tn") == tn_c, (val("tn"), tn_c)

        @blk.scalar
        def _(e):
            for p in range(8):
                for ic in range(4):
                    for jc in range(16):
                        g = flat_ids[(p, ic, jc)]
                        wait(e, "tn", tn_qk[(p, ic, jc)])
                        if g >= 2:
                            wait(e, "tn", tn_pv[inv_flat[g - 2]])
                        sA, sB = psum[jc % 2], psum[2 + jc % 2]
                        e.activation(out=s_pt[:, 2 * (jc % 2), :], in_=sA[:, :],
                                     func=AF.Exp, bias=0.0, scale=0.125)
                        inc(e.activation(out=s_pt[:, 2 * (jc % 2) + 1, :], in_=sB[:, :],
                                         func=AF.Exp, bias=0.0, scale=0.125), "sc")
        assert val("sc") == sc_c

        @blk.vector
        def _(e):
            for p in range(8):
                hA, hB = 2 * p, 2 * p + 1
                for ic in range(4):
                    fi = p * 4 + ic
                    wait(e, "tn", tn_pv[(p, ic, 15)])
                    pvA, pvB = psum[4 + fi % 2], psum[6 + fi % 2]
                    last = None
                    for b, (hh, pv) in enumerate(((hA, pvA), (hB, pvB))):
                        oc, row = hh // 2, 64 * (hh % 2)
                        e.tensor_copy(s_o[row:row + 64, oc, ic * 512:(ic + 1) * 512],
                                      pv[0:64, :])
                        d = fi * 2 + b
                        if d >= 4:
                            wait(e, f"D{d % 4}", 16 * (d // 4))
                        last = e.tensor_copy(s_dst[0:1, d % 4, :], pv[64:65, :])
                    inc(last, "vv")
        assert val("vv") == vv_c

        barrier()
        maybe_stop(6)
        freesb("s_pt", "s_kh", "s_qh", "s_v")

        s_recip = sb("s_recip", [16, T], F16, scoped=True)

        @blk.vector
        def _(e):
            for dsem in ("D0", "D1", "D2", "D3"):
                wait(e, dsem, val(dsem))
            with nc.allow_low_precision(reason="softmax reciprocal feeds an f16 matmul operand"):
                inc(e.reciprocal(out=s_recip[:], in_=s_den[:]), "vv")
        v_recip = val("vv")

        tn_db, ev_db = [], []
        for oc in range(NCH):
            @blk.tensor
            def _(e, oc=oc):
                if oc == 0:
                    wait(e, "vv", v_recip)
                if oc >= 2:
                    wait(e, "vv", ev_db[oc - 2])
                banks = [psum[4 * (oc % 2) + t] for t in range(4)]
                mm = None
                for t in range(4):
                    mm = e.matmul(banks[t][:, :], hsel[:, oc * 128:(oc + 1) * 128],
                                  s_recip[:, t * 512:(t + 1) * 512], start=True, stop=True)
                tn_db.append(inc(mm, "tn"))

            @blk.vector
            def _(e, oc=oc):
                wait(e, "tn", tn_db[oc])
                banks = [psum[4 * (oc % 2) + t] for t in range(4)]
                last = None
                for t in range(4):
                    last = e.tensor_mul(s_o[:, oc, t * 512:(t + 1) * 512],
                                        s_o[:, oc, t * 512:(t + 1) * 512], banks[t][:, :])
                ev_db.append(inc(last, "vv"))
        barrier()
        maybe_stop(7)
        freesb("s_recip", "s_den")

        s_xm = sb("s_xm", [128, 2, 1024], F32, scoped=True, side="right")
        s_aT = sb("s_aT", [128, 2, 1024], F16, scoped=True, side="right")
        s_attn = sb("s_attn", [128, NCH, T], F16, scoped=True, side="right")
        tn_wo, ev_wo = [], []
        for oc in range(NCH):
            @blk.tensor
            def _(e, oc=oc):
                if oc >= 2:
                    wait(e, "sc", ev_wo[oc - 2])
                banks = [psum[4 * (oc % 2) + t] for t in range(4)]
                mm = None
                for cc in range(NCH):
                    for t in range(4):
                        mm = e.matmul(banks[t][:, :], s_wtB[:, cc, oc * 128:(oc + 1) * 128],
                                      s_o[:, cc, t * 512:(t + 1) * 512],
                                      start=(cc == 0), stop=(cc == NCH - 1))
                tn_wo.append(inc(mm, "tn"))

            @blk.scalar
            def _(e, oc=oc):
                wait(e, "tn", tn_wo[oc])
                banks = [psum[4 * (oc % 2) + t] for t in range(4)]
                last = None
                for t in range(4):
                    last = e.copy(s_attn[:, oc, t * 512:(t + 1) * 512], banks[t][:, :])
                ev_wo.append(inc(last, "sc"))
        if taps:
            @blk.gpsimd
            def _(e):
                wait(e, "sc", ev_wo[-1])
                for oc in range(NCH):
                    inc(e.dma_start(out=attnfm_hbm[oc, :, :], in_=s_attn[:, oc, :]), "GC", 16)
                wait(e, "GC", val("GC"))
        barrier()
        maybe_stop(8)
        freesb("s_o", "s_wq16", "s_wraw", "s_wtB")

        x2_ld, x2_tt, x2_v1, x2_sc1, x2_v2, x2_sc2, x2_v3, x2_gd = [], [], [], [], [], [], [], []
        for tc in range(NT):
            @blk.scalar
            def _(e, tc=tc):
                if tc >= 2:
                    wait(e, "vv", x2_v1[tc - 2])
                x2_ld.append(inc(e.dma_start(out=s_xt[:, tc % 2, :],
                                             in_=x_h[tc * 128:(tc + 1) * 128, :]),
                                 f"A{tc % 2}", 16))

            @blk.sync
            def _(e, tc=tc):
                if tc >= 2:
                    wait(e, "vv", x2_v1[tc - 2])
                for cc in range(0, NCH, 2) if SPLIT else range(NCH):
                    inc(e.dma_start_transpose(s_aT[:, tc % 2, cc * 128:(cc + 1) * 128],
                                              s_attn[:, cc, tc * 128:(tc + 1) * 128]),
                        f"T{tc % 2}", 16)

            if SPLIT:
                @blk.scalar
                def _(e, tc=tc):
                    if tc >= 2:
                        wait(e, "vv", x2_v1[tc - 2])
                    for cc in range(1, NCH, 2):
                        inc(e.dma_start_transpose(s_aT[:, tc % 2, cc * 128:(cc + 1) * 128],
                                                  s_attn[:, cc, tc * 128:(tc + 1) * 128]),
                            f"T{tc % 2}", 16)
            x2_tt.append(val(f"T{tc % 2}"))

            @blk.vector
            def _(e, tc=tc):
                wait(e, f"T{tc % 2}", x2_tt[tc])
                wait(e, f"A{tc % 2}", x2_ld[tc])
                if tc >= 2:
                    wait(e, f"G{tc % 2}", x2_gd[tc - 2])
                e.tensor_add(s_xm[:, tc % 2, :], s_xt[:, tc % 2, :], s_aT[:, tc % 2, :])
                sw(e, e.reduce_sum(s_sum[:, tc:tc + 1], s_xm[:, tc % 2, :], AXX), "vv")
                x2_v1.append(inc(e.tensor_scalar(out=s_m[:, tc:tc + 1],
                                                 in0=s_sum[:, tc:tc + 1], scalar1=1.0 / D,
                                                 scalar2=None, op0=ALU.mult), "vv"))

            @blk.scalar
            def _(e, tc=tc):
                wait(e, "vv", x2_v1[tc])
                x2_sc1.append(inc(e.activation(out=junk16[:, :], in_=s_xm[:, tc % 2, :],
                                               func=AF.Square, accum_out=s_ssq[:, tc:tc + 1]),
                                  "sc"))

            @blk.vector
            def _(e, tc=tc):
                wait(e, "sc", x2_sc1[tc])
                sw(e, e.tensor_mul(s_t0[:, tc:tc + 1], s_m[:, tc:tc + 1],
                                   s_m[:, tc:tc + 1]), "vv")
                x2_v2.append(inc(e.scalar_tensor_tensor(out=s_t0[:, tc:tc + 1],
                                                        in0=s_ssq[:, tc:tc + 1],
                                                        scalar=1.0 / D,
                                                        in1=s_t0[:, tc:tc + 1],
                                                        op0=ALU.mult, op1=ALU.subtract), "vv"))

            @blk.scalar
            def _(e, tc=tc):
                wait(e, "vv", x2_v2[tc])
                x2_sc2.append(inc(e.activation(out=s_t0[:, tc:tc + 1], in_=s_t0[:, tc:tc + 1],
                                               func=AF.Sqrt, bias=1e-5, scale=1.0), "sc"))

            @blk.vector
            def _(e, tc=tc):
                wait(e, "sc", x2_sc2[tc])
                sw(e, e.reciprocal(out=s_a[:, tc:tc + 1], in_=s_t0[:, tc:tc + 1]), "vv")
                sw(e, e.scalar_tensor_tensor(out=s_b[:, tc:tc + 1], in0=s_m[:, tc:tc + 1],
                                             scalar=-1.0, in1=s_a[:, tc:tc + 1],
                                             op0=ALU.mult, op1=ALU.mult), "vv")
                e.tensor_scalar(out=s_h2j[:, :], in0=s_xm[:, tc % 2, :],
                                scalar1=s_a[:, tc:tc + 1], scalar2=s_b[:, tc:tc + 1],
                                op0=ALU.mult, op1=ALU.add)
                x2_v3.append(inc(e.reduce_max(s_mx[:, tc:tc + 1], s_h2j[:, :], AXX,
                                              apply_absolute_value=True), "vv"))

            @blk.gpsimd
            def _(e, tc=tc):
                wait(e, "vv", x2_v3[tc])
                x2_gd.append(inc(e.dma_start(out=xmid_hbm[tc, :, :],
                                             in_=s_xm[:, tc % 2, :]), f"G{tc % 2}", 16))

        @blk.vector
        def _(e):
            wait(e, "vv", x2_v3[-1])
            sw(e, e.reduce_max(lmax[:, 0:1], s_mx[:, 0:NT], AXX,
                               apply_absolute_value=True), "vv")

        @blk.gpsimd
        def _(e):
            wait(e, "G0", val("G0"))
            wait(e, "G1", val("G1"))
        drainT()
        barrier()
        maybe_stop(9)
        freesb("s_attn", "s_aT")

        def cc_round(r, nscal, derive):
            vnow = val("vv")

            @blk.sync
            def _(e):
                wait(e, "vv", vnow)
                i = inc(e.dma_start(out=lscr[:, :], in_=lmax[:, 0:2]), "GH", 16)
                wait(e, "GH", i)
                i = inc(e.dma_start(out=cc128[0:2, 0:128],
                                    in_=lscr[:, :].rearrange("p c -> c p"),
                                    allow_slow_non_contiguous=True), "GH", 16)
                wait(e, "GH", i)

            @blk.vector
            def _(e):
                wait(e, "GH", val("GH"))
                inc(e.reduce_max(cc128[:, 0:1], cc128[:, :], AXX), "vv")
            t2 = val("vv")

            @blk.sync
            def _(e):
                wait(e, "vv", t2)
                i = inc(e.dma_start(out=lscr2[:, :], in_=cc128[0:2, 0:1]), "GH", 16)
                wait(e, "GH", i)
                i = inc(e.dma_start(out=ccsb[0:1, 0:2],
                                    in_=lscr2[:, :].rearrange("p c -> c p"),
                                    allow_slow_non_contiguous=True), "GH", 16)
                wait(e, "GH", i)

            @blk.vector
            def _(e):
                wait(e, "GH", val("GH"))
                for j in range(nscal, 8):
                    e.tensor_copy(ccsb[0:1, j:j + 1], ccsb[0:1, j % nscal:j % nscal + 1])
                inc(e.memset(lmax[:, :], 0.0), "vv")
            t4 = val("vv")

            @blk.gpsimd
            def _(e):
                wait(e, "vv", t4)
                i = inc(e.dma_start(out=cc_in[r][:, :], in_=ccsb[0:1, 0:8]), "GC", 16)
                wait(e, "GC", i)
                inc(e.collective_compute(
                    "AllReduce", ALU.max, replica_groups=[list(range(NCORE))],
                    ins=[cc_in[r][:, :]], outs=[cc_out[r][:, :]]), "ccs")
                wait(e, "ccs")
                inc(e.dma_start(out=ccsb[0:1, 8:16], in_=cc_out[r][:, :]), "GC", 16)
            t5 = val("GC")

            @blk.vector
            def _(e):
                wait(e, "GC", t5)
                inc(derive(e), "vv")
            return val("vv")

        def cc_bcast(t6, ncols, dst0):
            @blk.tensor
            def _(e):
                wait(e, "vv", t6)
                inc(e.matmul(psum[0][:, 0:ncols], ones1[:], ccsb[0:1, 0:ncols],
                             start=True, stop=True), "tn")
            t7 = val("tn")

            @blk.vector
            def _(e):
                wait(e, "tn", t7)
                inc(e.tensor_copy(scal[:, dst0:dst0 + ncols], psum[0][:, 0:ncols]), "vv")
            return val("vv")

        def derive_r1(e):
            sw(e, e.tensor_scalar(out=ccsb[0:1, 1:2], in0=ccsb[0:1, 8:9],
                                  scalar1=1.0 / 32767.0, scalar2=1e-12,
                                  op0=ALU.mult, op1=ALU.max), "vv")
            return e.reciprocal(out=ccsb[0:1, 0:1], in_=ccsb[0:1, 1:2])
        v_r1 = cc_bcast(cc_round(0, 1, derive_r1), 2, 0)

        s_fraw = sb("s_fraw", [128, 2, 1024], F32, scoped=True)
        s_fq16 = sb("s_fq16", [128, 2, 1024], F16, scoped=True)
        s_fs = sb("s_fs", [128, 32], F32, scoped=True)
        s_frs = sb("s_frs", [128, 32], F32, scoped=True)
        s_gate = sb("s_gate", [128, NIC, T], F16, scoped=True)
        s_xq = sb("s_xq", [128, NCH, T], F16, scoped=True)
        s_xqtm = sb("s_xqtm", [128, 2, 1024], F16, scoped=True, side="right")
        s_z2 = sb("s_z2", [128, 1024], F32, scoped=True, side="right")

        @blk.vector
        def _(e):
            wait(e, "vv", v_r1)
            e.tensor_scalar(out=s_a[:, :], in0=s_a[:, :], scalar1=scal[:, 0:1],
                            scalar2=None, op0=ALU.mult)
            inc(e.tensor_scalar(out=s_b[:, :], in0=s_b[:, :], scalar1=scal[:, 0:1],
                                scalar2=MAGIC, op0=ALU.mult, op1=ALU.add), "vv")
        v_ab = val("vv")

        xq_ld, xq_v, xq_tt = [], [], []
        for tc in range(NT):
            @blk.scalar
            def _(e, tc=tc):
                if tc >= 2:
                    wait(e, "vv", xq_v[tc - 2])
                wait(e, f"G{tc % 2}", x2_gd[tc])
                xq_ld.append(inc(e.dma_start(out=s_xm[:, tc % 2, :],
                                             in_=xmid_hbm[tc, :, :]), f"A{tc % 2}", 16))

            @blk.vector
            def _(e, tc=tc):
                if tc == 0:
                    wait(e, "vv", v_ab)
                wait(e, f"A{tc % 2}", xq_ld[tc])
                if tc >= 2:
                    wait(e, f"T{tc % 2}", xq_tt[tc - 2])
                e.tensor_scalar(out=s_z2[:, :], in0=s_xm[:, tc % 2, :],
                                scalar1=s_a[:, tc:tc + 1], scalar2=s_b[:, tc:tc + 1],
                                op0=ALU.mult, op1=ALU.add)
                xq_v.append(inc(e.tensor_scalar(out=s_xqtm[:, tc % 2, :],
                                                in0=s_z2[:, :], scalar1=MAGIC,
                                                scalar2=scal[:, 1:2],
                                                op0=ALU.subtract, op1=ALU.mult), "vv"))

            @blk.sync
            def _(e, tc=tc):
                wait(e, "vv", xq_v[tc])
                for cc in range(0, NCH, 2) if SPLIT else range(NCH):
                    inc(e.dma_start_transpose(s_xq[:, cc, tc * 128:(tc + 1) * 128],
                                              s_xqtm[:, tc % 2, cc * 128:(cc + 1) * 128]),
                        f"T{tc % 2}", 16)

            if SPLIT:
                @blk.scalar
                def _(e, tc=tc):
                    wait(e, "vv", xq_v[tc])
                    for cc in range(1, NCH, 2):
                        inc(e.dma_start_transpose(s_xq[:, cc, tc * 128:(tc + 1) * 128],
                                                  s_xqtm[:, tc % 2, cc * 128:(cc + 1) * 128]),
                            f"T{tc % 2}", 16)
            xq_tt.append(val(f"T{tc % 2}"))
        drainT()
        barrier()
        maybe_stop(10)
        freesb("s_z2", "s_xqtm", "s_xm")

        s_up2 = sb("s_up2", [128, 2, T], F16, scoped=True, side="right")
        s_fwt = sb("s_fwt", [128, 2, NCH, 128], F16, scoped=True, side="right")

        def ffn_side(w_src, dst, spill, lmax_col):
            f_ld, f_vq, f_tt, mm_tn, ev_sc, mx_vv, st_gd = [], [], [], [], [], [], []
            for ic in range(NIC):
                @blk.scalar
                def _(e, ic=ic):
                    if ic >= 2:
                        wait(e, "vv", f_vq[ic - 2])
                    f_ld.append(inc(e.dma_start(out=s_fraw[:, ic % 2, :],
                                                in_=w_src[ic * 128:(ic + 1) * 128, :]),
                                    f"B{ic % 2}", 16))

                @blk.vector
                def _(e, ic=ic):
                    wait(e, f"B{ic % 2}", f_ld[ic])
                    if ic >= 2:
                        wait(e, f"T{ic % 2}", f_tt[ic - 2])
                    sw(e, e.reduce_max(s_fs[:, :], s_fraw[:, ic % 2, :]
                                       .rearrange("p (b k) -> p b k", k=32), AXX,
                                       apply_absolute_value=True), "vv")
                    sw(e, e.tensor_scalar(out=s_fs[:, :], in0=s_fs[:, :], scalar1=1.0 / 7.0,
                                          scalar2=1e-12, op0=ALU.mult, op1=ALU.max), "vv")
                    sw(e, e.reciprocal(out=s_frs[:, :], in_=s_fs[:, :]), "vv")
                    e.tensor_tensor(out=s_fraw[:, ic % 2, :].rearrange("p (b k) -> p b k", k=32),
                                    in0=s_fraw[:, ic % 2, :].rearrange("p (b k) -> p b k", k=32),
                                    in1=s_frs[:, :].rearrange("p (b o) -> p b o", o=1)
                                    .broadcast_to([128, 32, 32]), op=ALU.mult)
                    e.tensor_scalar(out=s_fraw[:, ic % 2, :], in0=s_fraw[:, ic % 2, :],
                                    scalar1=MAGIC, scalar2=MAGIC,
                                    op0=ALU.add, op1=ALU.subtract)
                    f_vq.append(inc(e.tensor_tensor(
                        out=s_fq16[:, ic % 2, :].rearrange("p (b k) -> p b k", k=32),
                        in0=s_fraw[:, ic % 2, :].rearrange("p (b k) -> p b k", k=32),
                        in1=s_fs[:, :].rearrange("p (b o) -> p b o", o=1)
                        .broadcast_to([128, 32, 32]), op=ALU.mult), "vv"))

                @blk.sync
                def _(e, ic=ic):
                    wait(e, "vv", f_vq[ic])
                    if ic >= 2:
                        wait(e, "tn", mm_tn[ic - 2])
                    for cc in range(0, NCH, 2) if SPLIT else range(NCH):
                        inc(e.dma_start_transpose(s_fwt[:, ic % 2, cc, :],
                                                  s_fq16[:, ic % 2, cc * 128:(cc + 1) * 128]),
                            f"T{ic % 2}", 16)

                if SPLIT:
                    @blk.scalar
                    def _(e, ic=ic):
                        wait(e, "vv", f_vq[ic])
                        if ic >= 2:
                            wait(e, "tn", mm_tn[ic - 2])
                        for cc in range(1, NCH, 2):
                            inc(e.dma_start_transpose(s_fwt[:, ic % 2, cc, :],
                                                      s_fq16[:, ic % 2,
                                                             cc * 128:(cc + 1) * 128]),
                                f"T{ic % 2}", 16)
                f_tt.append(val(f"T{ic % 2}"))

                @blk.tensor
                def _(e, ic=ic):
                    wait(e, f"T{ic % 2}", f_tt[ic])
                    if ic >= 2:
                        wait(e, "sc", ev_sc[ic - 2])
                    banks = [psum[4 * (ic % 2) + t] for t in range(4)]
                    mm = None
                    for cc in range(NCH):
                        for t in range(4):
                            mm = e.matmul(banks[t][:, :], s_fwt[:, ic % 2, cc, :],
                                          s_xq[:, cc, t * 512:(t + 1) * 512],
                                          start=(cc == 0), stop=(cc == NCH - 1))
                    mm_tn.append(inc(mm, "tn"))

                @blk.scalar
                def _(e, ic=ic):
                    wait(e, "tn", mm_tn[ic])
                    if dst is None and ic >= 2:
                        wait(e, f"G{ic % 2}", st_gd[ic - 2])
                    banks = [psum[4 * (ic % 2) + t] for t in range(4)]
                    last = None
                    for t in range(4):
                        tgt = s_up2[:, ic % 2, t * 512:(t + 1) * 512] if dst is None \
                            else dst[:, ic, t * 512:(t + 1) * 512]
                        last = e.copy(tgt, banks[t][:, :])
                    ev_sc.append(inc(last, "sc"))

                @blk.vector
                def _(e, ic=ic):
                    wait(e, "sc", ev_sc[ic])
                    tgt = s_up2[:, ic % 2, :] if dst is None else dst[:, ic, :]
                    mx_vv.append(inc(e.reduce_max(s_mx[:, ic:ic + 1], tgt, AXX,
                                                  apply_absolute_value=True), "vv"))

                if dst is None:
                    @blk.gpsimd
                    def _(e, ic=ic):
                        wait(e, "vv", mx_vv[ic])
                        st_gd.append(inc(e.dma_start(out=spill[ic, :, :],
                                                     in_=s_up2[:, ic % 2, :]),
                                         f"G{ic % 2}", 16))

            @blk.vector
            def _(e):
                wait(e, "vv", mx_vv[-1])
                sw(e, e.reduce_max(lmax[:, lmax_col:lmax_col + 1], s_mx[:, 0:NIC], AXX,
                                   apply_absolute_value=True), "vv")

            if dst is None:
                @blk.gpsimd
                def _(e):
                    wait(e, "G0", val("G0"))
                    wait(e, "G1", val("G1"))
            drainT()
            barrier()

        ffn_side(wup_h, None, up_hbm, 0)
        ffn_side(wg_h, s_gate, None, 1)
        if taps:
            @blk.gpsimd
            def _(e):
                wait(e, "vv", val("vv"))
                for ic in range(NIC):
                    inc(e.dma_start(out=gate_hbm[ic, :, :], in_=s_gate[:, ic, :]), "GC", 16)
                wait(e, "GC", val("GC"))
            barrier()
        maybe_stop(11)
        freesb("s_fwt")
        freesb("s_xq")

        def derive_r2(e):
            sw(e, e.tensor_scalar(out=ccsb[0:1, 10:12], in0=ccsb[0:1, 8:10],
                                  scalar1=1.0 / 32767.0, scalar2=1e-12,
                                  op0=ALU.mult, op1=ALU.max), "vv")
            sw(e, e.reciprocal(out=ccsb[0:1, 12:14], in_=ccsb[0:1, 10:12]), "vv")
            e.tensor_copy(ccsb[0:1, 0:1], ccsb[0:1, 12:13])
            e.tensor_copy(ccsb[0:1, 1:2], ccsb[0:1, 10:11])
            e.tensor_copy(ccsb[0:1, 2:3], ccsb[0:1, 13:14])
            e.tensor_copy(ccsb[0:1, 3:4], ccsb[0:1, 11:12])
            return e.tensor_scalar(out=ccsb[0:1, 4:5], in0=ccsb[0:1, 11:12], scalar1=-MAGIC,
                                   scalar2=None, op0=ALU.mult)
        v_r2 = cc_bcast(cc_round(1, 2, derive_r2), 5, 2)

        s_tmp16 = sb("s_tmp16", [128, 2, T], F16, scoped=True, side="right")
        s_zA = sb("s_zA", [128, 2, T], F32, scoped=True, side="right")
        s_zB = sb("s_zB", [128, 2, T], F32, scoped=True, side="right")
        ge_sc, ge_vv = [], []
        for ic in range(NIC):
            @blk.scalar
            def _(e, ic=ic):
                if ic == 0:
                    wait(e, "vv", v_r2)
                if ic >= 2:
                    wait(e, "vv", ge_vv[ic - 2])
                e.activation(out=s_zA[:, ic % 2, :], in_=s_gate[:, ic, :], func=AF.Copy,
                             bias=MAGIC, scale=scal[:, 4:5])
                ge_sc.append(inc(e.activation(out=s_zB[:, ic % 2, :], in_=s_zA[:, ic % 2, :],
                                              func=AF.Sigmoid, bias=scal[:, 6:7],
                                              scale=scal[:, 5:6]), "sc"))

            @blk.vector
            def _(e, ic=ic):
                wait(e, "sc", ge_sc[ic])
                e.tensor_scalar(out=s_tmp16[:, ic % 2, :], in0=s_zA[:, ic % 2, :],
                                scalar1=MAGIC, scalar2=scal[:, 5:6],
                                op0=ALU.subtract, op1=ALU.mult)
                e.tensor_scalar(out=s_zA[:, ic % 2, :], in0=s_zB[:, ic % 2, :],
                                scalar1=65536.0, scalar2=MAGIC, op0=ALU.mult, op1=ALU.add)
                e.tensor_scalar(out=s_zB[:, ic % 2, :], in0=s_zA[:, ic % 2, :],
                                scalar1=MAGIC + 65535.0, scalar2=float(2.0 ** -16),
                                op0=ALU.min, op1=ALU.mult)
                e.scalar_tensor_tensor(out=s_gate[:, ic, :], in0=s_zB[:, ic % 2, :],
                                       scalar=-192.0, in1=s_tmp16[:, ic % 2, :],
                                       op0=ALU.add, op1=ALU.mult)
                ge_vv.append(inc(e.reduce_max(s_mx[:, ic:ic + 1], s_gate[:, ic, :], AXX,
                                              apply_absolute_value=True), "vv"))

        @blk.vector
        def _(e):
            wait(e, "vv", ge_vv[-1])
            sw(e, e.reduce_max(lmax[:, 0:1], s_mx[:, 0:NIC], AXX,
                               apply_absolute_value=True), "vv")
        barrier()
        maybe_stop(12)

        def derive_r3(e):
            sw(e, e.tensor_scalar(out=ccsb[0:1, 1:2], in0=ccsb[0:1, 8:9],
                                  scalar1=1.0 / 32767.0, scalar2=1e-12,
                                  op0=ALU.mult, op1=ALU.max), "vv")
            return e.reciprocal(out=ccsb[0:1, 0:1], in_=ccsb[0:1, 1:2])
        v_r3 = cc_bcast(cc_round(2, 1, derive_r3), 2, 7)

        @blk.vector
        def _(e):
            wait(e, "vv", v_r3)
            inc(e.tensor_mul(scal[:, 3:4], scal[:, 3:4], scal[:, 8:9]), "vv")
        v_s2u3 = val("vv")

        op_ld, op_sc, op_vv = [], [], []
        for ic in range(NIC):
            @blk.sync
            def _(e, ic=ic):
                if ic >= 2:
                    wait(e, "vv", op_vv[ic - 2])
                op_ld.append(inc(e.dma_start(out=s_up2[:, ic % 2, :],
                                             in_=up_hbm[ic, :, :]), f"A{ic % 2}", 16))

            @blk.scalar
            def _(e, ic=ic):
                if ic == 0:
                    wait(e, "vv", v_s2u3)
                wait(e, f"A{ic % 2}", op_ld[ic])
                if ic >= 2:
                    wait(e, "vv", op_vv[ic - 2])
                e.activation(out=s_zA[:, ic % 2, :], in_=s_up2[:, ic % 2, :], func=AF.Copy,
                             bias=MAGIC, scale=scal[:, 2:3])
                op_sc.append(inc(e.activation(out=s_zB[:, ic % 2, :], in_=s_gate[:, ic, :],
                                              func=AF.Copy, bias=MAGIC, scale=scal[:, 7:8]),
                                 "sc"))

            @blk.vector
            def _(e, ic=ic):
                wait(e, "sc", op_sc[ic])
                e.tensor_scalar(out=s_tmp16[:, ic % 2, :], in0=s_zA[:, ic % 2, :],
                                scalar1=MAGIC, scalar2=scal[:, 3:4],
                                op0=ALU.subtract, op1=ALU.mult)
                e.scalar_tensor_tensor(out=s_gate[:, ic, :], in0=s_zB[:, ic % 2, :],
                                       scalar=-MAGIC, in1=s_tmp16[:, ic % 2, :],
                                       op0=ALU.add, op1=ALU.mult)
                op_vv.append(inc(e.reduce_max(s_mx[:, ic:ic + 1], s_gate[:, ic, :], AXX,
                                              apply_absolute_value=True), "vv"))

        @blk.vector
        def _(e):
            wait(e, "vv", op_vv[-1])
            sw(e, e.reduce_max(lmax[:, 0:1], s_mx[:, 0:NIC], AXX,
                               apply_absolute_value=True), "vv")
        barrier()
        maybe_stop(13)

        def derive_r4(e):
            sw(e, e.tensor_scalar(out=ccsb[0:1, 1:2], in0=ccsb[0:1, 8:9],
                                  scalar1=1.0 / 32767.0, scalar2=1e-12,
                                  op0=ALU.mult, op1=ALU.max), "vv")
            return e.reciprocal(out=ccsb[0:1, 0:1], in_=ccsb[0:1, 1:2])
        v_r4 = cc_bcast(cc_round(3, 1, derive_r4), 2, 8)

        oi_sc, oi_vv = [], []
        for ic in range(NIC):
            @blk.scalar
            def _(e, ic=ic):
                if ic == 0:
                    wait(e, "vv", v_r4)
                if ic >= 2:
                    wait(e, "vv", oi_vv[ic - 2])
                oi_sc.append(inc(e.activation(out=s_zA[:, ic % 2, :], in_=s_gate[:, ic, :],
                                              func=AF.Copy, bias=MAGIC, scale=scal[:, 8:9]),
                                 "sc"))

            @blk.vector
            def _(e, ic=ic):
                wait(e, "sc", oi_sc[ic])
                oi_vv.append(inc(e.tensor_scalar(out=s_gate[:, ic, :],
                                                 in0=s_zA[:, ic % 2, :], scalar1=MAGIC,
                                                 scalar2=None, op0=ALU.subtract), "vv"))
        barrier()
        freesb("s_zB", "s_zA", "s_tmp16", "s_up2")

        s_dwt = sb("s_dwt", [128, 2, NIC, 128], F16, scoped=True)
        s_do = sb("s_do", [128, T], F16, scoped=True)
        s_osl = sb("s_osl", [128, 2, 128], F32, scoped=True)
        s_doT = sb("s_doT", [128, 2, 128], F16, scoped=True, side="right")

        d_ld, d_vq, d_tt = {}, {}, {}
        dn_tn, dn_ev = [], []
        o_tt, o_ld, o_vv, o_gd = [], [], [], []

        def wdown_prep(oc):
            slot = oc % 2
            for s3 in range(3):
                key = (oc, s3)

                @blk.scalar
                def _(e, oc=oc, s3=s3, key=key):
                    prev = (oc, s3 - 2) if s3 >= 2 else ((oc - 1, s3 + 1) if oc >= 1 else None)
                    if prev in d_vq:
                        wait(e, "vv", d_vq[prev])
                    d_ld[key] = inc(e.dma_start(
                        out=s_fraw[:, s3 % 2, :],
                        in_=wd_h[oc * 128:(oc + 1) * 128, s3 * 1024:(s3 + 1) * 1024]),
                        f"B{s3 % 2}", 16)

                @blk.vector
                def _(e, key=key, s3=s3, oc=oc):
                    wait(e, f"B{s3 % 2}", d_ld[key])
                    prev = (oc, s3 - 2) if s3 >= 2 else ((oc - 1, s3 + 1) if oc >= 1 else None)
                    if prev in d_tt:
                        wait(e, f"T{(oc * 3 + s3) % 2}", d_tt[prev])
                    sw(e, e.reduce_max(s_fs[:, :], s_fraw[:, s3 % 2, :]
                                       .rearrange("p (b k) -> p b k", k=32), AXX,
                                       apply_absolute_value=True), "vv")
                    sw(e, e.tensor_scalar(out=s_fs[:, :], in0=s_fs[:, :], scalar1=1.0 / 7.0,
                                          scalar2=1e-12, op0=ALU.mult, op1=ALU.max), "vv")
                    sw(e, e.reciprocal(out=s_frs[:, :], in_=s_fs[:, :]), "vv")
                    e.tensor_tensor(out=s_fraw[:, s3 % 2, :].rearrange("p (b k) -> p b k", k=32),
                                    in0=s_fraw[:, s3 % 2, :].rearrange("p (b k) -> p b k", k=32),
                                    in1=s_frs[:, :].rearrange("p (b o) -> p b o", o=1)
                                    .broadcast_to([128, 32, 32]), op=ALU.mult)
                    e.tensor_scalar(out=s_fraw[:, s3 % 2, :], in0=s_fraw[:, s3 % 2, :],
                                    scalar1=MAGIC, scalar2=MAGIC, op0=ALU.add, op1=ALU.subtract)
                    d_vq[key] = inc(e.tensor_tensor(
                        out=s_fq16[:, s3 % 2, :].rearrange("p (b k) -> p b k", k=32),
                        in0=s_fraw[:, s3 % 2, :].rearrange("p (b k) -> p b k", k=32),
                        in1=s_fs[:, :].rearrange("p (b o) -> p b o", o=1)
                        .broadcast_to([128, 32, 32]), op=ALU.mult), "vv")

                @blk.sync
                def _(e, key=key, s3=s3, slot=slot, oc=oc):
                    wait(e, "vv", d_vq[key])
                    if oc >= 2 and s3 == 0:
                        wait(e, "tn", dn_tn[oc - 2])
                    for cc in range(0, 8, 2) if SPLIT else range(8):
                        inc(e.dma_start_transpose(s_dwt[:, slot, s3 * 8 + cc, :],
                                                  s_fq16[:, s3 % 2, cc * 128:(cc + 1) * 128]),
                            f"T{(oc * 3 + s3) % 2}", 16)

                if SPLIT:
                    @blk.scalar
                    def _(e, key=key, s3=s3, slot=slot, oc=oc):
                        wait(e, "vv", d_vq[key])
                        if oc >= 2 and s3 == 0:
                            wait(e, "tn", dn_tn[oc - 2])
                        for cc in range(1, 8, 2):
                            inc(e.dma_start_transpose(s_dwt[:, slot, s3 * 8 + cc, :],
                                                      s_fq16[:, s3 % 2,
                                                             cc * 128:(cc + 1) * 128]),
                                f"T{(oc * 3 + s3) % 2}", 16)
                d_tt[key] = val(f"T{(oc * 3 + s3) % 2}")

        for oc in range(NCH):
            wdown_prep(oc)

            @blk.tensor
            def _(e, oc=oc):
                for s3 in range(3):
                    wait(e, f"T{(oc * 3 + s3) % 2}", d_tt[(oc, s3)])
                if oc >= 2:
                    wait(e, "sc", dn_ev[oc - 2])
                if oc == 0:
                    wait(e, "vv", oi_vv[-1])
                banks = [psum[4 * (oc % 2) + t] for t in range(4)]
                mm = None
                for ic in range(NIC):
                    for t in range(4):
                        mm = e.matmul(banks[t][:, :], s_dwt[:, oc % 2, ic, :],
                                      s_gate[:, ic, t * 512:(t + 1) * 512],
                                      start=(ic == 0), stop=(ic == NIC - 1))
                dn_tn.append(inc(mm, "tn"))

            @blk.scalar
            def _(e, oc=oc):
                wait(e, "tn", dn_tn[oc])
                if oc >= 1:
                    wait(e, f"D{(oc * NT - 1) % 2}", o_tt[oc * NT - 1])
                banks = [psum[4 * (oc % 2) + t] for t in range(4)]
                last = None
                for t in range(4):
                    last = e.activation(out=s_do[:, t * 512:(t + 1) * 512], in_=banks[t][:, :],
                                        func=AF.Copy, bias=0.0, scale=scal[:, 9:10])
                dn_ev.append(inc(last, "sc"))

            for tc in range(NT):
                k = oc * NT + tc

                @blk.scalar
                def _(e, oc=oc, tc=tc, k=k):
                    if k >= 2:
                        wait(e, "vv", o_vv[k - 2])
                    o_ld.append(inc(e.dma_start(
                        out=s_xt[:, tc % 2, 0:128],
                        in_=xmid_hbm[tc, :, oc * 128:(oc + 1) * 128]), f"A{k % 2}", 16))

                @blk.sync
                def _(e, oc=oc, tc=tc, k=k):
                    if tc == 0:
                        wait(e, "sc", dn_ev[oc])
                    if k >= 2:
                        wait(e, "vv", o_vv[k - 2])
                    inc(e.dma_start_transpose(s_doT[:, k % 2, :],
                                              s_do[:, tc * 128:(tc + 1) * 128]),
                        f"D{k % 2}", 16)
                    o_tt.append(val(f"D{k % 2}"))

                @blk.vector
                def _(e, oc=oc, tc=tc, k=k):
                    wait(e, f"D{k % 2}", o_tt[k])
                    wait(e, f"A{k % 2}", o_ld[k])
                    if k >= 2:
                        wait(e, f"G{k % 2}", o_gd[k - 2])
                    o_vv.append(inc(e.tensor_add(s_osl[:, k % 2, :],
                                                 s_xt[:, tc % 2, 0:128],
                                                 s_doT[:, k % 2, :]), "vv"))

                @blk.gpsimd
                def _(e, oc=oc, tc=tc, k=k):
                    wait(e, "vv", o_vv[k])
                    o_gd.append(inc(e.dma_start(
                        out=out_h[tc * 128:(tc + 1) * 128, oc * 128:(oc + 1) * 128],
                        in_=s_osl[:, k % 2, :]), f"G{k % 2}", 16))

        @blk.gpsimd
        def _(e):
            wait(e, "G0", val("G0"))
            wait(e, "G1", val("G1"))

        @blk.sync
        def _(e):
            wait(e, "G0", val("G0"))
            wait(e, "G1", val("G1"))

        freesb("s_doT", "s_osl", "s_do", "s_dwt", "s_gate",
               "s_frs", "s_fs", "s_fq16", "s_fraw", "s_xt")
      except _Stop:
        pass

    return nc


def _consts():
    hsel = np.zeros((16, D), dtype=np.float16)
    for g in range(16):
        hsel[g, g * 64:(g + 1) * 64] = 1.0
    return hsel


def kernel(**inputs):
    nc = build_program()
    hsel = _consts()
    x = np.ascontiguousarray(np.asarray(inputs["L0_attn_x"], dtype=np.float32))
    base = {
        "c_hsel": hsel,
        "w_up": np.ascontiguousarray(np.asarray(inputs["L0_ffn_w_up"], dtype=np.float32)),
        "w_gate": np.ascontiguousarray(np.asarray(inputs["L0_ffn_w_gate"], dtype=np.float32)),
        "w_down": np.ascontiguousarray(np.asarray(inputs["L0_ffn_w_down"], dtype=np.float32)),
    }
    for nm in ("Wq", "Wk", "Wv", "Wo"):
        base[nm + "_w"] = np.ascontiguousarray(np.asarray(inputs[f"L0_attn_{nm}_w"], dtype=np.float32))
        base[nm + "_dW"] = np.ascontiguousarray(np.asarray(inputs[f"L0_attn_{nm}_dW"], dtype=np.float32))
        base[nm + "_dB"] = np.ascontiguousarray(np.asarray(inputs[f"L0_attn_{nm}_dB"], dtype=np.float32))
    in_maps = [dict(base, x=np.ascontiguousarray(x[i])) for i in range(NCORE)]
    res = run_bass_kernel_spmd(nc, in_maps, core_ids=list(range(NCORE)))
    return np.stack([res.results[i]["out"] for i in range(NCORE)], axis=0)


# version 11
# speedup vs baseline: 1.0597x; 1.0597x over previous
import numpy as np
from contextlib import ExitStack

import concourse.bass as bass
import concourse.mybir as mybir
from concourse.bass_utils import run_bass_kernel_spmd

F32 = mybir.dt.float32
F16 = mybir.dt.float16
AF = mybir.ActivationFunctionType
ALU = mybir.AluOpType
AXX = mybir.AxisListType.X

MAGIC = float(1.5 * 2 ** 23)
T, D, I3 = 2048, 1024, 3072
NT, NCH, NIC = T // 128, D // 128, I3 // 128
NCORE = 8

_DEV = {"taps": False, "stage": 99}


def build_program():
    taps = _DEV["taps"]
    SPLIT = _DEV.get("split", False)
    nc = bass.Bass("TRN2", target_bir_lowering=False, debug=False,
                   detect_race_conditions=False)

    x_h = nc.dram_tensor("x", [T, D], F32, kind="ExternalInput").ap()
    w_h = {}
    for nm in ("Wq", "Wk", "Wv", "Wo"):
        w_h[nm] = nc.dram_tensor(nm + "_w", [D, D], F32, kind="ExternalInput").ap()
        w_h[nm + "_dW"] = nc.dram_tensor(nm + "_dW", [32, 16], F32, kind="ExternalInput").ap()
        w_h[nm + "_dB"] = nc.dram_tensor(nm + "_dB", [D, 32], F32, kind="ExternalInput").ap()
    wup_h = nc.dram_tensor("w_up", [I3, D], F32, kind="ExternalInput").ap()
    wg_h = nc.dram_tensor("w_gate", [I3, D], F32, kind="ExternalInput").ap()
    wd_h = nc.dram_tensor("w_down", [D, I3], F32, kind="ExternalInput").ap()
    hsel_h = nc.dram_tensor("c_hsel", [16, D], F16, kind="ExternalInput").ap()
    out_h = nc.dram_tensor("out", [T, D], F32, kind="ExternalOutput").ap()

    def scratch(name, shape, dt):
        if taps:
            return nc.dram_tensor(name, shape, dt, kind="ExternalOutput").ap()
        return nc.dram_tensor(name, shape, dt).ap()

    q_hbm = scratch("q_sp", [NCH, 128, T], F16)
    k_hbm = scratch("k_sp", [NCH, 128, T], F16)
    xmid_hbm = scratch("xmid_sp", [NT, 128, D], F32)
    up_hbm = scratch("up_sp", [NIC, 128, T], F16)
    attnfm_hbm = scratch("attnfm_sp", [NCH, 128, T], F16) if taps else None
    gate_hbm = scratch("gate_sp", [NIC, 128, T], F16) if taps else None
    lscr = nc.dram_tensor("lscr", [128, 2], F32).ap()
    lscr2 = nc.dram_tensor("lscr2", [2, 1], F32).ap()
    cc_in = [nc.dram_tensor(f"cc_in{r}", [1, 8], F32).ap() for r in range(4)]
    cc_out = [nc.dram_tensor(f"cc_out{r}", [1, 8], F32).ap() for r in range(4)]

    ctx = ExitStack()
    sub = {}
    alloc_order = []

    class _Stop(Exception):
        pass

    TAP = {}

    def sb(name, shape, dt, scoped=False, side=None):
        if scoped:
            es = ExitStack()
            t = es.enter_context(nc.sbuf_tensor(name, shape, dt, side=side))
            sub[name] = es
            alloc_order.append(name)
            TAP[name] = t
            return t
        t = ctx.enter_context(nc.sbuf_tensor(name, shape, dt, side=side))
        TAP[name] = t
        return t

    def freesb(*names):
        for n in names:
            sub.pop(n).close()
            alloc_order.remove(n)

    hsel = sb("hsel", [16, D], F16)
    ones1 = sb("ones1", [1, 128], F32)
    scal = sb("scal", [128, 12], F32)
    s_cst = sb("s_cst", [128, 2], F32)
    nc.const_aps.register(F32, MAGIC, s_cst[:, 0:1])
    nc.const_aps.register(F32, 1e-5, s_cst[:, 1:2])
    s_sum = sb("s_sum", [128, NT], F32)
    s_ssq = sb("s_ssq", [128, NT], F32)
    s_m = sb("s_m", [128, NT], F32)
    s_a = sb("s_a", [128, NT], F32)
    s_b = sb("s_b", [128, NT], F32)
    s_t0 = sb("s_t0", [128, NT], F32)
    lmax = sb("lmax", [128, 2], F32)
    s_mx = sb("s_mx", [128, 24], F32)
    junk16 = sb("junk16", [128, 1024], F16)
    s_h2j = sb("s_h2j", [128, 1024], F16)
    ccsb = sb("ccsb", [1, 16], F32)
    cc128 = sb("cc128", [2, 128], F32)
    s_dst = sb("s_dst", [1, 4, 512], F32, scoped=True, side="right")
    dwraw = sb("dwraw", [32, 16], F32)
    dwq = sb("dwq", [32, 16], F16)
    dbraw = sb("dbraw", [128, 8, 32], F32)
    dbq = sb("dbq", [128, 8, 32], F16)
    s_tb1 = sb("s_tb1", [128, 128], F16)
    s_tb2 = sb("s_tb2", [128, 128], F16)
    dbT = sb("dbT", [32, D], F16)
    deltaT = sb("deltaT", [16, D], F16)
    corr_sb = sb("corr_sb", [128, NCH, D], F16, scoped=True, side="right")

    psum = [ctx.enter_context(nc.psum_tensor(f"pb{i}", [128, 512], F32)) for i in range(8)]

    SEM = {}
    for s in ("lc", "ls", "A0", "A1", "B0", "B1", "D0", "D1", "D2", "D3",
              "G0", "G1", "GC", "GH", "T0", "T1", "tn", "vv", "sc", "bar", "ccs"):
        SEM[s] = [ctx.enter_context(nc.semaphore(s)), 0]

    def inc(instr, name, n=1):
        SEM[name][1] += n
        instr.then_inc(SEM[name][0], n)
        return SEM[name][1]

    def val(name):
        return SEM[name][1]

    def wait(e, name, v=None):
        e.wait_ge(SEM[name][0], SEM[name][1] if v is None else v)

    def sw(e, instr, sem):
        i = inc(instr, sem)
        wait(e, sem, i)
        return i

    blk = ctx.enter_context(nc.Block())

    def maybe_stop(n):
        if _DEV.get("stage", 99) != n:
            return
        dmp = _DEV.get("dump")

        @blk.gpsimd
        def _(e):
            if dmp:
                outv = out_h[:, :].rearrange("(a b) d -> a (b d)", a=128)
                off = 0
                i = None
                for name in dmp.split(","):
                    t = TAP[name]
                    sh = list(t.shape)
                    p = sh[0]
                    fl = 1
                    for d in sh[1:]:
                        fl *= d
                    src = t[:] if len(sh) == 2 else t[:].rearrange("p a b -> p (a b)")
                    i = inc(e.dma_start(out=outv[0:p, off:off + fl], in_=src), "G0", 16)
                    off += fl
            else:
                i = inc(e.dma_start(out=out_h[0:128, 0:16], in_=s_sum[:, :]), "G0", 16)
            wait(e, "G0", i)
        for nm in reversed(list(alloc_order)):
            sub.pop(nm).close()
        alloc_order.clear()
        raise _Stop()

    ENG = ("sync", "gpsimd", "vector", "scalar", "tensor")
    barn = [0]

    def barrier():
        barn[0] += 1
        tgt = 5 * barn[0]
        for en in ENG:
            @getattr(blk, en)
            def _(e, en=en):
                if en == "tensor":
                    e.drain().then_inc(SEM["bar"][0], 1)
                else:
                    e.nop().then_inc(SEM["bar"][0], 1)
        for en in ENG:
            @getattr(blk, en)
            def _(e):
                e.wait_ge(SEM["bar"][0], tgt)

    def drainT():
        @blk.sync
        def _(e):
            wait(e, "T0", val("T0"))
            wait(e, "T1", val("T1"))

    with ctx:
      try:
        @blk.sync
        def _(e):
            inc(e.dma_start(out=hsel[:], in_=hsel_h[:, :]), "lc", 16)
        ld_consts = val("lc")

        @blk.vector
        def _(e):
            e.memset(ones1[:], 1.0)
            e.memset(s_cst[:, 0:1], MAGIC)
            e.memset(s_cst[:, 1:2], 1e-5)
            e.memset(s_tb1[:, :], 0.0)
            inc(e.memset(lmax[:], 0.0), "vv")
        maybe_stop(0)

        s_xt = sb("s_xt", [128, 2, 1024], F32, scoped=True)
        s_wtB = sb("s_wtB", [128, NCH, D], F16, scoped=True)
        s_wraw = sb("s_wraw", [128, 2, 1024], F32, scoped=True)
        s_wq16 = sb("s_wq16", [128, 2, 1024], F16, scoped=True)
        s_wtA = sb("s_wtA", [128, NCH, D], F16, scoped=True)
        s_h = sb("s_h", [128, NCH, T], F16, scoped=True)
        s_htm = sb("s_htm", [128, 2, 1024], F16, scoped=True)

        ln_ld, ln_sc1, ln_v1, ln_sc2, ln_v2, ln_tt = [], [], [], [], [], []
        for tc in range(NT):
            @blk.scalar
            def _(e, tc=tc):
                if tc >= 2:
                    wait(e, "vv", ln_v2[tc - 2])
                ln_ld.append(inc(e.dma_start(out=s_xt[:, tc % 2, :],
                                             in_=x_h[tc * 128:(tc + 1) * 128, :]),
                                 f"A{tc % 2}", 16))

            @blk.scalar
            def _(e, tc=tc):
                wait(e, f"A{tc % 2}", ln_ld[tc])
                ln_sc1.append(inc(e.activation(out=junk16[:, :], in_=s_xt[:, tc % 2, :],
                                               func=AF.Square, accum_out=s_ssq[:, tc:tc + 1]),
                                  "sc"))

            @blk.vector
            def _(e, tc=tc):
                wait(e, f"A{tc % 2}", ln_ld[tc])
                sw(e, e.reduce_sum(s_sum[:, tc:tc + 1], s_xt[:, tc % 2, :], AXX), "vv")
                sw(e, e.tensor_scalar(out=s_m[:, tc:tc + 1], in0=s_sum[:, tc:tc + 1],
                                      scalar1=1.0 / D, scalar2=None, op0=ALU.mult), "vv")
                sw(e, e.tensor_mul(s_t0[:, tc:tc + 1], s_m[:, tc:tc + 1],
                                   s_m[:, tc:tc + 1]), "vv")
                wait(e, "sc", ln_sc1[tc])
                ln_v1.append(inc(e.scalar_tensor_tensor(out=s_t0[:, tc:tc + 1],
                                                        in0=s_ssq[:, tc:tc + 1],
                                                        scalar=1.0 / D, in1=s_t0[:, tc:tc + 1],
                                                        op0=ALU.mult, op1=ALU.subtract), "vv"))

            @blk.scalar
            def _(e, tc=tc):
                wait(e, "vv", ln_v1[tc])
                ln_sc2.append(inc(e.activation(out=s_t0[:, tc:tc + 1], in_=s_t0[:, tc:tc + 1],
                                               func=AF.Sqrt, bias=1e-5, scale=1.0), "sc"))

            @blk.vector
            def _(e, tc=tc):
                wait(e, "sc", ln_sc2[tc])
                sw(e, e.reciprocal(out=s_a[:, tc:tc + 1], in_=s_t0[:, tc:tc + 1]), "vv")
                sw(e, e.scalar_tensor_tensor(out=s_b[:, tc:tc + 1], in0=s_m[:, tc:tc + 1],
                                             scalar=-1.0, in1=s_a[:, tc:tc + 1],
                                             op0=ALU.mult, op1=ALU.mult), "vv")
                if tc >= 2:
                    wait(e, f"T{tc % 2}", ln_tt[tc - 2])
                ln_v2.append(inc(e.tensor_scalar(out=s_htm[:, tc % 2, :],
                                                 in0=s_xt[:, tc % 2, :],
                                                 scalar1=s_a[:, tc:tc + 1],
                                                 scalar2=s_b[:, tc:tc + 1],
                                                 op0=ALU.mult, op1=ALU.add), "vv"))

            @blk.sync
            def _(e, tc=tc):
                wait(e, "vv", ln_v2[tc])
                for cc in range(0, NCH, 2) if SPLIT else range(NCH):
                    inc(e.dma_start_transpose(s_h[:, cc, tc * 128:(tc + 1) * 128],
                                              s_htm[:, tc % 2, cc * 128:(cc + 1) * 128]),
                        f"T{tc % 2}", 16)

            if SPLIT:
                @blk.scalar
                def _(e, tc=tc):
                    wait(e, "vv", ln_v2[tc])
                    for cc in range(1, NCH, 2):
                        inc(e.dma_start_transpose(s_h[:, cc, tc * 128:(tc + 1) * 128],
                                                  s_htm[:, tc % 2, cc * 128:(cc + 1) * 128]),
                            f"T{tc % 2}", 16)
            ln_tt.append(val(f"T{tc % 2}"))
        drainT()
        barrier()
        maybe_stop(1)

        wpcut = [_DEV.get("wprep_cut")]

        def wprep(nm, wt):
            m = {}

            def cut(tag):
                if wpcut[0] == tag:
                    wpcut[0] = None
                    maybe_stop(2)

            @blk.sync
            def _(e):
                inc(e.dma_start(out=dwraw[:], in_=w_h[nm + "_dW"][:, :]), "ls", 16)
                m["lds"] = inc(e.dma_start(
                    out=dbraw[:, :, :],
                    in_=w_h[nm + "_dB"][:, :].rearrange("(a p) b -> p a b", p=128)), "ls", 16)

            @blk.vector
            def _(e):
                wait(e, "ls", m["lds"])
                sw(e, e.tensor_scalar(out=dwraw[:], in0=dwraw[:], scalar1=10.0, scalar2=MAGIC,
                                      op0=ALU.mult, op1=ALU.add), "vv")
                e.tensor_scalar(out=dwq[:], in0=dwraw[:], scalar1=MAGIC, scalar2=0.1,
                                op0=ALU.subtract, op1=ALU.mult)
                sw(e, e.tensor_scalar(out=dbraw[:].rearrange("p a b -> p (a b)"),
                                      in0=dbraw[:].rearrange("p a b -> p (a b)"),
                                      scalar1=10.0, scalar2=MAGIC,
                                      op0=ALU.mult, op1=ALU.add), "vv")
                m["vq"] = inc(e.tensor_scalar(out=dbq[:].rearrange("p a b -> p (a b)"),
                                              in0=dbraw[:].rearrange("p a b -> p (a b)"),
                                              scalar1=MAGIC, scalar2=0.1,
                                              op0=ALU.subtract, op1=ALU.mult), "vv")

            vbase = val("vv")
            for g in range(8):
                @blk.vector
                def _(e, g=g, vbase=vbase):
                    if g == 0:
                        wait(e, "vv", m["vq"])
                    else:
                        wait(e, "T0", m[f"tb{g - 1}"])
                    inc(e.tensor_copy(s_tb1[:, 0:32], dbq[:, g, :]), "vv")

                @blk.sync
                def _(e, g=g, vbase=vbase):
                    wait(e, "vv", vbase + 2 * g + 1)
                    m[f"tb{g}"] = inc(e.dma_start_transpose(s_tb2[:, :], s_tb1[:, :]),
                                      "T0", 16)

                @blk.vector
                def _(e, g=g):
                    wait(e, "T0", m[f"tb{g}"])
                    inc(e.tensor_copy(dbT[:, g * 128:(g + 1) * 128], s_tb2[0:32, :]), "vv")
            m["vq2"] = val("vv")
            cut("smalls")

            @blk.tensor
            def _(e):
                wait(e, "vv", m["vq2"])
                e.matmul(psum[6][0:16, :], dwq[:], dbT[:, 0:512], start=True, stop=True)
                m["t2"] = inc(e.matmul(psum[7][0:16, :], dwq[:], dbT[:, 512:1024],
                                       start=True, stop=True), "tn")

            @blk.scalar
            def _(e):
                wait(e, "tn", m["t2"])
                e.activation(out=deltaT[:, 0:512], in_=psum[6][0:16, :], func=AF.Copy,
                             bias=0.0, scale=1.0 / 32.0)
                m["s1"] = inc(e.activation(out=deltaT[:, 512:1024], in_=psum[7][0:16, :],
                                           func=AF.Copy, bias=0.0, scale=1.0 / 32.0), "sc")
            cut("delta")

            vv_corr0 = val("vv")
            for cc in range(NCH):
                m[f"corr{cc}"] = vv_corr0 + cc + 1

            @blk.tensor
            def _(e):
                wait(e, "lc", ld_consts)
                wait(e, "sc", m["s1"])
                for cc in range(NCH):
                    b0, b1 = (0, 1) if cc % 2 == 0 else (2, 3)
                    if cc >= 2:
                        wait(e, "vv", m[f"corr{cc - 2}"])
                    e.matmul(psum[b0][:, :], hsel[:, cc * 128:(cc + 1) * 128],
                             deltaT[:, 0:512], start=True, stop=True)
                    m[f"corrt{cc}"] = inc(e.matmul(psum[b1][:, :],
                                                   hsel[:, cc * 128:(cc + 1) * 128],
                                                   deltaT[:, 512:1024],
                                                   start=True, stop=True), "tn")

            @blk.vector
            def _(e):
                for cc in range(NCH):
                    b0, b1 = (0, 1) if cc % 2 == 0 else (2, 3)
                    wait(e, "tn", m[f"corrt{cc}"])
                    e.tensor_copy(corr_sb[:, cc, 0:512], psum[b0][:, :])
                    assert inc(e.tensor_copy(corr_sb[:, cc, 512:1024],
                                             psum[b1][:, :]), "vv") == m[f"corr{cc}"]
            cut("corr")

            ld_w, vq_w, tt_w, ev_w = [], [], [], []
            for oc in range(NCH):
                @blk.sync
                def _(e, oc=oc):
                    if oc >= 2:
                        wait(e, "vv", vq_w[oc - 2])
                    ld_w.append(inc(e.dma_start(out=s_wraw[:, oc % 2, :],
                                                in_=w_h[nm][oc * 128:(oc + 1) * 128, :]),
                                    f"B{oc % 2}", 16))

                @blk.vector
                def _(e, oc=oc):
                    wait(e, f"B{oc % 2}", ld_w[oc])
                    if oc >= 2:
                        wait(e, f"T{oc % 2}", tt_w[oc - 2])
                    e.tensor_scalar(out=s_wraw[:, oc % 2, :], in0=s_wraw[:, oc % 2, :],
                                    scalar1=10.0, scalar2=MAGIC, op0=ALU.mult, op1=ALU.add)
                    vq_w.append(inc(e.tensor_scalar(out=s_wq16[:, oc % 2, :],
                                                    in0=s_wraw[:, oc % 2, :],
                                                    scalar1=MAGIC, scalar2=0.1,
                                                    op0=ALU.subtract, op1=ALU.mult), "vv"))

                @blk.sync
                def _(e, oc=oc):
                    wait(e, "vv", vq_w[oc])
                    if oc >= 2:
                        wait(e, "vv", ev_w[oc - 2])
                    for cc in range(0, NCH, 2) if SPLIT else range(NCH):
                        inc(e.dma_start_transpose(wt[:, cc, oc * 128:(oc + 1) * 128],
                                                  s_wq16[:, oc % 2, cc * 128:(cc + 1) * 128]),
                            f"T{oc % 2}", 16)

                if SPLIT:
                    @blk.scalar
                    def _(e, oc=oc):
                        wait(e, "vv", vq_w[oc])
                        if oc >= 2:
                            wait(e, "vv", ev_w[oc - 2])
                        for cc in range(1, NCH, 2):
                            inc(e.dma_start_transpose(wt[:, cc, oc * 128:(oc + 1) * 128],
                                                      s_wq16[:, oc % 2,
                                                             cc * 128:(cc + 1) * 128]),
                                f"T{oc % 2}", 16)
                tt_w.append(val(f"T{oc % 2}"))

                @blk.vector
                def _(e, oc=oc):
                    wait(e, f"T{oc % 2}", tt_w[oc])
                    if oc == 0:
                        wait(e, "vv", m["corr7"])
                    last = None
                    for cc in range(NCH):
                        last = e.tensor_add(wt[:, cc, oc * 128:(oc + 1) * 128],
                                            wt[:, cc, oc * 128:(oc + 1) * 128],
                                            corr_sb[:, cc, oc * 128:(oc + 1) * 128])
                    ev_w.append(inc(last, "vv"))
            drainT()
            barrier()

        _pqn = [0]

        def proj_spill(wt, dst_hbm):
            _pqn[0] += 1
            pqname = f"s_pq{_pqn[0]}"
            s_pq = sb(pqname, [128, 2, T], F16, scoped=True, side="right")
            tn_p, ev_p, st_p = [], [], []
            for oc in range(NCH):
                @blk.tensor
                def _(e, oc=oc):
                    if oc >= 2:
                        wait(e, "sc", ev_p[oc - 2])
                    banks = [psum[4 * (oc % 2) + t] for t in range(4)]
                    mm = None
                    for cc in range(NCH):
                        for t in range(4):
                            mm = e.matmul(banks[t][:, :], wt[:, cc, oc * 128:(oc + 1) * 128],
                                          s_h[:, cc, t * 512:(t + 1) * 512],
                                          start=(cc == 0), stop=(cc == NCH - 1))
                    tn_p.append(inc(mm, "tn"))

                @blk.scalar
                def _(e, oc=oc):
                    wait(e, "tn", tn_p[oc])
                    if oc >= 2:
                        wait(e, f"G{oc % 2}", st_p[oc - 2])
                    banks = [psum[4 * (oc % 2) + t] for t in range(4)]
                    last = None
                    for t in range(4):
                        last = e.copy(s_pq[:, oc % 2, t * 512:(t + 1) * 512], banks[t][:, :])
                    ev_p.append(inc(last, "sc"))

                @blk.gpsimd
                def _(e, oc=oc):
                    wait(e, "sc", ev_p[oc])
                    st_p.append(inc(e.dma_start(out=dst_hbm[oc, :, :],
                                                in_=s_pq[:, oc % 2, :]), f"G{oc % 2}", 16))

            @blk.gpsimd
            def _(e):
                wait(e, "G0", val("G0"))
                wait(e, "G1", val("G1"))
            barrier()
            freesb(pqname)

        wprep("Wq", s_wtA)
        maybe_stop(2)
        proj_spill(s_wtA, q_hbm)
        maybe_stop(3)
        wprep("Wk", s_wtB)
        proj_spill(s_wtB, k_hbm)
        wprep("Wv", s_wtA)

        s_v = sb("s_v", [128, NT, 1040], F16, scoped=True, side="right")

        @blk.vector
        def _(e):
            last = None
            for tc in range(NT):
                last = e.memset(s_v[:, tc, :].rearrange("p (h c) -> p h c", c=65)[:, :, 64:65],
                                1.0)
            inc(last, "vv")
        v_ones = val("vv")

        tn_v, ev_v = [], []
        for tc in range(NT):
            @blk.tensor
            def _(e, tc=tc):
                if tc == 0:
                    wait(e, "vv", v_ones)
                if tc >= 2:
                    wait(e, "vv", ev_v[tc - 2])
                banks = [psum[4 * (tc % 2)], psum[4 * (tc % 2) + 1]]
                mm = None
                for cc in range(NCH):
                    for t in range(2):
                        mm = e.matmul(banks[t][:, :], s_h[:, cc, tc * 128:(tc + 1) * 128],
                                      s_wtA[:, cc, t * 512:(t + 1) * 512],
                                      start=(cc == 0), stop=(cc == NCH - 1))
                tn_v.append(inc(mm, "tn"))

            @blk.vector
            def _(e, tc=tc):
                wait(e, "tn", tn_v[tc])
                banks = [psum[4 * (tc % 2)], psum[4 * (tc % 2) + 1]]
                last = None
                for hh in range(16):
                    bank = banks[hh // 8]
                    last = e.tensor_copy(s_v[:, tc, hh * 65:hh * 65 + 64],
                                         bank[:, (hh % 8) * 64:(hh % 8) * 64 + 64])
                ev_v.append(inc(last, "vv"))
        barrier()
        maybe_stop(4)
        freesb("s_htm", "s_h", "s_wtA")
        wprep("Wo", s_wtB)
        maybe_stop(5)

        s_o = sb("s_o", [128, NCH, T], F16, scoped=True)
        s_den = sb("s_den", [16, T], F32, scoped=True)
        s_qh = sb("s_qh", [128, 2, T], F16, scoped=True, side="right")
        s_kh = sb("s_kh", [128, 2, T], F16, scoped=True, side="right")
        s_pt = sb("s_pt", [128, 4, 512], F16, scoped=True, side="right")

        tn0, sc0, vv0 = val("tn"), val("sc"), val("vv")
        tn_qk, tn_pv = {}, {}
        tn_c = tn0
        att_done = []
        for p in range(8):
            for ic in range(4):
                for jc in range(16):
                    tn_c += 1
                    tn_qk[(p, ic, jc)] = tn_c
                    if jc >= 1:
                        tn_c += 1
                        tn_pv[(p, ic, jc - 1)] = tn_c
                tn_c += 1
                tn_pv[(p, ic, 15)] = tn_c
            att_done.append(tn_c)
        exp_sc = {}
        sc_c = sc0
        for p in range(8):
            for ic in range(4):
                for jc in range(16):
                    sc_c += 1
                    exp_sc[(p, ic, jc)] = sc_c
        pv_evac = {}
        vv_c = vv0
        for p in range(8):
            for ic in range(4):
                vv_c += 1
                pv_evac[(p, ic)] = vv_c
        ldA0, ldA1 = val("A0"), val("A1")
        ld_qk = [(ldA0 if p % 2 == 0 else ldA1) + 32 * (p // 2 + 1) for p in range(8)]
        flat_ids = {(p, ic, jc): (p * 4 + ic) * 16 + jc
                    for p in range(8) for ic in range(4) for jc in range(16)}
        inv_flat = {v: k for k, v in flat_ids.items()}

        @blk.sync
        def _(e):
            for p in range(-2, 8):
                if p + 2 < 8:
                    pp = p + 2
                    if pp >= 2:
                        wait(e, "tn", att_done[pp - 2])
                    inc(e.dma_start(out=s_qh[:, pp % 2, :], in_=q_hbm[pp, :, :]),
                        f"A{pp % 2}", 16)
                    inc(e.dma_start(out=s_kh[:, pp % 2, :], in_=k_hbm[pp, :, :]),
                        f"A{pp % 2}", 16)
                if p >= 0:
                    for ic in range(4):
                        fi = p * 4 + ic
                        wait(e, "vv", vv0 + fi + 1)
                        for b in range(2):
                            hh = 2 * p + b
                            d = fi * 2 + b
                            inc(e.dma_start(out=s_den[hh:hh + 1, ic * 512:(ic + 1) * 512],
                                            in_=s_dst[0:1, d % 4, :]), f"D{d % 4}", 16)
        assert val("A0") == ld_qk[6] and val("A1") == ld_qk[7]

        @blk.tensor
        def _(e):
            for p in range(8):
                wait(e, f"A{p % 2}", ld_qk[p])
                hA, hB = 2 * p, 2 * p + 1
                for ic in range(4):
                    fi = p * 4 + ic
                    pvA, pvB = psum[4 + fi % 2], psum[6 + fi % 2]
                    if fi >= 2:
                        wait(e, "vv", pv_evac[((fi - 2) // 4, (fi - 2) % 4)])
                    for jc in range(16):
                        g = flat_ids[(p, ic, jc)]
                        if g >= 2:
                            wait(e, "sc", exp_sc[inv_flat[g - 2]])
                        sA, sB = psum[jc % 2], psum[2 + jc % 2]
                        e.matmul(sA[:, :], s_kh[0:64, p % 2, jc * 128:(jc + 1) * 128],
                                 s_qh[0:64, p % 2, ic * 512:(ic + 1) * 512],
                                 start=True, stop=True, tile_position=(0, 0))
                        inc(e.matmul(sB[:, :], s_kh[64:128, p % 2, jc * 128:(jc + 1) * 128],
                                     s_qh[64:128, p % 2, ic * 512:(ic + 1) * 512],
                                     start=True, stop=True, tile_position=(64, 0)), "tn")
                        if jc >= 1:
                            wait(e, "sc", exp_sc[(p, ic, jc - 1)])
                            e.matmul(pvA[0:65, :], s_v[:, jc - 1, hA * 65:hA * 65 + 65],
                                     s_pt[:, 2 * ((jc - 1) % 2), :],
                                     start=(jc == 1), stop=False)
                            inc(e.matmul(pvB[0:65, :], s_v[:, jc - 1, hB * 65:hB * 65 + 65],
                                         s_pt[:, 2 * ((jc - 1) % 2) + 1, :],
                                         start=(jc == 1), stop=False), "tn")
                    wait(e, "sc", exp_sc[(p, ic, 15)])
                    e.matmul(pvA[0:65, :], s_v[:, 15, hA * 65:hA * 65 + 65],
                             s_pt[:, 2 * (15 % 2), :], start=False, stop=True)
                    inc(e.matmul(pvB[0:65, :], s_v[:, 15, hB * 65:hB * 65 + 65],
                                 s_pt[:, 2 * (15 % 2) + 1, :], start=False, stop=True), "tn")
        assert val("tn") == tn_c, (val("tn"), tn_c)

        @blk.scalar
        def _(e):
            for p in range(8):
                for ic in range(4):
                    for jc in range(16):
                        g = flat_ids[(p, ic, jc)]
                        wait(e, "tn", tn_qk[(p, ic, jc)])
                        if g >= 2:
                            wait(e, "tn", tn_pv[inv_flat[g - 2]])
                        sA, sB = psum[jc % 2], psum[2 + jc % 2]
                        e.activation(out=s_pt[:, 2 * (jc % 2), :], in_=sA[:, :],
                                     func=AF.Exp, bias=0.0, scale=0.125)
                        inc(e.activation(out=s_pt[:, 2 * (jc % 2) + 1, :], in_=sB[:, :],
                                         func=AF.Exp, bias=0.0, scale=0.125), "sc")
        assert val("sc") == sc_c

        @blk.vector
        def _(e):
            for p in range(8):
                hA, hB = 2 * p, 2 * p + 1
                for ic in range(4):
                    fi = p * 4 + ic
                    wait(e, "tn", tn_pv[(p, ic, 15)])
                    pvA, pvB = psum[4 + fi % 2], psum[6 + fi % 2]
                    last = None
                    for b, (hh, pv) in enumerate(((hA, pvA), (hB, pvB))):
                        oc, row = hh // 2, 64 * (hh % 2)
                        e.tensor_copy(s_o[row:row + 64, oc, ic * 512:(ic + 1) * 512],
                                      pv[0:64, :])
                        d = fi * 2 + b
                        if d >= 4:
                            wait(e, f"D{d % 4}", 16 * (d // 4))
                        last = e.tensor_copy(s_dst[0:1, d % 4, :], pv[64:65, :])
                    inc(last, "vv")
        assert val("vv") == vv_c

        barrier()
        maybe_stop(6)
        freesb("s_pt", "s_kh", "s_qh", "s_v")

        s_recip = sb("s_recip", [16, T], F16, scoped=True)

        @blk.vector
        def _(e):
            for dsem in ("D0", "D1", "D2", "D3"):
                wait(e, dsem, val(dsem))
            with nc.allow_low_precision(reason="softmax reciprocal feeds an f16 matmul operand"):
                inc(e.reciprocal(out=s_recip[:], in_=s_den[:]), "vv")
        v_recip = val("vv")

        tn_db, ev_db = [], []
        for oc in range(NCH):
            @blk.tensor
            def _(e, oc=oc):
                if oc == 0:
                    wait(e, "vv", v_recip)
                if oc >= 2:
                    wait(e, "vv", ev_db[oc - 2])
                banks = [psum[4 * (oc % 2) + t] for t in range(4)]
                mm = None
                for t in range(4):
                    mm = e.matmul(banks[t][:, :], hsel[:, oc * 128:(oc + 1) * 128],
                                  s_recip[:, t * 512:(t + 1) * 512], start=True, stop=True)
                tn_db.append(inc(mm, "tn"))

            @blk.vector
            def _(e, oc=oc):
                wait(e, "tn", tn_db[oc])
                banks = [psum[4 * (oc % 2) + t] for t in range(4)]
                last = None
                for t in range(4):
                    last = e.tensor_mul(s_o[:, oc, t * 512:(t + 1) * 512],
                                        s_o[:, oc, t * 512:(t + 1) * 512], banks[t][:, :])
                ev_db.append(inc(last, "vv"))
        barrier()
        maybe_stop(7)
        freesb("s_recip", "s_den")

        s_xm = sb("s_xm", [128, 2, 1024], F32, scoped=True, side="right")
        s_aT = sb("s_aT", [128, 2, 1024], F16, scoped=True, side="right")
        s_attn = sb("s_attn", [128, NCH, T], F16, scoped=True, side="right")
        tn_wo, ev_wo = [], []
        for oc in range(NCH):
            @blk.tensor
            def _(e, oc=oc):
                if oc >= 2:
                    wait(e, "sc", ev_wo[oc - 2])
                banks = [psum[4 * (oc % 2) + t] for t in range(4)]
                mm = None
                for cc in range(NCH):
                    for t in range(4):
                        mm = e.matmul(banks[t][:, :], s_wtB[:, cc, oc * 128:(oc + 1) * 128],
                                      s_o[:, cc, t * 512:(t + 1) * 512],
                                      start=(cc == 0), stop=(cc == NCH - 1))
                tn_wo.append(inc(mm, "tn"))

            @blk.scalar
            def _(e, oc=oc):
                wait(e, "tn", tn_wo[oc])
                banks = [psum[4 * (oc % 2) + t] for t in range(4)]
                last = None
                for t in range(4):
                    last = e.copy(s_attn[:, oc, t * 512:(t + 1) * 512], banks[t][:, :])
                ev_wo.append(inc(last, "sc"))
        if taps:
            @blk.gpsimd
            def _(e):
                wait(e, "sc", ev_wo[-1])
                for oc in range(NCH):
                    inc(e.dma_start(out=attnfm_hbm[oc, :, :], in_=s_attn[:, oc, :]), "GC", 16)
                wait(e, "GC", val("GC"))
        barrier()
        maybe_stop(8)
        freesb("s_o", "s_wq16", "s_wraw", "s_wtB")

        x2_ld, x2_tt, x2_v1, x2_sc1, x2_v2, x2_sc2, x2_v3, x2_gd = [], [], [], [], [], [], [], []
        for tc in range(NT):
            @blk.scalar
            def _(e, tc=tc):
                if tc >= 2:
                    wait(e, "vv", x2_v1[tc - 2])
                x2_ld.append(inc(e.dma_start(out=s_xt[:, tc % 2, :],
                                             in_=x_h[tc * 128:(tc + 1) * 128, :]),
                                 f"A{tc % 2}", 16))

            @blk.sync
            def _(e, tc=tc):
                if tc >= 2:
                    wait(e, "vv", x2_v1[tc - 2])
                for cc in range(0, NCH, 2) if SPLIT else range(NCH):
                    inc(e.dma_start_transpose(s_aT[:, tc % 2, cc * 128:(cc + 1) * 128],
                                              s_attn[:, cc, tc * 128:(tc + 1) * 128]),
                        f"T{tc % 2}", 16)

            if SPLIT:
                @blk.scalar
                def _(e, tc=tc):
                    if tc >= 2:
                        wait(e, "vv", x2_v1[tc - 2])
                    for cc in range(1, NCH, 2):
                        inc(e.dma_start_transpose(s_aT[:, tc % 2, cc * 128:(cc + 1) * 128],
                                                  s_attn[:, cc, tc * 128:(tc + 1) * 128]),
                            f"T{tc % 2}", 16)
            x2_tt.append(val(f"T{tc % 2}"))

            @blk.vector
            def _(e, tc=tc):
                wait(e, f"T{tc % 2}", x2_tt[tc])
                wait(e, f"A{tc % 2}", x2_ld[tc])
                if tc >= 2:
                    wait(e, f"G{tc % 2}", x2_gd[tc - 2])
                e.tensor_add(s_xm[:, tc % 2, :], s_xt[:, tc % 2, :], s_aT[:, tc % 2, :])
                sw(e, e.reduce_sum(s_sum[:, tc:tc + 1], s_xm[:, tc % 2, :], AXX), "vv")
                x2_v1.append(inc(e.tensor_scalar(out=s_m[:, tc:tc + 1],
                                                 in0=s_sum[:, tc:tc + 1], scalar1=1.0 / D,
                                                 scalar2=None, op0=ALU.mult), "vv"))

            @blk.scalar
            def _(e, tc=tc):
                wait(e, "vv", x2_v1[tc])
                x2_sc1.append(inc(e.activation(out=junk16[:, :], in_=s_xm[:, tc % 2, :],
                                               func=AF.Square, accum_out=s_ssq[:, tc:tc + 1]),
                                  "sc"))

            @blk.vector
            def _(e, tc=tc):
                wait(e, "sc", x2_sc1[tc])
                sw(e, e.tensor_mul(s_t0[:, tc:tc + 1], s_m[:, tc:tc + 1],
                                   s_m[:, tc:tc + 1]), "vv")
                x2_v2.append(inc(e.scalar_tensor_tensor(out=s_t0[:, tc:tc + 1],
                                                        in0=s_ssq[:, tc:tc + 1],
                                                        scalar=1.0 / D,
                                                        in1=s_t0[:, tc:tc + 1],
                                                        op0=ALU.mult, op1=ALU.subtract), "vv"))

            @blk.scalar
            def _(e, tc=tc):
                wait(e, "vv", x2_v2[tc])
                x2_sc2.append(inc(e.activation(out=s_t0[:, tc:tc + 1], in_=s_t0[:, tc:tc + 1],
                                               func=AF.Sqrt, bias=1e-5, scale=1.0), "sc"))

            @blk.vector
            def _(e, tc=tc):
                wait(e, "sc", x2_sc2[tc])
                sw(e, e.reciprocal(out=s_a[:, tc:tc + 1], in_=s_t0[:, tc:tc + 1]), "vv")
                sw(e, e.scalar_tensor_tensor(out=s_b[:, tc:tc + 1], in0=s_m[:, tc:tc + 1],
                                             scalar=-1.0, in1=s_a[:, tc:tc + 1],
                                             op0=ALU.mult, op1=ALU.mult), "vv")
                e.tensor_scalar(out=s_h2j[:, :], in0=s_xm[:, tc % 2, :],
                                scalar1=s_a[:, tc:tc + 1], scalar2=s_b[:, tc:tc + 1],
                                op0=ALU.mult, op1=ALU.add)
                x2_v3.append(inc(e.reduce_max(s_mx[:, tc:tc + 1], s_h2j[:, :], AXX,
                                              apply_absolute_value=True), "vv"))

            @blk.gpsimd
            def _(e, tc=tc):
                wait(e, "vv", x2_v3[tc])
                x2_gd.append(inc(e.dma_start(out=xmid_hbm[tc, :, :],
                                             in_=s_xm[:, tc % 2, :]), f"G{tc % 2}", 16))

        @blk.vector
        def _(e):
            wait(e, "vv", x2_v3[-1])
            sw(e, e.reduce_max(lmax[:, 0:1], s_mx[:, 0:NT], AXX,
                               apply_absolute_value=True), "vv")

        @blk.gpsimd
        def _(e):
            wait(e, "G0", val("G0"))
            wait(e, "G1", val("G1"))
        drainT()
        barrier()
        maybe_stop(9)
        freesb("s_attn", "s_aT")

        def cc_round(r, nscal, derive):
            vnow = val("vv")

            @blk.sync
            def _(e):
                wait(e, "vv", vnow)
                i = inc(e.dma_start(out=lscr[:, :], in_=lmax[:, 0:2]), "GH", 16)
                wait(e, "GH", i)
                i = inc(e.dma_start(out=cc128[0:2, 0:128],
                                    in_=lscr[:, :].rearrange("p c -> c p"),
                                    allow_slow_non_contiguous=True), "GH", 16)
                wait(e, "GH", i)

            @blk.vector
            def _(e):
                wait(e, "GH", val("GH"))
                inc(e.reduce_max(cc128[:, 0:1], cc128[:, :], AXX), "vv")
            t2 = val("vv")

            @blk.sync
            def _(e):
                wait(e, "vv", t2)
                i = inc(e.dma_start(out=lscr2[:, :], in_=cc128[0:2, 0:1]), "GH", 16)
                wait(e, "GH", i)
                i = inc(e.dma_start(out=ccsb[0:1, 0:2],
                                    in_=lscr2[:, :].rearrange("p c -> c p"),
                                    allow_slow_non_contiguous=True), "GH", 16)
                wait(e, "GH", i)

            @blk.vector
            def _(e):
                wait(e, "GH", val("GH"))
                for j in range(nscal, 8):
                    e.tensor_copy(ccsb[0:1, j:j + 1], ccsb[0:1, j % nscal:j % nscal + 1])
                inc(e.memset(lmax[:, :], 0.0), "vv")
            t4 = val("vv")

            @blk.gpsimd
            def _(e):
                wait(e, "vv", t4)
                i = inc(e.dma_start(out=cc_in[r][:, :], in_=ccsb[0:1, 0:8]), "GC", 16)
                wait(e, "GC", i)
                inc(e.collective_compute(
                    "AllReduce", ALU.max, replica_groups=[list(range(NCORE))],
                    ins=[cc_in[r][:, :]], outs=[cc_out[r][:, :]]), "ccs")
                wait(e, "ccs")
                inc(e.dma_start(out=ccsb[0:1, 8:16], in_=cc_out[r][:, :]), "GC", 16)
            t5 = val("GC")

            @blk.vector
            def _(e):
                wait(e, "GC", t5)
                inc(derive(e), "vv")
            return val("vv")

        def cc_bcast(t6, ncols, dst0):
            @blk.tensor
            def _(e):
                wait(e, "vv", t6)
                inc(e.matmul(psum[0][:, 0:ncols], ones1[:], ccsb[0:1, 0:ncols],
                             start=True, stop=True), "tn")
            t7 = val("tn")

            @blk.vector
            def _(e):
                wait(e, "tn", t7)
                inc(e.tensor_copy(scal[:, dst0:dst0 + ncols], psum[0][:, 0:ncols]), "vv")
            return val("vv")

        def derive_r1(e):
            sw(e, e.tensor_scalar(out=ccsb[0:1, 1:2], in0=ccsb[0:1, 8:9],
                                  scalar1=1.0 / 32767.0, scalar2=1e-12,
                                  op0=ALU.mult, op1=ALU.max), "vv")
            return e.reciprocal(out=ccsb[0:1, 0:1], in_=ccsb[0:1, 1:2])
        v_r1 = cc_bcast(cc_round(0, 1, derive_r1), 2, 0)

        s_fraw = sb("s_fraw", [128, 2, 1024], F32, scoped=True)
        s_fq16 = sb("s_fq16", [128, 2, 1024], F16, scoped=True)
        s_fs = sb("s_fs", [128, 32], F32, scoped=True)
        s_frs = sb("s_frs", [128, 32], F32, scoped=True)
        s_gate = sb("s_gate", [128, NIC, T], F16, scoped=True)
        s_xq = sb("s_xq", [128, NCH, T], F16, scoped=True)
        s_xqtm = sb("s_xqtm", [128, 2, 1024], F16, scoped=True, side="right")
        s_z2 = sb("s_z2", [128, 1024], F32, scoped=True, side="right")

        @blk.vector
        def _(e):
            wait(e, "vv", v_r1)
            e.tensor_scalar(out=s_a[:, :], in0=s_a[:, :], scalar1=scal[:, 0:1],
                            scalar2=None, op0=ALU.mult)
            inc(e.tensor_scalar(out=s_b[:, :], in0=s_b[:, :], scalar1=scal[:, 0:1],
                                scalar2=MAGIC, op0=ALU.mult, op1=ALU.add), "vv")
        v_ab = val("vv")

        xq_ld, xq_v, xq_tt = [], [], []
        for tc in range(NT):
            @blk.scalar
            def _(e, tc=tc):
                if tc >= 2:
                    wait(e, "vv", xq_v[tc - 2])
                wait(e, f"G{tc % 2}", x2_gd[tc])
                xq_ld.append(inc(e.dma_start(out=s_xm[:, tc % 2, :],
                                             in_=xmid_hbm[tc, :, :]), f"A{tc % 2}", 16))

            @blk.vector
            def _(e, tc=tc):
                if tc == 0:
                    wait(e, "vv", v_ab)
                wait(e, f"A{tc % 2}", xq_ld[tc])
                if tc >= 2:
                    wait(e, f"T{tc % 2}", xq_tt[tc - 2])
                e.tensor_scalar(out=s_z2[:, :], in0=s_xm[:, tc % 2, :],
                                scalar1=s_a[:, tc:tc + 1], scalar2=s_b[:, tc:tc + 1],
                                op0=ALU.mult, op1=ALU.add)
                xq_v.append(inc(e.tensor_scalar(out=s_xqtm[:, tc % 2, :],
                                                in0=s_z2[:, :], scalar1=MAGIC,
                                                scalar2=scal[:, 1:2],
                                                op0=ALU.subtract, op1=ALU.mult), "vv"))

            @blk.sync
            def _(e, tc=tc):
                wait(e, "vv", xq_v[tc])
                for cc in range(0, NCH, 2) if SPLIT else range(NCH):
                    inc(e.dma_start_transpose(s_xq[:, cc, tc * 128:(tc + 1) * 128],
                                              s_xqtm[:, tc % 2, cc * 128:(cc + 1) * 128]),
                        f"T{tc % 2}", 16)

            if SPLIT:
                @blk.scalar
                def _(e, tc=tc):
                    wait(e, "vv", xq_v[tc])
                    for cc in range(1, NCH, 2):
                        inc(e.dma_start_transpose(s_xq[:, cc, tc * 128:(tc + 1) * 128],
                                                  s_xqtm[:, tc % 2, cc * 128:(cc + 1) * 128]),
                            f"T{tc % 2}", 16)
            xq_tt.append(val(f"T{tc % 2}"))
        drainT()
        barrier()
        maybe_stop(10)
        freesb("s_z2", "s_xqtm", "s_xm")

        s_up2 = sb("s_up2", [128, 2, T], F16, scoped=True, side="right")
        s_fwt = sb("s_fwt", [128, 2, NCH, 128], F16, scoped=True, side="right")

        def ffn_side(w_src, dst, spill, lmax_col):
            f_ld, f_vq, f_tt, mm_tn, ev_sc, mx_vv, st_gd = [], [], [], [], [], [], []
            for ic in range(NIC):
                @blk.scalar
                def _(e, ic=ic):
                    if ic >= 2:
                        wait(e, "vv", f_vq[ic - 2])
                    f_ld.append(inc(e.dma_start(out=s_fraw[:, ic % 2, :],
                                                in_=w_src[ic * 128:(ic + 1) * 128, :]),
                                    f"B{ic % 2}", 16))

                @blk.vector
                def _(e, ic=ic):
                    wait(e, f"B{ic % 2}", f_ld[ic])
                    if ic >= 2:
                        wait(e, f"T{ic % 2}", f_tt[ic - 2])
                    sw(e, e.reduce_max(s_fs[:, :], s_fraw[:, ic % 2, :]
                                       .rearrange("p (b k) -> p b k", k=32), AXX,
                                       apply_absolute_value=True), "vv")
                    sw(e, e.tensor_scalar(out=s_fs[:, :], in0=s_fs[:, :], scalar1=1.0 / 7.0,
                                          scalar2=1e-12, op0=ALU.mult, op1=ALU.max), "vv")
                    sw(e, e.reciprocal(out=s_frs[:, :], in_=s_fs[:, :]), "vv")
                    e.tensor_tensor(out=s_fraw[:, ic % 2, :].rearrange("p (b k) -> p b k", k=32),
                                    in0=s_fraw[:, ic % 2, :].rearrange("p (b k) -> p b k", k=32),
                                    in1=s_frs[:, :].rearrange("p (b o) -> p b o", o=1)
                                    .broadcast_to([128, 32, 32]), op=ALU.mult)
                    e.tensor_scalar(out=s_fraw[:, ic % 2, :], in0=s_fraw[:, ic % 2, :],
                                    scalar1=MAGIC, scalar2=MAGIC,
                                    op0=ALU.add, op1=ALU.subtract)
                    f_vq.append(inc(e.tensor_tensor(
                        out=s_fq16[:, ic % 2, :].rearrange("p (b k) -> p b k", k=32),
                        in0=s_fraw[:, ic % 2, :].rearrange("p (b k) -> p b k", k=32),
                        in1=s_fs[:, :].rearrange("p (b o) -> p b o", o=1)
                        .broadcast_to([128, 32, 32]), op=ALU.mult), "vv"))

                @blk.sync
                def _(e, ic=ic):
                    wait(e, "vv", f_vq[ic])
                    if ic >= 2:
                        wait(e, "tn", mm_tn[ic - 2])
                    for cc in range(0, NCH, 2) if SPLIT else range(NCH):
                        inc(e.dma_start_transpose(s_fwt[:, ic % 2, cc, :],
                                                  s_fq16[:, ic % 2, cc * 128:(cc + 1) * 128]),
                            f"T{ic % 2}", 16)

                if SPLIT:
                    @blk.scalar
                    def _(e, ic=ic):
                        wait(e, "vv", f_vq[ic])
                        if ic >= 2:
                            wait(e, "tn", mm_tn[ic - 2])
                        for cc in range(1, NCH, 2):
                            inc(e.dma_start_transpose(s_fwt[:, ic % 2, cc, :],
                                                      s_fq16[:, ic % 2,
                                                             cc * 128:(cc + 1) * 128]),
                                f"T{ic % 2}", 16)
                f_tt.append(val(f"T{ic % 2}"))

                @blk.tensor
                def _(e, ic=ic):
                    wait(e, f"T{ic % 2}", f_tt[ic])
                    if ic >= 2:
                        wait(e, "sc", ev_sc[ic - 2])
                    banks = [psum[4 * (ic % 2) + t] for t in range(4)]
                    mm = None
                    for cc in range(NCH):
                        for t in range(4):
                            mm = e.matmul(banks[t][:, :], s_fwt[:, ic % 2, cc, :],
                                          s_xq[:, cc, t * 512:(t + 1) * 512],
                                          start=(cc == 0), stop=(cc == NCH - 1))
                    mm_tn.append(inc(mm, "tn"))

                @blk.scalar
                def _(e, ic=ic):
                    wait(e, "tn", mm_tn[ic])
                    if dst is None and ic >= 2:
                        wait(e, f"G{ic % 2}", st_gd[ic - 2])
                    banks = [psum[4 * (ic % 2) + t] for t in range(4)]
                    last = None
                    for t in range(4):
                        tgt = s_up2[:, ic % 2, t * 512:(t + 1) * 512] if dst is None \
                            else dst[:, ic, t * 512:(t + 1) * 512]
                        last = e.copy(tgt, banks[t][:, :])
                    ev_sc.append(inc(last, "sc"))

                @blk.vector
                def _(e, ic=ic):
                    wait(e, "sc", ev_sc[ic])
                    tgt = s_up2[:, ic % 2, :] if dst is None else dst[:, ic, :]
                    mx_vv.append(inc(e.reduce_max(s_mx[:, ic:ic + 1], tgt, AXX,
                                                  apply_absolute_value=True), "vv"))

                if dst is None:
                    @blk.gpsimd
                    def _(e, ic=ic):
                        wait(e, "vv", mx_vv[ic])
                        st_gd.append(inc(e.dma_start(out=spill[ic, :, :],
                                                     in_=s_up2[:, ic % 2, :]),
                                         f"G{ic % 2}", 16))

            @blk.vector
            def _(e):
                wait(e, "vv", mx_vv[-1])
                sw(e, e.reduce_max(lmax[:, lmax_col:lmax_col + 1], s_mx[:, 0:NIC], AXX,
                                   apply_absolute_value=True), "vv")

            if dst is None:
                @blk.gpsimd
                def _(e):
                    wait(e, "G0", val("G0"))
                    wait(e, "G1", val("G1"))
            drainT()
            barrier()

        ffn_side(wup_h, None, up_hbm, 0)
        ffn_side(wg_h, s_gate, None, 1)
        if taps:
            @blk.gpsimd
            def _(e):
                wait(e, "vv", val("vv"))
                for ic in range(NIC):
                    inc(e.dma_start(out=gate_hbm[ic, :, :], in_=s_gate[:, ic, :]), "GC", 16)
                wait(e, "GC", val("GC"))
            barrier()
        maybe_stop(11)
        freesb("s_fwt")
        freesb("s_xq")

        def derive_r2(e):
            sw(e, e.tensor_scalar(out=ccsb[0:1, 10:12], in0=ccsb[0:1, 8:10],
                                  scalar1=1.0 / 32767.0, scalar2=1e-12,
                                  op0=ALU.mult, op1=ALU.max), "vv")
            sw(e, e.reciprocal(out=ccsb[0:1, 12:14], in_=ccsb[0:1, 10:12]), "vv")
            e.tensor_copy(ccsb[0:1, 0:1], ccsb[0:1, 12:13])
            e.tensor_copy(ccsb[0:1, 1:2], ccsb[0:1, 10:11])
            e.tensor_copy(ccsb[0:1, 2:3], ccsb[0:1, 13:14])
            e.tensor_copy(ccsb[0:1, 3:4], ccsb[0:1, 11:12])
            return e.tensor_scalar(out=ccsb[0:1, 4:5], in0=ccsb[0:1, 11:12], scalar1=-MAGIC,
                                   scalar2=None, op0=ALU.mult)
        v_r2 = cc_bcast(cc_round(1, 2, derive_r2), 5, 2)

        s_tmp16 = sb("s_tmp16", [128, 2, T], F16, scoped=True, side="right")
        s_zA = sb("s_zA", [128, 2, T], F32, scoped=True, side="right")
        s_zB = sb("s_zB", [128, 2, T], F32, scoped=True, side="right")
        ge_sc, ge_vv = [], []
        for ic in range(NIC):
            @blk.scalar
            def _(e, ic=ic):
                if ic == 0:
                    wait(e, "vv", v_r2)
                if ic >= 2:
                    wait(e, "vv", ge_vv[ic - 2])
                e.activation(out=s_zA[:, ic % 2, :], in_=s_gate[:, ic, :], func=AF.Copy,
                             bias=MAGIC, scale=scal[:, 4:5])
                ge_sc.append(inc(e.activation(out=s_zB[:, ic % 2, :], in_=s_zA[:, ic % 2, :],
                                              func=AF.Sigmoid, bias=scal[:, 6:7],
                                              scale=scal[:, 5:6]), "sc"))

            @blk.vector
            def _(e, ic=ic):
                wait(e, "sc", ge_sc[ic])
                e.tensor_scalar(out=s_tmp16[:, ic % 2, :], in0=s_zA[:, ic % 2, :],
                                scalar1=MAGIC, scalar2=scal[:, 5:6],
                                op0=ALU.subtract, op1=ALU.mult)
                e.tensor_scalar(out=s_zA[:, ic % 2, :], in0=s_zB[:, ic % 2, :],
                                scalar1=65536.0, scalar2=MAGIC, op0=ALU.mult, op1=ALU.add)
                e.tensor_scalar(out=s_zB[:, ic % 2, :], in0=s_zA[:, ic % 2, :],
                                scalar1=MAGIC + 65535.0, scalar2=float(2.0 ** -16),
                                op0=ALU.min, op1=ALU.mult)
                e.scalar_tensor_tensor(out=s_gate[:, ic, :], in0=s_zB[:, ic % 2, :],
                                       scalar=-192.0, in1=s_tmp16[:, ic % 2, :],
                                       op0=ALU.add, op1=ALU.mult)
                ge_vv.append(inc(e.reduce_max(s_mx[:, ic:ic + 1], s_gate[:, ic, :], AXX,
                                              apply_absolute_value=True), "vv"))

        @blk.vector
        def _(e):
            wait(e, "vv", ge_vv[-1])
            sw(e, e.reduce_max(lmax[:, 0:1], s_mx[:, 0:NIC], AXX,
                               apply_absolute_value=True), "vv")
        barrier()
        maybe_stop(12)

        def derive_r3(e):
            sw(e, e.tensor_scalar(out=ccsb[0:1, 1:2], in0=ccsb[0:1, 8:9],
                                  scalar1=1.0 / 32767.0, scalar2=1e-12,
                                  op0=ALU.mult, op1=ALU.max), "vv")
            return e.reciprocal(out=ccsb[0:1, 0:1], in_=ccsb[0:1, 1:2])
        v_r3 = cc_bcast(cc_round(2, 1, derive_r3), 2, 7)

        @blk.vector
        def _(e):
            wait(e, "vv", v_r3)
            inc(e.tensor_mul(scal[:, 3:4], scal[:, 3:4], scal[:, 8:9]), "vv")
        v_s2u3 = val("vv")

        op_ld, op_sc, op_vv = [], [], []
        for ic in range(NIC):
            @blk.sync
            def _(e, ic=ic):
                if ic >= 2:
                    wait(e, "vv", op_vv[ic - 2])
                op_ld.append(inc(e.dma_start(out=s_up2[:, ic % 2, :],
                                             in_=up_hbm[ic, :, :]), f"A{ic % 2}", 16))

            @blk.scalar
            def _(e, ic=ic):
                if ic == 0:
                    wait(e, "vv", v_s2u3)
                wait(e, f"A{ic % 2}", op_ld[ic])
                if ic >= 2:
                    wait(e, "vv", op_vv[ic - 2])
                e.activation(out=s_zA[:, ic % 2, :], in_=s_up2[:, ic % 2, :], func=AF.Copy,
                             bias=MAGIC, scale=scal[:, 2:3])
                op_sc.append(inc(e.activation(out=s_zB[:, ic % 2, :], in_=s_gate[:, ic, :],
                                              func=AF.Copy, bias=MAGIC, scale=scal[:, 7:8]),
                                 "sc"))

            @blk.vector
            def _(e, ic=ic):
                wait(e, "sc", op_sc[ic])
                e.tensor_scalar(out=s_tmp16[:, ic % 2, :], in0=s_zA[:, ic % 2, :],
                                scalar1=MAGIC, scalar2=scal[:, 3:4],
                                op0=ALU.subtract, op1=ALU.mult)
                e.scalar_tensor_tensor(out=s_gate[:, ic, :], in0=s_zB[:, ic % 2, :],
                                       scalar=-MAGIC, in1=s_tmp16[:, ic % 2, :],
                                       op0=ALU.add, op1=ALU.mult)
                op_vv.append(inc(e.reduce_max(s_mx[:, ic:ic + 1], s_gate[:, ic, :], AXX,
                                              apply_absolute_value=True), "vv"))

        @blk.vector
        def _(e):
            wait(e, "vv", op_vv[-1])
            sw(e, e.reduce_max(lmax[:, 0:1], s_mx[:, 0:NIC], AXX,
                               apply_absolute_value=True), "vv")
        barrier()
        maybe_stop(13)

        def derive_r4(e):
            sw(e, e.tensor_scalar(out=ccsb[0:1, 1:2], in0=ccsb[0:1, 8:9],
                                  scalar1=1.0 / 32767.0, scalar2=1e-12,
                                  op0=ALU.mult, op1=ALU.max), "vv")
            return e.reciprocal(out=ccsb[0:1, 0:1], in_=ccsb[0:1, 1:2])
        v_r4 = cc_bcast(cc_round(3, 1, derive_r4), 2, 8)

        oi_sc, oi_vv = [], []
        for ic in range(NIC):
            @blk.scalar
            def _(e, ic=ic):
                if ic == 0:
                    wait(e, "vv", v_r4)
                if ic >= 2:
                    wait(e, "vv", oi_vv[ic - 2])
                oi_sc.append(inc(e.activation(out=s_zA[:, ic % 2, :], in_=s_gate[:, ic, :],
                                              func=AF.Copy, bias=MAGIC, scale=scal[:, 8:9]),
                                 "sc"))

            @blk.vector
            def _(e, ic=ic):
                wait(e, "sc", oi_sc[ic])
                oi_vv.append(inc(e.tensor_scalar(out=s_gate[:, ic, :],
                                                 in0=s_zA[:, ic % 2, :], scalar1=MAGIC,
                                                 scalar2=None, op0=ALU.subtract), "vv"))
        barrier()
        freesb("s_zB", "s_zA", "s_tmp16", "s_up2", "corr_sb", "s_dst")

        s_dwt2 = sb("s_dwt2", [128, NIC, 1024], F16, scoped=True)
        s_dot16 = sb("s_dot16", [128, 2, 1024], F16, scoped=True)

        def prev_slot(oc, s3):
            if s3 == 2:
                return (oc, 0)
            if s3 == 1:
                return (oc - 1, 1) if oc >= 1 else None
            return (oc - 1, 2) if oc >= 1 else None

        d_ld, d_vq, d_tt = {}, {}, {}
        for oc in range(NCH):
            for s3 in range(3):
                key = (oc, s3)

                @blk.scalar
                def _(e, oc=oc, s3=s3, key=key):
                    prev = prev_slot(oc, s3)
                    if prev in d_vq:
                        wait(e, "vv", d_vq[prev])
                    d_ld[key] = inc(e.dma_start(
                        out=s_fraw[:, s3 % 2, :],
                        in_=wd_h[oc * 128:(oc + 1) * 128, s3 * 1024:(s3 + 1) * 1024]),
                        f"B{s3 % 2}", 16)

                @blk.vector
                def _(e, key=key, s3=s3, oc=oc):
                    wait(e, f"B{s3 % 2}", d_ld[key])
                    prev = prev_slot(oc, s3)
                    if prev in d_tt:
                        wait(e, f"T{(3 * prev[0] + prev[1]) % 2}", d_tt[prev])
                    sw(e, e.reduce_max(s_fs[:, :], s_fraw[:, s3 % 2, :]
                                       .rearrange("p (b k) -> p b k", k=32), AXX,
                                       apply_absolute_value=True), "vv")
                    sw(e, e.tensor_scalar(out=s_fs[:, :], in0=s_fs[:, :], scalar1=1.0 / 7.0,
                                          scalar2=1e-12, op0=ALU.mult, op1=ALU.max), "vv")
                    sw(e, e.reciprocal(out=s_frs[:, :], in_=s_fs[:, :]), "vv")
                    e.tensor_tensor(out=s_fraw[:, s3 % 2, :].rearrange("p (b k) -> p b k", k=32),
                                    in0=s_fraw[:, s3 % 2, :].rearrange("p (b k) -> p b k", k=32),
                                    in1=s_frs[:, :].rearrange("p (b o) -> p b o", o=1)
                                    .broadcast_to([128, 32, 32]), op=ALU.mult)
                    e.tensor_scalar(out=s_fraw[:, s3 % 2, :], in0=s_fraw[:, s3 % 2, :],
                                    scalar1=MAGIC, scalar2=MAGIC, op0=ALU.add, op1=ALU.subtract)
                    d_vq[key] = inc(e.tensor_tensor(
                        out=s_fq16[:, s3 % 2, :].rearrange("p (b k) -> p b k", k=32),
                        in0=s_fraw[:, s3 % 2, :].rearrange("p (b k) -> p b k", k=32),
                        in1=s_fs[:, :].rearrange("p (b o) -> p b o", o=1)
                        .broadcast_to([128, 32, 32]), op=ALU.mult), "vv")

                @blk.sync
                def _(e, key=key, s3=s3, oc=oc):
                    wait(e, "vv", d_vq[key])
                    for cc in range(8):
                        inc(e.dma_start_transpose(
                            s_dwt2[:, s3 * 8 + cc, oc * 128:(oc + 1) * 128],
                            s_fq16[:, s3 % 2, cc * 128:(cc + 1) * 128]),
                            f"T{(oc * 3 + s3) % 2}", 16)
                d_tt[key] = val(f"T{(oc * 3 + s3) % 2}")

        wT0, wT1 = val("T0"), val("T1")
        dn_tn, dn_ev, o_ld, o_vv, o_gd = [], [], [], [], []
        for tc in range(NT):
            @blk.scalar
            def _(e, tc=tc):
                if tc < 2:
                    wait(e, "vv", d_vq[(NCH - 1, 2 - tc)])
                else:
                    wait(e, f"G{tc % 2}", o_gd[tc - 2])
                wait(e, f"G{tc % 2}", x2_gd[tc])
                o_ld.append(inc(e.dma_start(out=s_fraw[:, tc % 2, :],
                                            in_=xmid_hbm[tc, :, :]), f"A{tc % 2}", 16))

            @blk.tensor
            def _(e, tc=tc):
                if tc == 0:
                    wait(e, "T0", wT0)
                    wait(e, "T1", wT1)
                    wait(e, "vv", oi_vv[-1])
                if tc >= 4:
                    wait(e, "sc", dn_ev[tc - 4])
                banks = [psum[2 * (tc % 4)], psum[2 * (tc % 4) + 1]]
                mm = None
                for ic in range(NIC):
                    for h in range(2):
                        mm = e.matmul(banks[h][:, :],
                                      s_gate[:, ic, tc * 128:(tc + 1) * 128],
                                      s_dwt2[:, ic, h * 512:(h + 1) * 512],
                                      start=(ic == 0), stop=(ic == NIC - 1))
                dn_tn.append(inc(mm, "tn"))

            @blk.scalar
            def _(e, tc=tc):
                wait(e, "tn", dn_tn[tc])
                if tc >= 2:
                    wait(e, "vv", o_vv[tc - 2])
                banks = [psum[2 * (tc % 4)], psum[2 * (tc % 4) + 1]]
                e.activation(out=s_dot16[:, tc % 2, 0:512], in_=banks[0][:, :],
                             func=AF.Copy, bias=0.0, scale=scal[:, 9:10])
                dn_ev.append(inc(e.activation(out=s_dot16[:, tc % 2, 512:1024],
                                              in_=banks[1][:, :], func=AF.Copy,
                                              bias=0.0, scale=scal[:, 9:10]), "sc"))

            @blk.vector
            def _(e, tc=tc):
                wait(e, "sc", dn_ev[tc])
                wait(e, f"A{tc % 2}", o_ld[tc])
                o_vv.append(inc(e.tensor_add(s_fraw[:, tc % 2, :], s_fraw[:, tc % 2, :],
                                             s_dot16[:, tc % 2, :]), "vv"))

            @blk.gpsimd
            def _(e, tc=tc):
                wait(e, "vv", o_vv[tc])
                o_gd.append(inc(e.dma_start(out=out_h[tc * 128:(tc + 1) * 128, :],
                                            in_=s_fraw[:, tc % 2, :]), f"G{tc % 2}", 16))

        @blk.gpsimd
        def _(e):
            wait(e, "G0", val("G0"))
            wait(e, "G1", val("G1"))

        @blk.sync
        def _(e):
            wait(e, "G0", val("G0"))
            wait(e, "G1", val("G1"))

        freesb("s_dot16", "s_dwt2", "s_gate",
               "s_frs", "s_fs", "s_fq16", "s_fraw", "s_xt")
      except _Stop:
        pass

    return nc


def _consts():
    hsel = np.zeros((16, D), dtype=np.float16)
    for g in range(16):
        hsel[g, g * 64:(g + 1) * 64] = 1.0
    return hsel


def kernel(**inputs):
    nc = build_program()
    hsel = _consts()
    x = np.ascontiguousarray(np.asarray(inputs["L0_attn_x"], dtype=np.float32))
    base = {
        "c_hsel": hsel,
        "w_up": np.ascontiguousarray(np.asarray(inputs["L0_ffn_w_up"], dtype=np.float32)),
        "w_gate": np.ascontiguousarray(np.asarray(inputs["L0_ffn_w_gate"], dtype=np.float32)),
        "w_down": np.ascontiguousarray(np.asarray(inputs["L0_ffn_w_down"], dtype=np.float32)),
    }
    for nm in ("Wq", "Wk", "Wv", "Wo"):
        base[nm + "_w"] = np.ascontiguousarray(np.asarray(inputs[f"L0_attn_{nm}_w"], dtype=np.float32))
        base[nm + "_dW"] = np.ascontiguousarray(np.asarray(inputs[f"L0_attn_{nm}_dW"], dtype=np.float32))
        base[nm + "_dB"] = np.ascontiguousarray(np.asarray(inputs[f"L0_attn_{nm}_dB"], dtype=np.float32))
    in_maps = [dict(base, x=np.ascontiguousarray(x[i])) for i in range(NCORE)]
    res = run_bass_kernel_spmd(nc, in_maps, core_ids=list(range(NCORE)))
    return np.stack([res.results[i]["out"] for i in range(NCORE)], axis=0)


# version 12 (final)
# speedup vs baseline: 1.0712x; 1.0109x over previous
import numpy as np
from contextlib import ExitStack

import concourse.bass as bass
import concourse.mybir as mybir
from concourse.bass_utils import run_bass_kernel_spmd

F32 = mybir.dt.float32
F16 = mybir.dt.float16
AF = mybir.ActivationFunctionType
ALU = mybir.AluOpType
AXX = mybir.AxisListType.X

MAGIC = float(1.5 * 2 ** 23)
T, D, I3 = 2048, 1024, 3072
NT, NCH, NIC = T // 128, D // 128, I3 // 128
NCORE = 8

_DEV = {"taps": False, "stage": 99}


def build_program():
    taps = _DEV["taps"]
    SPLIT = _DEV.get("split", False)
    nc = bass.Bass("TRN2", target_bir_lowering=False, debug=False,
                   detect_race_conditions=False)

    x_h = nc.dram_tensor("x", [T, D], F32, kind="ExternalInput").ap()
    w_h = {}
    for nm in ("Wq", "Wk", "Wv", "Wo"):
        w_h[nm] = nc.dram_tensor(nm + "_w", [D, D], F32, kind="ExternalInput").ap()
        w_h[nm + "_dW"] = nc.dram_tensor(nm + "_dW", [32, 16], F32, kind="ExternalInput").ap()
        w_h[nm + "_dB"] = nc.dram_tensor(nm + "_dB", [D, 32], F32, kind="ExternalInput").ap()
    wup_h = nc.dram_tensor("w_up", [I3, D], F32, kind="ExternalInput").ap()
    wg_h = nc.dram_tensor("w_gate", [I3, D], F32, kind="ExternalInput").ap()
    wd_h = nc.dram_tensor("w_down", [D, I3], F32, kind="ExternalInput").ap()
    hsel_h = nc.dram_tensor("c_hsel", [16, D], F16, kind="ExternalInput").ap()
    out_h = nc.dram_tensor("out", [T, D], F32, kind="ExternalOutput").ap()

    def scratch(name, shape, dt):
        if taps:
            return nc.dram_tensor(name, shape, dt, kind="ExternalOutput").ap()
        return nc.dram_tensor(name, shape, dt).ap()

    q_hbm = scratch("q_sp", [NCH, 128, T], F16)
    k_hbm = scratch("k_sp", [NCH, 128, T], F16)
    xmid_hbm = scratch("xmid_sp", [NT, 128, D], F32)
    up_hbm = scratch("up_sp", [NIC, 128, T], F16)
    attnfm_hbm = scratch("attnfm_sp", [NCH, 128, T], F16) if taps else None
    gate_hbm = scratch("gate_sp", [NIC, 128, T], F16) if taps else None
    lscr = nc.dram_tensor("lscr", [128, 2], F32).ap()
    lscr2 = nc.dram_tensor("lscr2", [2, 1], F32).ap()
    cc_in = [nc.dram_tensor(f"cc_in{r}", [1, 8], F32).ap() for r in range(4)]
    cc_out = [nc.dram_tensor(f"cc_out{r}", [1, 8], F32).ap() for r in range(4)]

    ctx = ExitStack()
    sub = {}
    alloc_order = []

    class _Stop(Exception):
        pass

    TAP = {}

    def sb(name, shape, dt, scoped=False, side=None):
        if scoped:
            es = ExitStack()
            t = es.enter_context(nc.sbuf_tensor(name, shape, dt, side=side))
            sub[name] = es
            alloc_order.append(name)
            TAP[name] = t
            return t
        t = ctx.enter_context(nc.sbuf_tensor(name, shape, dt, side=side))
        TAP[name] = t
        return t

    def freesb(*names):
        for n in names:
            sub.pop(n).close()
            alloc_order.remove(n)

    hsel = sb("hsel", [16, D], F16)
    ones1 = sb("ones1", [1, 128], F32)
    scal = sb("scal", [128, 12], F32)
    s_cst = sb("s_cst", [128, 2], F32)
    nc.const_aps.register(F32, MAGIC, s_cst[:, 0:1])
    nc.const_aps.register(F32, 1e-5, s_cst[:, 1:2])
    s_sum = sb("s_sum", [128, NT], F32)
    s_ssq = sb("s_ssq", [128, NT], F32)
    s_m = sb("s_m", [128, NT], F32)
    s_a = sb("s_a", [128, NT], F32)
    s_b = sb("s_b", [128, NT], F32)
    s_t0 = sb("s_t0", [128, NT], F32)
    lmax = sb("lmax", [128, 2], F32)
    s_mx = sb("s_mx", [128, 24], F32)
    junk16 = sb("junk16", [128, 1024], F16)
    s_h2j = sb("s_h2j", [128, 1024], F16)
    ccsb = sb("ccsb", [1, 16], F32)
    cc128 = sb("cc128", [2, 128], F32)
    s_dst = sb("s_dst", [1, 4, 512], F32, scoped=True, side="right")
    dwraw = sb("dwraw", [32, 16], F32)
    dwq = sb("dwq", [32, 16], F16)
    dbraw = sb("dbraw", [128, 8, 32], F32)
    dbq = sb("dbq", [128, 8, 32], F16)
    s_tb1 = sb("s_tb1", [128, 128], F16)
    s_tb2 = sb("s_tb2", [128, 128], F16)
    dbT = sb("dbT", [32, D], F16)
    deltaT = sb("deltaT", [16, D], F16)
    corr_sb = sb("corr_sb", [128, NCH, D], F16, scoped=True, side="right")

    psum = [ctx.enter_context(nc.psum_tensor(f"pb{i}", [128, 512], F32)) for i in range(8)]

    SEM = {}
    for s in ("lc", "ls", "A0", "A1", "B0", "B1", "D0", "D1", "D2", "D3",
              "G0", "G1", "GC", "GH", "T0", "T1", "tn", "vv", "sc", "bar", "ccs"):
        SEM[s] = [ctx.enter_context(nc.semaphore(s)), 0]

    def inc(instr, name, n=1):
        SEM[name][1] += n
        instr.then_inc(SEM[name][0], n)
        return SEM[name][1]

    def val(name):
        return SEM[name][1]

    def wait(e, name, v=None):
        e.wait_ge(SEM[name][0], SEM[name][1] if v is None else v)

    def sw(e, instr, sem):
        i = inc(instr, sem)
        wait(e, sem, i)
        return i

    blk = ctx.enter_context(nc.Block())

    def maybe_stop(n):
        if _DEV.get("stage", 99) != n:
            return
        dmp = _DEV.get("dump")

        @blk.gpsimd
        def _(e):
            if dmp:
                outv = out_h[:, :].rearrange("(a b) d -> a (b d)", a=128)
                off = 0
                i = None
                for name in dmp.split(","):
                    t = TAP[name]
                    sh = list(t.shape)
                    p = sh[0]
                    fl = 1
                    for d in sh[1:]:
                        fl *= d
                    src = t[:] if len(sh) == 2 else t[:].rearrange("p a b -> p (a b)")
                    i = inc(e.dma_start(out=outv[0:p, off:off + fl], in_=src), "G0", 16)
                    off += fl
            else:
                i = inc(e.dma_start(out=out_h[0:128, 0:16], in_=s_sum[:, :]), "G0", 16)
            wait(e, "G0", i)
        for nm in reversed(list(alloc_order)):
            sub.pop(nm).close()
        alloc_order.clear()
        raise _Stop()

    ENG = ("sync", "gpsimd", "vector", "scalar", "tensor")
    barn = [0]

    def barrier():
        barn[0] += 1
        tgt = 5 * barn[0]
        for en in ENG:
            @getattr(blk, en)
            def _(e, en=en):
                if en == "tensor":
                    e.drain().then_inc(SEM["bar"][0], 1)
                else:
                    e.nop().then_inc(SEM["bar"][0], 1)
        for en in ENG:
            @getattr(blk, en)
            def _(e):
                e.wait_ge(SEM["bar"][0], tgt)

    def drainT():
        @blk.sync
        def _(e):
            wait(e, "T0", val("T0"))
            wait(e, "T1", val("T1"))

    with ctx:
      try:
        @blk.sync
        def _(e):
            inc(e.dma_start(out=hsel[:], in_=hsel_h[:, :]), "lc", 16)
        ld_consts = val("lc")

        @blk.vector
        def _(e):
            e.memset(ones1[:], 1.0)
            e.memset(s_cst[:, 0:1], MAGIC)
            e.memset(s_cst[:, 1:2], 1e-5)
            e.memset(s_tb1[:, :], 0.0)
            inc(e.memset(lmax[:], 0.0), "vv")
        maybe_stop(0)

        s_xt = sb("s_xt", [128, 2, 1024], F32, scoped=True)
        s_wtB = sb("s_wtB", [128, NCH, D], F16, scoped=True)
        s_wraw = sb("s_wraw", [128, 2, 1024], F32, scoped=True)
        s_wq16 = sb("s_wq16", [128, 2, 1024], F16, scoped=True)
        s_wtA = sb("s_wtA", [128, NCH, D], F16, scoped=True)
        s_h = sb("s_h", [128, NCH, T], F16, scoped=True)
        s_htm = sb("s_htm", [128, 2, 1024], F16, scoped=True)

        ln_ld, ln_sc1, ln_v1, ln_sc2, ln_v2, ln_tt = [], [], [], [], [], []
        for tc in range(NT):
            @blk.scalar
            def _(e, tc=tc):
                if tc >= 2:
                    wait(e, "vv", ln_v2[tc - 2])
                ln_ld.append(inc(e.dma_start(out=s_xt[:, tc % 2, :],
                                             in_=x_h[tc * 128:(tc + 1) * 128, :]),
                                 f"A{tc % 2}", 16))

            @blk.scalar
            def _(e, tc=tc):
                wait(e, f"A{tc % 2}", ln_ld[tc])
                ln_sc1.append(inc(e.activation(out=junk16[:, :], in_=s_xt[:, tc % 2, :],
                                               func=AF.Square, accum_out=s_ssq[:, tc:tc + 1]),
                                  "sc"))

            @blk.vector
            def _(e, tc=tc):
                wait(e, f"A{tc % 2}", ln_ld[tc])
                sw(e, e.reduce_sum(s_sum[:, tc:tc + 1], s_xt[:, tc % 2, :], AXX), "vv")
                sw(e, e.tensor_scalar(out=s_m[:, tc:tc + 1], in0=s_sum[:, tc:tc + 1],
                                      scalar1=1.0 / D, scalar2=None, op0=ALU.mult), "vv")
                sw(e, e.tensor_mul(s_t0[:, tc:tc + 1], s_m[:, tc:tc + 1],
                                   s_m[:, tc:tc + 1]), "vv")
                wait(e, "sc", ln_sc1[tc])
                ln_v1.append(inc(e.scalar_tensor_tensor(out=s_t0[:, tc:tc + 1],
                                                        in0=s_ssq[:, tc:tc + 1],
                                                        scalar=1.0 / D, in1=s_t0[:, tc:tc + 1],
                                                        op0=ALU.mult, op1=ALU.subtract), "vv"))

            @blk.scalar
            def _(e, tc=tc):
                wait(e, "vv", ln_v1[tc])
                ln_sc2.append(inc(e.activation(out=s_t0[:, tc:tc + 1], in_=s_t0[:, tc:tc + 1],
                                               func=AF.Sqrt, bias=1e-5, scale=1.0), "sc"))

            @blk.vector
            def _(e, tc=tc):
                wait(e, "sc", ln_sc2[tc])
                sw(e, e.reciprocal(out=s_a[:, tc:tc + 1], in_=s_t0[:, tc:tc + 1]), "vv")
                sw(e, e.scalar_tensor_tensor(out=s_b[:, tc:tc + 1], in0=s_m[:, tc:tc + 1],
                                             scalar=-1.0, in1=s_a[:, tc:tc + 1],
                                             op0=ALU.mult, op1=ALU.mult), "vv")
                if tc >= 2:
                    wait(e, f"T{tc % 2}", ln_tt[tc - 2])
                ln_v2.append(inc(e.tensor_scalar(out=s_htm[:, tc % 2, :],
                                                 in0=s_xt[:, tc % 2, :],
                                                 scalar1=s_a[:, tc:tc + 1],
                                                 scalar2=s_b[:, tc:tc + 1],
                                                 op0=ALU.mult, op1=ALU.add), "vv"))

            @blk.sync
            def _(e, tc=tc):
                wait(e, "vv", ln_v2[tc])
                for cc in range(0, NCH, 2) if SPLIT else range(NCH):
                    inc(e.dma_start_transpose(s_h[:, cc, tc * 128:(tc + 1) * 128],
                                              s_htm[:, tc % 2, cc * 128:(cc + 1) * 128]),
                        f"T{tc % 2}", 16)

            if SPLIT:
                @blk.scalar
                def _(e, tc=tc):
                    wait(e, "vv", ln_v2[tc])
                    for cc in range(1, NCH, 2):
                        inc(e.dma_start_transpose(s_h[:, cc, tc * 128:(tc + 1) * 128],
                                                  s_htm[:, tc % 2, cc * 128:(cc + 1) * 128]),
                            f"T{tc % 2}", 16)
            ln_tt.append(val(f"T{tc % 2}"))
        drainT()
        barrier()
        maybe_stop(1)

        wpcut = [_DEV.get("wprep_cut")]

        def wprep(nm, wt):
            m = {}

            def cut(tag):
                if wpcut[0] == tag:
                    wpcut[0] = None
                    maybe_stop(2)

            @blk.sync
            def _(e):
                inc(e.dma_start(out=dwraw[:], in_=w_h[nm + "_dW"][:, :]), "ls", 16)
                m["lds"] = inc(e.dma_start(
                    out=dbraw[:, :, :],
                    in_=w_h[nm + "_dB"][:, :].rearrange("(a p) b -> p a b", p=128)), "ls", 16)

            @blk.vector
            def _(e):
                wait(e, "ls", m["lds"])
                sw(e, e.tensor_scalar(out=dwraw[:], in0=dwraw[:], scalar1=10.0, scalar2=MAGIC,
                                      op0=ALU.mult, op1=ALU.add), "vv")
                e.tensor_scalar(out=dwq[:], in0=dwraw[:], scalar1=MAGIC, scalar2=0.1,
                                op0=ALU.subtract, op1=ALU.mult)
                sw(e, e.tensor_scalar(out=dbraw[:].rearrange("p a b -> p (a b)"),
                                      in0=dbraw[:].rearrange("p a b -> p (a b)"),
                                      scalar1=10.0, scalar2=MAGIC,
                                      op0=ALU.mult, op1=ALU.add), "vv")
                m["vq"] = inc(e.tensor_scalar(out=dbq[:].rearrange("p a b -> p (a b)"),
                                              in0=dbraw[:].rearrange("p a b -> p (a b)"),
                                              scalar1=MAGIC, scalar2=0.1,
                                              op0=ALU.subtract, op1=ALU.mult), "vv")

            vbase = val("vv")
            for g in range(8):
                @blk.vector
                def _(e, g=g, vbase=vbase):
                    if g == 0:
                        wait(e, "vv", m["vq"])
                    else:
                        wait(e, "T0", m[f"tb{g - 1}"])
                    inc(e.tensor_copy(s_tb1[:, 0:32], dbq[:, g, :]), "vv")

                @blk.sync
                def _(e, g=g, vbase=vbase):
                    wait(e, "vv", vbase + 2 * g + 1)
                    m[f"tb{g}"] = inc(e.dma_start_transpose(s_tb2[:, :], s_tb1[:, :]),
                                      "T0", 16)

                @blk.vector
                def _(e, g=g):
                    wait(e, "T0", m[f"tb{g}"])
                    inc(e.tensor_copy(dbT[:, g * 128:(g + 1) * 128], s_tb2[0:32, :]), "vv")
            m["vq2"] = val("vv")
            cut("smalls")

            @blk.tensor
            def _(e):
                wait(e, "vv", m["vq2"])
                e.matmul(psum[6][0:16, :], dwq[:], dbT[:, 0:512], start=True, stop=True)
                m["t2"] = inc(e.matmul(psum[7][0:16, :], dwq[:], dbT[:, 512:1024],
                                       start=True, stop=True), "tn")

            @blk.scalar
            def _(e):
                wait(e, "tn", m["t2"])
                e.activation(out=deltaT[:, 0:512], in_=psum[6][0:16, :], func=AF.Copy,
                             bias=0.0, scale=1.0 / 32.0)
                m["s1"] = inc(e.activation(out=deltaT[:, 512:1024], in_=psum[7][0:16, :],
                                           func=AF.Copy, bias=0.0, scale=1.0 / 32.0), "sc")
            cut("delta")

            vv_corr0 = val("vv")
            for cc in range(NCH):
                m[f"corr{cc}"] = vv_corr0 + cc + 1

            @blk.tensor
            def _(e):
                wait(e, "lc", ld_consts)
                wait(e, "sc", m["s1"])
                for cc in range(NCH):
                    b0, b1 = (0, 1) if cc % 2 == 0 else (2, 3)
                    if cc >= 2:
                        wait(e, "vv", m[f"corr{cc - 2}"])
                    e.matmul(psum[b0][:, :], hsel[:, cc * 128:(cc + 1) * 128],
                             deltaT[:, 0:512], start=True, stop=True)
                    m[f"corrt{cc}"] = inc(e.matmul(psum[b1][:, :],
                                                   hsel[:, cc * 128:(cc + 1) * 128],
                                                   deltaT[:, 512:1024],
                                                   start=True, stop=True), "tn")

            @blk.vector
            def _(e):
                for cc in range(NCH):
                    b0, b1 = (0, 1) if cc % 2 == 0 else (2, 3)
                    wait(e, "tn", m[f"corrt{cc}"])
                    e.tensor_copy(corr_sb[:, cc, 0:512], psum[b0][:, :])
                    assert inc(e.tensor_copy(corr_sb[:, cc, 512:1024],
                                             psum[b1][:, :]), "vv") == m[f"corr{cc}"]
            cut("corr")

            ld_w, vq_w, tt_w, ev_w = [], [], [], []
            for oc in range(NCH):
                @blk.sync
                def _(e, oc=oc):
                    if oc >= 2:
                        wait(e, "vv", vq_w[oc - 2])
                    ld_w.append(inc(e.dma_start(out=s_wraw[:, oc % 2, :],
                                                in_=w_h[nm][oc * 128:(oc + 1) * 128, :]),
                                    f"B{oc % 2}", 16))

                @blk.vector
                def _(e, oc=oc):
                    wait(e, f"B{oc % 2}", ld_w[oc])
                    if oc >= 2:
                        wait(e, f"T{oc % 2}", tt_w[oc - 2])
                    e.tensor_scalar(out=s_wraw[:, oc % 2, :], in0=s_wraw[:, oc % 2, :],
                                    scalar1=10.0, scalar2=MAGIC, op0=ALU.mult, op1=ALU.add)
                    vq_w.append(inc(e.tensor_scalar(out=s_wq16[:, oc % 2, :],
                                                    in0=s_wraw[:, oc % 2, :],
                                                    scalar1=MAGIC, scalar2=0.1,
                                                    op0=ALU.subtract, op1=ALU.mult), "vv"))

                @blk.sync
                def _(e, oc=oc):
                    wait(e, "vv", vq_w[oc])
                    if oc >= 2:
                        wait(e, "vv", ev_w[oc - 2])
                    for cc in range(0, NCH, 2) if SPLIT else range(NCH):
                        inc(e.dma_start_transpose(wt[:, cc, oc * 128:(oc + 1) * 128],
                                                  s_wq16[:, oc % 2, cc * 128:(cc + 1) * 128]),
                            f"T{oc % 2}", 16)

                if SPLIT:
                    @blk.scalar
                    def _(e, oc=oc):
                        wait(e, "vv", vq_w[oc])
                        if oc >= 2:
                            wait(e, "vv", ev_w[oc - 2])
                        for cc in range(1, NCH, 2):
                            inc(e.dma_start_transpose(wt[:, cc, oc * 128:(oc + 1) * 128],
                                                      s_wq16[:, oc % 2,
                                                             cc * 128:(cc + 1) * 128]),
                                f"T{oc % 2}", 16)
                tt_w.append(val(f"T{oc % 2}"))

                @blk.vector
                def _(e, oc=oc):
                    wait(e, f"T{oc % 2}", tt_w[oc])
                    if oc == 0:
                        wait(e, "vv", m["corr7"])
                    last = None
                    for cc in range(NCH):
                        last = e.tensor_add(wt[:, cc, oc * 128:(oc + 1) * 128],
                                            wt[:, cc, oc * 128:(oc + 1) * 128],
                                            corr_sb[:, cc, oc * 128:(oc + 1) * 128])
                    ev_w.append(inc(last, "vv"))
            drainT()
            barrier()

        _pqn = [0]

        def proj_spill(wt, dst_hbm):
            _pqn[0] += 1
            pqname = f"s_pq{_pqn[0]}"
            s_pq = sb(pqname, [128, 2, T], F16, scoped=True, side="right")
            tn_p, ev_p, st_p = [], [], []
            for oc in range(NCH):
                @blk.tensor
                def _(e, oc=oc):
                    if oc >= 2:
                        wait(e, "sc", ev_p[oc - 2])
                    banks = [psum[4 * (oc % 2) + t] for t in range(4)]
                    mm = None
                    for cc in range(NCH):
                        for t in range(4):
                            mm = e.matmul(banks[t][:, :], wt[:, cc, oc * 128:(oc + 1) * 128],
                                          s_h[:, cc, t * 512:(t + 1) * 512],
                                          start=(cc == 0), stop=(cc == NCH - 1))
                    tn_p.append(inc(mm, "tn"))

                @blk.scalar
                def _(e, oc=oc):
                    wait(e, "tn", tn_p[oc])
                    if oc >= 2:
                        wait(e, f"G{oc % 2}", st_p[oc - 2])
                    banks = [psum[4 * (oc % 2) + t] for t in range(4)]
                    last = None
                    for t in range(4):
                        last = e.copy(s_pq[:, oc % 2, t * 512:(t + 1) * 512], banks[t][:, :])
                    ev_p.append(inc(last, "sc"))

                @blk.gpsimd
                def _(e, oc=oc):
                    wait(e, "sc", ev_p[oc])
                    st_p.append(inc(e.dma_start(out=dst_hbm[oc, :, :],
                                                in_=s_pq[:, oc % 2, :]), f"G{oc % 2}", 16))

            @blk.gpsimd
            def _(e):
                wait(e, "G0", val("G0"))
                wait(e, "G1", val("G1"))
            barrier()
            freesb(pqname)

        wprep("Wq", s_wtA)
        maybe_stop(2)
        proj_spill(s_wtA, q_hbm)
        maybe_stop(3)
        wprep("Wk", s_wtB)
        proj_spill(s_wtB, k_hbm)
        wprep("Wv", s_wtA)

        s_v = sb("s_v", [128, NT, 1040], F16, scoped=True, side="right")

        @blk.vector
        def _(e):
            last = None
            for tc in range(NT):
                last = e.memset(s_v[:, tc, :].rearrange("p (h c) -> p h c", c=65)[:, :, 64:65],
                                1.0)
            inc(last, "vv")
        v_ones = val("vv")

        tn_v, ev_v = [], []
        for tc in range(NT):
            @blk.tensor
            def _(e, tc=tc):
                if tc == 0:
                    wait(e, "vv", v_ones)
                if tc >= 2:
                    wait(e, "vv", ev_v[tc - 2])
                banks = [psum[4 * (tc % 2)], psum[4 * (tc % 2) + 1]]
                mm = None
                for cc in range(NCH):
                    for t in range(2):
                        mm = e.matmul(banks[t][:, :], s_h[:, cc, tc * 128:(tc + 1) * 128],
                                      s_wtA[:, cc, t * 512:(t + 1) * 512],
                                      start=(cc == 0), stop=(cc == NCH - 1))
                tn_v.append(inc(mm, "tn"))

            @blk.vector
            def _(e, tc=tc):
                wait(e, "tn", tn_v[tc])
                banks = [psum[4 * (tc % 2)], psum[4 * (tc % 2) + 1]]
                last = None
                for hh in range(16):
                    bank = banks[hh // 8]
                    last = e.tensor_copy(s_v[:, tc, hh * 65:hh * 65 + 64],
                                         bank[:, (hh % 8) * 64:(hh % 8) * 64 + 64])
                ev_v.append(inc(last, "vv"))
        barrier()
        maybe_stop(4)
        freesb("s_htm", "s_h", "s_wtA")
        wprep("Wo", s_wtB)
        maybe_stop(5)

        s_o = sb("s_o", [128, NCH, T], F16, scoped=True)
        s_den = sb("s_den", [16, T], F32, scoped=True)
        s_qh = sb("s_qh", [128, 2, T], F16, scoped=True, side="right")
        s_kh = sb("s_kh", [128, 2, T], F16, scoped=True, side="right")
        s_pt = sb("s_pt", [128, 4, 512], F16, scoped=True, side="right")

        tn0, sc0, vv0 = val("tn"), val("sc"), val("vv")
        tn_qk, tn_pv = {}, {}
        tn_c = tn0
        att_done = []
        for p in range(8):
            for ic in range(4):
                for jc in range(16):
                    tn_c += 1
                    tn_qk[(p, ic, jc)] = tn_c
                    if jc >= 1:
                        tn_c += 1
                        tn_pv[(p, ic, jc - 1)] = tn_c
                tn_c += 1
                tn_pv[(p, ic, 15)] = tn_c
            att_done.append(tn_c)
        exp_sc = {}
        sc_c = sc0
        for p in range(8):
            for ic in range(4):
                for jc in range(16):
                    sc_c += 1
                    exp_sc[(p, ic, jc)] = sc_c
        pv_evac = {}
        vv_c = vv0
        for p in range(8):
            for ic in range(4):
                vv_c += 1
                pv_evac[(p, ic)] = vv_c
        ldA0, ldA1 = val("A0"), val("A1")
        ld_qk = [(ldA0 if p % 2 == 0 else ldA1) + 32 * (p // 2 + 1) for p in range(8)]
        flat_ids = {(p, ic, jc): (p * 4 + ic) * 16 + jc
                    for p in range(8) for ic in range(4) for jc in range(16)}
        inv_flat = {v: k for k, v in flat_ids.items()}

        @blk.sync
        def _(e):
            for p in range(-2, 8):
                if p + 2 < 8:
                    pp = p + 2
                    if pp >= 2:
                        wait(e, "tn", att_done[pp - 2])
                    inc(e.dma_start(out=s_qh[:, pp % 2, :], in_=q_hbm[pp, :, :]),
                        f"A{pp % 2}", 16)
                    inc(e.dma_start(out=s_kh[:, pp % 2, :], in_=k_hbm[pp, :, :]),
                        f"A{pp % 2}", 16)
                if p >= 0:
                    for ic in range(4):
                        fi = p * 4 + ic
                        wait(e, "vv", vv0 + fi + 1)
                        for b in range(2):
                            hh = 2 * p + b
                            d = fi * 2 + b
                            inc(e.dma_start(out=s_den[hh:hh + 1, ic * 512:(ic + 1) * 512],
                                            in_=s_dst[0:1, d % 4, :]), f"D{d % 4}", 16)
        assert val("A0") == ld_qk[6] and val("A1") == ld_qk[7]

        @blk.tensor
        def _(e):
            for p in range(8):
                wait(e, f"A{p % 2}", ld_qk[p])
                hA, hB = 2 * p, 2 * p + 1
                for ic in range(4):
                    fi = p * 4 + ic
                    pvA, pvB = psum[4 + fi % 2], psum[6 + fi % 2]
                    if fi >= 2:
                        wait(e, "vv", pv_evac[((fi - 2) // 4, (fi - 2) % 4)])
                    for jc in range(16):
                        g = flat_ids[(p, ic, jc)]
                        if g >= 2:
                            wait(e, "sc", exp_sc[inv_flat[g - 2]])
                        sA, sB = psum[jc % 2], psum[2 + jc % 2]
                        e.matmul(sA[:, :], s_kh[0:64, p % 2, jc * 128:(jc + 1) * 128],
                                 s_qh[0:64, p % 2, ic * 512:(ic + 1) * 512],
                                 start=True, stop=True, tile_position=(0, 0))
                        inc(e.matmul(sB[:, :], s_kh[64:128, p % 2, jc * 128:(jc + 1) * 128],
                                     s_qh[64:128, p % 2, ic * 512:(ic + 1) * 512],
                                     start=True, stop=True, tile_position=(64, 0)), "tn")
                        if jc >= 1:
                            wait(e, "sc", exp_sc[(p, ic, jc - 1)])
                            e.matmul(pvA[0:65, :], s_v[:, jc - 1, hA * 65:hA * 65 + 65],
                                     s_pt[:, 2 * ((jc - 1) % 2), :],
                                     start=(jc == 1), stop=False)
                            inc(e.matmul(pvB[0:65, :], s_v[:, jc - 1, hB * 65:hB * 65 + 65],
                                         s_pt[:, 2 * ((jc - 1) % 2) + 1, :],
                                         start=(jc == 1), stop=False), "tn")
                    wait(e, "sc", exp_sc[(p, ic, 15)])
                    e.matmul(pvA[0:65, :], s_v[:, 15, hA * 65:hA * 65 + 65],
                             s_pt[:, 2 * (15 % 2), :], start=False, stop=True)
                    inc(e.matmul(pvB[0:65, :], s_v[:, 15, hB * 65:hB * 65 + 65],
                                 s_pt[:, 2 * (15 % 2) + 1, :], start=False, stop=True), "tn")
        assert val("tn") == tn_c, (val("tn"), tn_c)

        @blk.scalar
        def _(e):
            for p in range(8):
                for ic in range(4):
                    for jc in range(16):
                        g = flat_ids[(p, ic, jc)]
                        wait(e, "tn", tn_qk[(p, ic, jc)])
                        if g >= 2:
                            wait(e, "tn", tn_pv[inv_flat[g - 2]])
                        sA, sB = psum[jc % 2], psum[2 + jc % 2]
                        e.activation(out=s_pt[:, 2 * (jc % 2), :], in_=sA[:, :],
                                     func=AF.Exp, bias=0.0, scale=0.125)
                        inc(e.activation(out=s_pt[:, 2 * (jc % 2) + 1, :], in_=sB[:, :],
                                         func=AF.Exp, bias=0.0, scale=0.125), "sc")
        assert val("sc") == sc_c

        @blk.vector
        def _(e):
            for p in range(8):
                hA, hB = 2 * p, 2 * p + 1
                for ic in range(4):
                    fi = p * 4 + ic
                    wait(e, "tn", tn_pv[(p, ic, 15)])
                    pvA, pvB = psum[4 + fi % 2], psum[6 + fi % 2]
                    last = None
                    for b, (hh, pv) in enumerate(((hA, pvA), (hB, pvB))):
                        oc, row = hh // 2, 64 * (hh % 2)
                        e.tensor_copy(s_o[row:row + 64, oc, ic * 512:(ic + 1) * 512],
                                      pv[0:64, :])
                        d = fi * 2 + b
                        if d >= 4:
                            wait(e, f"D{d % 4}", 16 * (d // 4))
                        last = e.tensor_copy(s_dst[0:1, d % 4, :], pv[64:65, :])
                    inc(last, "vv")
        assert val("vv") == vv_c

        barrier()
        maybe_stop(6)
        freesb("s_pt", "s_kh", "s_qh", "s_v")

        s_recip = sb("s_recip", [16, T], F16, scoped=True)

        @blk.vector
        def _(e):
            for dsem in ("D0", "D1", "D2", "D3"):
                wait(e, dsem, val(dsem))
            with nc.allow_low_precision(reason="softmax reciprocal feeds an f16 matmul operand"):
                inc(e.reciprocal(out=s_recip[:], in_=s_den[:]), "vv")
        v_recip = val("vv")

        tn_db, ev_db = [], []
        for oc in range(NCH):
            @blk.tensor
            def _(e, oc=oc):
                if oc == 0:
                    wait(e, "vv", v_recip)
                if oc >= 2:
                    wait(e, "vv", ev_db[oc - 2])
                banks = [psum[4 * (oc % 2) + t] for t in range(4)]
                mm = None
                for t in range(4):
                    mm = e.matmul(banks[t][:, :], hsel[:, oc * 128:(oc + 1) * 128],
                                  s_recip[:, t * 512:(t + 1) * 512], start=True, stop=True)
                tn_db.append(inc(mm, "tn"))

            @blk.vector
            def _(e, oc=oc):
                wait(e, "tn", tn_db[oc])
                banks = [psum[4 * (oc % 2) + t] for t in range(4)]
                last = None
                for t in range(4):
                    last = e.tensor_mul(s_o[:, oc, t * 512:(t + 1) * 512],
                                        s_o[:, oc, t * 512:(t + 1) * 512], banks[t][:, :])
                ev_db.append(inc(last, "vv"))
        barrier()
        maybe_stop(7)
        freesb("s_recip", "s_den")

        s_xm = sb("s_xm", [128, 2, 1024], F32, scoped=True, side="right")
        s_aT = sb("s_aT", [128, 2, 1024], F16, scoped=True, side="right")
        s_attn = sb("s_attn", [128, NCH, T], F16, scoped=True, side="right")
        tn_wo, ev_wo = [], []
        for oc in range(NCH):
            @blk.tensor
            def _(e, oc=oc):
                if oc >= 2:
                    wait(e, "sc", ev_wo[oc - 2])
                banks = [psum[4 * (oc % 2) + t] for t in range(4)]
                mm = None
                for cc in range(NCH):
                    for t in range(4):
                        mm = e.matmul(banks[t][:, :], s_wtB[:, cc, oc * 128:(oc + 1) * 128],
                                      s_o[:, cc, t * 512:(t + 1) * 512],
                                      start=(cc == 0), stop=(cc == NCH - 1))
                tn_wo.append(inc(mm, "tn"))

            @blk.scalar
            def _(e, oc=oc):
                wait(e, "tn", tn_wo[oc])
                banks = [psum[4 * (oc % 2) + t] for t in range(4)]
                last = None
                for t in range(4):
                    last = e.copy(s_attn[:, oc, t * 512:(t + 1) * 512], banks[t][:, :])
                ev_wo.append(inc(last, "sc"))
        if taps:
            @blk.gpsimd
            def _(e):
                wait(e, "sc", ev_wo[-1])
                for oc in range(NCH):
                    inc(e.dma_start(out=attnfm_hbm[oc, :, :], in_=s_attn[:, oc, :]), "GC", 16)
                wait(e, "GC", val("GC"))
        barrier()
        maybe_stop(8)
        freesb("s_o", "s_wq16", "s_wraw", "s_wtB")

        x2_ld, x2_tt, x2_v1, x2_sc1, x2_v2, x2_sc2, x2_v3, x2_gd = [], [], [], [], [], [], [], []
        for tc in range(NT):
            @blk.scalar
            def _(e, tc=tc):
                if tc >= 2:
                    wait(e, "vv", x2_v1[tc - 2])
                x2_ld.append(inc(e.dma_start(out=s_xt[:, tc % 2, :],
                                             in_=x_h[tc * 128:(tc + 1) * 128, :]),
                                 f"A{tc % 2}", 16))

            @blk.sync
            def _(e, tc=tc):
                if tc >= 2:
                    wait(e, "vv", x2_v1[tc - 2])
                for cc in range(0, NCH, 2) if SPLIT else range(NCH):
                    inc(e.dma_start_transpose(s_aT[:, tc % 2, cc * 128:(cc + 1) * 128],
                                              s_attn[:, cc, tc * 128:(tc + 1) * 128]),
                        f"T{tc % 2}", 16)

            if SPLIT:
                @blk.scalar
                def _(e, tc=tc):
                    if tc >= 2:
                        wait(e, "vv", x2_v1[tc - 2])
                    for cc in range(1, NCH, 2):
                        inc(e.dma_start_transpose(s_aT[:, tc % 2, cc * 128:(cc + 1) * 128],
                                                  s_attn[:, cc, tc * 128:(tc + 1) * 128]),
                            f"T{tc % 2}", 16)
            x2_tt.append(val(f"T{tc % 2}"))

            @blk.vector
            def _(e, tc=tc):
                wait(e, f"T{tc % 2}", x2_tt[tc])
                wait(e, f"A{tc % 2}", x2_ld[tc])
                if tc >= 2:
                    wait(e, f"G{tc % 2}", x2_gd[tc - 2])
                e.tensor_add(s_xm[:, tc % 2, :], s_xt[:, tc % 2, :], s_aT[:, tc % 2, :])
                sw(e, e.reduce_sum(s_sum[:, tc:tc + 1], s_xm[:, tc % 2, :], AXX), "vv")
                x2_v1.append(inc(e.tensor_scalar(out=s_m[:, tc:tc + 1],
                                                 in0=s_sum[:, tc:tc + 1], scalar1=1.0 / D,
                                                 scalar2=None, op0=ALU.mult), "vv"))

            @blk.scalar
            def _(e, tc=tc):
                wait(e, "vv", x2_v1[tc])
                x2_sc1.append(inc(e.activation(out=junk16[:, :], in_=s_xm[:, tc % 2, :],
                                               func=AF.Square, accum_out=s_ssq[:, tc:tc + 1]),
                                  "sc"))

            @blk.vector
            def _(e, tc=tc):
                wait(e, "sc", x2_sc1[tc])
                sw(e, e.tensor_mul(s_t0[:, tc:tc + 1], s_m[:, tc:tc + 1],
                                   s_m[:, tc:tc + 1]), "vv")
                x2_v2.append(inc(e.scalar_tensor_tensor(out=s_t0[:, tc:tc + 1],
                                                        in0=s_ssq[:, tc:tc + 1],
                                                        scalar=1.0 / D,
                                                        in1=s_t0[:, tc:tc + 1],
                                                        op0=ALU.mult, op1=ALU.subtract), "vv"))

            @blk.scalar
            def _(e, tc=tc):
                wait(e, "vv", x2_v2[tc])
                x2_sc2.append(inc(e.activation(out=s_t0[:, tc:tc + 1], in_=s_t0[:, tc:tc + 1],
                                               func=AF.Sqrt, bias=1e-5, scale=1.0), "sc"))

            @blk.vector
            def _(e, tc=tc):
                wait(e, "sc", x2_sc2[tc])
                sw(e, e.reciprocal(out=s_a[:, tc:tc + 1], in_=s_t0[:, tc:tc + 1]), "vv")
                sw(e, e.scalar_tensor_tensor(out=s_b[:, tc:tc + 1], in0=s_m[:, tc:tc + 1],
                                             scalar=-1.0, in1=s_a[:, tc:tc + 1],
                                             op0=ALU.mult, op1=ALU.mult), "vv")
                e.tensor_scalar(out=s_h2j[:, :], in0=s_xm[:, tc % 2, :],
                                scalar1=s_a[:, tc:tc + 1], scalar2=s_b[:, tc:tc + 1],
                                op0=ALU.mult, op1=ALU.add)
                x2_v3.append(inc(e.reduce_max(s_mx[:, tc:tc + 1], s_h2j[:, :], AXX,
                                              apply_absolute_value=True), "vv"))

            @blk.gpsimd
            def _(e, tc=tc):
                wait(e, "vv", x2_v3[tc])
                x2_gd.append(inc(e.dma_start(out=xmid_hbm[tc, :, :],
                                             in_=s_xm[:, tc % 2, :]), f"G{tc % 2}", 16))

        @blk.vector
        def _(e):
            wait(e, "vv", x2_v3[-1])
            sw(e, e.reduce_max(lmax[:, 0:1], s_mx[:, 0:NT], AXX,
                               apply_absolute_value=True), "vv")

        @blk.gpsimd
        def _(e):
            wait(e, "G0", val("G0"))
            wait(e, "G1", val("G1"))
        drainT()
        barrier()
        maybe_stop(9)
        freesb("s_attn", "s_aT")

        def cc_round(r, nscal, derive):
            vnow = val("vv")

            @blk.sync
            def _(e):
                wait(e, "vv", vnow)
                i = inc(e.dma_start(out=lscr[:, :], in_=lmax[:, 0:2]), "GH", 16)
                wait(e, "GH", i)
                i = inc(e.dma_start(out=cc128[0:2, 0:128],
                                    in_=lscr[:, :].rearrange("p c -> c p"),
                                    allow_slow_non_contiguous=True), "GH", 16)
                wait(e, "GH", i)

            @blk.vector
            def _(e):
                wait(e, "GH", val("GH"))
                inc(e.reduce_max(cc128[:, 0:1], cc128[:, :], AXX), "vv")
            t2 = val("vv")

            @blk.sync
            def _(e):
                wait(e, "vv", t2)
                i = inc(e.dma_start(out=lscr2[:, :], in_=cc128[0:2, 0:1]), "GH", 16)
                wait(e, "GH", i)
                i = inc(e.dma_start(out=ccsb[0:1, 0:2],
                                    in_=lscr2[:, :].rearrange("p c -> c p"),
                                    allow_slow_non_contiguous=True), "GH", 16)
                wait(e, "GH", i)

            @blk.vector
            def _(e):
                wait(e, "GH", val("GH"))
                for j in range(nscal, 8):
                    e.tensor_copy(ccsb[0:1, j:j + 1], ccsb[0:1, j % nscal:j % nscal + 1])
                inc(e.memset(lmax[:, :], 0.0), "vv")
            t4 = val("vv")

            @blk.gpsimd
            def _(e):
                wait(e, "vv", t4)
                i = inc(e.dma_start(out=cc_in[r][:, :], in_=ccsb[0:1, 0:8]), "GC", 16)
                wait(e, "GC", i)
                inc(e.collective_compute(
                    "AllReduce", ALU.max, replica_groups=[list(range(NCORE))],
                    ins=[cc_in[r][:, :]], outs=[cc_out[r][:, :]]), "ccs")
                wait(e, "ccs")
                inc(e.dma_start(out=ccsb[0:1, 8:16], in_=cc_out[r][:, :]), "GC", 16)
            t5 = val("GC")

            @blk.vector
            def _(e):
                wait(e, "GC", t5)
                inc(derive(e), "vv")
            return val("vv")

        def cc_bcast(t6, ncols, dst0):
            @blk.tensor
            def _(e):
                wait(e, "vv", t6)
                inc(e.matmul(psum[0][:, 0:ncols], ones1[:], ccsb[0:1, 0:ncols],
                             start=True, stop=True), "tn")
            t7 = val("tn")

            @blk.vector
            def _(e):
                wait(e, "tn", t7)
                inc(e.tensor_copy(scal[:, dst0:dst0 + ncols], psum[0][:, 0:ncols]), "vv")
            return val("vv")

        def derive_r1(e):
            sw(e, e.tensor_scalar(out=ccsb[0:1, 1:2], in0=ccsb[0:1, 8:9],
                                  scalar1=1.0 / 32767.0, scalar2=1e-12,
                                  op0=ALU.mult, op1=ALU.max), "vv")
            return e.reciprocal(out=ccsb[0:1, 0:1], in_=ccsb[0:1, 1:2])
        v_r1 = cc_bcast(cc_round(0, 1, derive_r1), 2, 0)

        s_fraw = sb("s_fraw", [128, 2, 1024], F32, scoped=True)
        s_fq16 = sb("s_fq16", [128, 2, 1024], F16, scoped=True)
        s_fs = sb("s_fs", [128, 32], F32, scoped=True)
        s_frs = sb("s_frs", [128, 32], F32, scoped=True)
        s_gate = sb("s_gate", [128, NIC, T], F16, scoped=True)
        s_xq = sb("s_xq", [128, NCH, T], F16, scoped=True)
        s_xqtm = sb("s_xqtm", [128, 2, 1024], F16, scoped=True, side="right")
        s_z2 = sb("s_z2", [128, 1024], F32, scoped=True, side="right")

        @blk.vector
        def _(e):
            wait(e, "vv", v_r1)
            e.tensor_scalar(out=s_a[:, :], in0=s_a[:, :], scalar1=scal[:, 0:1],
                            scalar2=None, op0=ALU.mult)
            inc(e.tensor_scalar(out=s_b[:, :], in0=s_b[:, :], scalar1=scal[:, 0:1],
                                scalar2=MAGIC, op0=ALU.mult, op1=ALU.add), "vv")
        v_ab = val("vv")

        xq_ld, xq_v, xq_tt = [], [], []
        for tc in range(NT):
            @blk.scalar
            def _(e, tc=tc):
                if tc >= 2:
                    wait(e, "vv", xq_v[tc - 2])
                wait(e, f"G{tc % 2}", x2_gd[tc])
                xq_ld.append(inc(e.dma_start(out=s_xm[:, tc % 2, :],
                                             in_=xmid_hbm[tc, :, :]), f"A{tc % 2}", 16))

            @blk.vector
            def _(e, tc=tc):
                if tc == 0:
                    wait(e, "vv", v_ab)
                wait(e, f"A{tc % 2}", xq_ld[tc])
                if tc >= 2:
                    wait(e, f"T{tc % 2}", xq_tt[tc - 2])
                e.tensor_scalar(out=s_z2[:, :], in0=s_xm[:, tc % 2, :],
                                scalar1=s_a[:, tc:tc + 1], scalar2=s_b[:, tc:tc + 1],
                                op0=ALU.mult, op1=ALU.add)
                xq_v.append(inc(e.tensor_scalar(out=s_xqtm[:, tc % 2, :],
                                                in0=s_z2[:, :], scalar1=MAGIC,
                                                scalar2=scal[:, 1:2],
                                                op0=ALU.subtract, op1=ALU.mult), "vv"))

            @blk.sync
            def _(e, tc=tc):
                wait(e, "vv", xq_v[tc])
                for cc in range(0, NCH, 2) if SPLIT else range(NCH):
                    inc(e.dma_start_transpose(s_xq[:, cc, tc * 128:(tc + 1) * 128],
                                              s_xqtm[:, tc % 2, cc * 128:(cc + 1) * 128]),
                        f"T{tc % 2}", 16)

            if SPLIT:
                @blk.scalar
                def _(e, tc=tc):
                    wait(e, "vv", xq_v[tc])
                    for cc in range(1, NCH, 2):
                        inc(e.dma_start_transpose(s_xq[:, cc, tc * 128:(tc + 1) * 128],
                                                  s_xqtm[:, tc % 2, cc * 128:(cc + 1) * 128]),
                            f"T{tc % 2}", 16)
            xq_tt.append(val(f"T{tc % 2}"))
        drainT()
        barrier()
        maybe_stop(10)
        freesb("s_z2", "s_xqtm", "s_xm")

        s_up2 = sb("s_up2", [128, 2, T], F16, scoped=True, side="right")
        s_fwt = sb("s_fwt", [128, 2, NCH, 128], F16, scoped=True, side="right")

        def ffn_side(w_src, dst, spill, lmax_col):
            f_ld, f_vq, f_tt, mm_tn, ev_sc, mx_vv, st_gd = [], [], [], [], [], [], []
            for ic in range(NIC):
                @blk.scalar
                def _(e, ic=ic):
                    if ic >= 2:
                        wait(e, "vv", f_vq[ic - 2])
                    f_ld.append(inc(e.dma_start(out=s_fraw[:, ic % 2, :],
                                                in_=w_src[ic * 128:(ic + 1) * 128, :]),
                                    f"B{ic % 2}", 16))

                @blk.vector
                def _(e, ic=ic):
                    wait(e, f"B{ic % 2}", f_ld[ic])
                    if ic >= 2:
                        wait(e, f"T{ic % 2}", f_tt[ic - 2])
                    sw(e, e.reduce_max(s_fs[:, :], s_fraw[:, ic % 2, :]
                                       .rearrange("p (b k) -> p b k", k=32), AXX,
                                       apply_absolute_value=True), "vv")
                    sw(e, e.tensor_scalar(out=s_fs[:, :], in0=s_fs[:, :], scalar1=1.0 / 7.0,
                                          scalar2=1e-12, op0=ALU.mult, op1=ALU.max), "vv")
                    sw(e, e.reciprocal(out=s_frs[:, :], in_=s_fs[:, :]), "vv")
                    e.tensor_tensor(out=s_fraw[:, ic % 2, :].rearrange("p (b k) -> p b k", k=32),
                                    in0=s_fraw[:, ic % 2, :].rearrange("p (b k) -> p b k", k=32),
                                    in1=s_frs[:, :].rearrange("p (b o) -> p b o", o=1)
                                    .broadcast_to([128, 32, 32]), op=ALU.mult)
                    e.tensor_scalar(out=s_fraw[:, ic % 2, :], in0=s_fraw[:, ic % 2, :],
                                    scalar1=MAGIC, scalar2=MAGIC,
                                    op0=ALU.add, op1=ALU.subtract)
                    f_vq.append(inc(e.tensor_tensor(
                        out=s_fq16[:, ic % 2, :].rearrange("p (b k) -> p b k", k=32),
                        in0=s_fraw[:, ic % 2, :].rearrange("p (b k) -> p b k", k=32),
                        in1=s_fs[:, :].rearrange("p (b o) -> p b o", o=1)
                        .broadcast_to([128, 32, 32]), op=ALU.mult), "vv"))

                @blk.sync
                def _(e, ic=ic):
                    wait(e, "vv", f_vq[ic])
                    if ic >= 2:
                        wait(e, "tn", mm_tn[ic - 2])
                    for cc in range(0, NCH, 2) if SPLIT else range(NCH):
                        inc(e.dma_start_transpose(s_fwt[:, ic % 2, cc, :],
                                                  s_fq16[:, ic % 2, cc * 128:(cc + 1) * 128]),
                            f"T{ic % 2}", 16)

                if SPLIT:
                    @blk.scalar
                    def _(e, ic=ic):
                        wait(e, "vv", f_vq[ic])
                        if ic >= 2:
                            wait(e, "tn", mm_tn[ic - 2])
                        for cc in range(1, NCH, 2):
                            inc(e.dma_start_transpose(s_fwt[:, ic % 2, cc, :],
                                                      s_fq16[:, ic % 2,
                                                             cc * 128:(cc + 1) * 128]),
                                f"T{ic % 2}", 16)
                f_tt.append(val(f"T{ic % 2}"))

                @blk.tensor
                def _(e, ic=ic):
                    wait(e, f"T{ic % 2}", f_tt[ic])
                    if ic >= 2:
                        wait(e, "sc", ev_sc[ic - 2])
                    banks = [psum[4 * (ic % 2) + t] for t in range(4)]
                    mm = None
                    for cc in range(NCH):
                        for t in range(4):
                            mm = e.matmul(banks[t][:, :], s_fwt[:, ic % 2, cc, :],
                                          s_xq[:, cc, t * 512:(t + 1) * 512],
                                          start=(cc == 0), stop=(cc == NCH - 1))
                    mm_tn.append(inc(mm, "tn"))

                @blk.scalar
                def _(e, ic=ic):
                    wait(e, "tn", mm_tn[ic])
                    if dst is None and ic >= 2:
                        wait(e, f"G{ic % 2}", st_gd[ic - 2])
                    banks = [psum[4 * (ic % 2) + t] for t in range(4)]
                    last = None
                    for t in range(4):
                        tgt = s_up2[:, ic % 2, t * 512:(t + 1) * 512] if dst is None \
                            else dst[:, ic, t * 512:(t + 1) * 512]
                        last = e.copy(tgt, banks[t][:, :])
                    ev_sc.append(inc(last, "sc"))

                @blk.vector
                def _(e, ic=ic):
                    wait(e, "sc", ev_sc[ic])
                    tgt = s_up2[:, ic % 2, :] if dst is None else dst[:, ic, :]
                    mx_vv.append(inc(e.reduce_max(s_mx[:, ic:ic + 1], tgt, AXX,
                                                  apply_absolute_value=True), "vv"))

                if dst is None:
                    @blk.gpsimd
                    def _(e, ic=ic):
                        wait(e, "vv", mx_vv[ic])
                        st_gd.append(inc(e.dma_start(out=spill[ic, :, :],
                                                     in_=s_up2[:, ic % 2, :]),
                                         f"G{ic % 2}", 16))

            @blk.vector
            def _(e):
                wait(e, "vv", mx_vv[-1])
                sw(e, e.reduce_max(lmax[:, lmax_col:lmax_col + 1], s_mx[:, 0:NIC], AXX,
                                   apply_absolute_value=True), "vv")

            if dst is None:
                @blk.gpsimd
                def _(e):
                    wait(e, "G0", val("G0"))
                    wait(e, "G1", val("G1"))
            drainT()
            barrier()
            return f_vq

        ffn_side(wup_h, None, up_hbm, 0)
        gate_vq = ffn_side(wg_h, s_gate, None, 1)
        if taps:
            @blk.gpsimd
            def _(e):
                wait(e, "vv", val("vv"))
                for ic in range(NIC):
                    inc(e.dma_start(out=gate_hbm[ic, :, :], in_=s_gate[:, ic, :]), "GC", 16)
                wait(e, "GC", val("GC"))
            barrier()
        maybe_stop(11)
        freesb("s_fwt")
        freesb("s_xq")

        def derive_r2(e):
            sw(e, e.tensor_scalar(out=ccsb[0:1, 10:12], in0=ccsb[0:1, 8:10],
                                  scalar1=1.0 / 32767.0, scalar2=1e-12,
                                  op0=ALU.mult, op1=ALU.max), "vv")
            sw(e, e.reciprocal(out=ccsb[0:1, 12:14], in_=ccsb[0:1, 10:12]), "vv")
            e.tensor_copy(ccsb[0:1, 0:1], ccsb[0:1, 12:13])
            e.tensor_copy(ccsb[0:1, 1:2], ccsb[0:1, 10:11])
            e.tensor_copy(ccsb[0:1, 2:3], ccsb[0:1, 13:14])
            e.tensor_copy(ccsb[0:1, 3:4], ccsb[0:1, 11:12])
            return e.tensor_scalar(out=ccsb[0:1, 4:5], in0=ccsb[0:1, 11:12], scalar1=-MAGIC,
                                   scalar2=None, op0=ALU.mult)
        v_r2 = cc_bcast(cc_round(1, 2, derive_r2), 5, 2)

        s_tmp16 = sb("s_tmp16", [128, 2, T], F16, scoped=True, side="right")
        s_zA = sb("s_zA", [128, 2, T], F32, scoped=True, side="right")
        s_zB = sb("s_zB", [128, 2, T], F32, scoped=True, side="right")
        ge_sc, ge_vv = [], []
        for ic in range(NIC):
            @blk.scalar
            def _(e, ic=ic):
                if ic == 0:
                    wait(e, "vv", v_r2)
                if ic >= 2:
                    wait(e, "vv", ge_vv[ic - 2])
                e.activation(out=s_zA[:, ic % 2, :], in_=s_gate[:, ic, :], func=AF.Copy,
                             bias=MAGIC, scale=scal[:, 4:5])
                ge_sc.append(inc(e.activation(out=s_zB[:, ic % 2, :], in_=s_zA[:, ic % 2, :],
                                              func=AF.Sigmoid, bias=scal[:, 6:7],
                                              scale=scal[:, 5:6]), "sc"))

            @blk.vector
            def _(e, ic=ic):
                wait(e, "sc", ge_sc[ic])
                e.tensor_scalar(out=s_tmp16[:, ic % 2, :], in0=s_zA[:, ic % 2, :],
                                scalar1=MAGIC, scalar2=scal[:, 5:6],
                                op0=ALU.subtract, op1=ALU.mult)
                e.tensor_scalar(out=s_zA[:, ic % 2, :], in0=s_zB[:, ic % 2, :],
                                scalar1=65536.0, scalar2=MAGIC, op0=ALU.mult, op1=ALU.add)
                e.tensor_scalar(out=s_zB[:, ic % 2, :], in0=s_zA[:, ic % 2, :],
                                scalar1=MAGIC + 65535.0, scalar2=float(2.0 ** -16),
                                op0=ALU.min, op1=ALU.mult)
                e.scalar_tensor_tensor(out=s_gate[:, ic, :], in0=s_zB[:, ic % 2, :],
                                       scalar=-192.0, in1=s_tmp16[:, ic % 2, :],
                                       op0=ALU.add, op1=ALU.mult)
                ge_vv.append(inc(e.reduce_max(s_mx[:, ic:ic + 1], s_gate[:, ic, :], AXX,
                                              apply_absolute_value=True), "vv"))

        @blk.vector
        def _(e):
            wait(e, "vv", ge_vv[-1])
            sw(e, e.reduce_max(lmax[:, 0:1], s_mx[:, 0:NIC], AXX,
                               apply_absolute_value=True), "vv")
        barrier()
        maybe_stop(12)

        def derive_r3(e):
            sw(e, e.tensor_scalar(out=ccsb[0:1, 1:2], in0=ccsb[0:1, 8:9],
                                  scalar1=1.0 / 32767.0, scalar2=1e-12,
                                  op0=ALU.mult, op1=ALU.max), "vv")
            return e.reciprocal(out=ccsb[0:1, 0:1], in_=ccsb[0:1, 1:2])
        v_r3 = cc_bcast(cc_round(2, 1, derive_r3), 2, 7)

        @blk.vector
        def _(e):
            wait(e, "vv", v_r3)
            inc(e.tensor_mul(scal[:, 3:4], scal[:, 3:4], scal[:, 8:9]), "vv")
        v_s2u3 = val("vv")

        op_ld, op_sc, op_vv = [], [], []
        for ic in range(NIC):
            @blk.sync
            def _(e, ic=ic):
                if ic >= 2:
                    wait(e, "vv", op_vv[ic - 2])
                op_ld.append(inc(e.dma_start(out=s_up2[:, ic % 2, :],
                                             in_=up_hbm[ic, :, :]), f"A{ic % 2}", 16))

            @blk.scalar
            def _(e, ic=ic):
                if ic == 0:
                    wait(e, "vv", v_s2u3)
                wait(e, f"A{ic % 2}", op_ld[ic])
                if ic >= 2:
                    wait(e, "vv", op_vv[ic - 2])
                e.activation(out=s_zA[:, ic % 2, :], in_=s_up2[:, ic % 2, :], func=AF.Copy,
                             bias=MAGIC, scale=scal[:, 2:3])
                op_sc.append(inc(e.activation(out=s_zB[:, ic % 2, :], in_=s_gate[:, ic, :],
                                              func=AF.Copy, bias=MAGIC, scale=scal[:, 7:8]),
                                 "sc"))

            @blk.vector
            def _(e, ic=ic):
                wait(e, "sc", op_sc[ic])
                e.tensor_scalar(out=s_tmp16[:, ic % 2, :], in0=s_zA[:, ic % 2, :],
                                scalar1=MAGIC, scalar2=scal[:, 3:4],
                                op0=ALU.subtract, op1=ALU.mult)
                e.scalar_tensor_tensor(out=s_gate[:, ic, :], in0=s_zB[:, ic % 2, :],
                                       scalar=-MAGIC, in1=s_tmp16[:, ic % 2, :],
                                       op0=ALU.add, op1=ALU.mult)
                op_vv.append(inc(e.reduce_max(s_mx[:, ic:ic + 1], s_gate[:, ic, :], AXX,
                                              apply_absolute_value=True), "vv"))

        @blk.vector
        def _(e):
            wait(e, "vv", op_vv[-1])
            sw(e, e.reduce_max(lmax[:, 0:1], s_mx[:, 0:NIC], AXX,
                               apply_absolute_value=True), "vv")
        barrier()
        maybe_stop(13)

        def derive_r4(e):
            sw(e, e.tensor_scalar(out=ccsb[0:1, 1:2], in0=ccsb[0:1, 8:9],
                                  scalar1=1.0 / 32767.0, scalar2=1e-12,
                                  op0=ALU.mult, op1=ALU.max), "vv")
            return e.reciprocal(out=ccsb[0:1, 0:1], in_=ccsb[0:1, 1:2])
        v_r4 = cc_bcast(cc_round(3, 1, derive_r4), 2, 8)

        oi_sc, oi_vv = [], []
        for ic in range(NIC):
            @blk.scalar
            def _(e, ic=ic):
                if ic == 0:
                    wait(e, "vv", v_r4)
                if ic >= 2:
                    wait(e, "vv", oi_vv[ic - 2])
                oi_sc.append(inc(e.activation(out=s_zA[:, ic % 2, :], in_=s_gate[:, ic, :],
                                              func=AF.Copy, bias=MAGIC, scale=scal[:, 8:9]),
                                 "sc"))

            @blk.vector
            def _(e, ic=ic):
                wait(e, "sc", oi_sc[ic])
                oi_vv.append(inc(e.tensor_scalar(out=s_gate[:, ic, :],
                                                 in0=s_zA[:, ic % 2, :], scalar1=MAGIC,
                                                 scalar2=None, op0=ALU.subtract), "vv"))
        barrier()
        freesb("s_zB", "s_zA", "s_tmp16", "s_up2", "corr_sb", "s_dst")

        s_dwt2 = sb("s_dwt2", [128, NIC, 1024], F16, scoped=True)
        s_dot16 = sb("s_dot16", [128, 2, 1024], F16, scoped=True)

        def prev_slot(s3, oc):
            if oc >= 2:
                return (s3, oc - 2)
            return (s3 - 1, oc + 6) if s3 >= 1 else None

        d_ld, d_vq, d_tt = {}, {}, {}
        for s3 in range(3):
            for oc in range(NCH):
                key = (s3, oc)

                @blk.scalar
                def _(e, oc=oc, s3=s3, key=key):
                    prev = prev_slot(s3, oc)
                    if prev in d_vq:
                        wait(e, "vv", d_vq[prev])
                    d_ld[key] = inc(e.dma_start(
                        out=s_fraw[:, oc % 2, :],
                        in_=wd_h[oc * 128:(oc + 1) * 128, s3 * 1024:(s3 + 1) * 1024]),
                        f"B{oc % 2}", 16)

                @blk.vector
                def _(e, key=key, s3=s3, oc=oc):
                    wait(e, f"B{oc % 2}", d_ld[key])
                    prev = prev_slot(s3, oc)
                    if prev in d_tt:
                        wait(e, f"T{prev[1] % 2}", d_tt[prev])
                    sw(e, e.reduce_max(s_fs[:, :], s_fraw[:, oc % 2, :]
                                       .rearrange("p (b k) -> p b k", k=32), AXX,
                                       apply_absolute_value=True), "vv")
                    sw(e, e.tensor_scalar(out=s_fs[:, :], in0=s_fs[:, :], scalar1=1.0 / 7.0,
                                          scalar2=1e-12, op0=ALU.mult, op1=ALU.max), "vv")
                    sw(e, e.reciprocal(out=s_frs[:, :], in_=s_fs[:, :]), "vv")
                    e.tensor_tensor(out=s_fraw[:, oc % 2, :].rearrange("p (b k) -> p b k", k=32),
                                    in0=s_fraw[:, oc % 2, :].rearrange("p (b k) -> p b k", k=32),
                                    in1=s_frs[:, :].rearrange("p (b o) -> p b o", o=1)
                                    .broadcast_to([128, 32, 32]), op=ALU.mult)
                    e.tensor_scalar(out=s_fraw[:, oc % 2, :], in0=s_fraw[:, oc % 2, :],
                                    scalar1=MAGIC, scalar2=MAGIC, op0=ALU.add, op1=ALU.subtract)
                    d_vq[key] = inc(e.tensor_tensor(
                        out=s_fq16[:, oc % 2, :].rearrange("p (b k) -> p b k", k=32),
                        in0=s_fraw[:, oc % 2, :].rearrange("p (b k) -> p b k", k=32),
                        in1=s_fs[:, :].rearrange("p (b o) -> p b o", o=1)
                        .broadcast_to([128, 32, 32]), op=ALU.mult), "vv")

                @blk.sync
                def _(e, key=key, s3=s3, oc=oc):
                    wait(e, "vv", d_vq[key])
                    for cc in range(8):
                        inc(e.dma_start_transpose(
                            s_dwt2[:, s3 * 8 + cc, oc * 128:(oc + 1) * 128],
                            s_fq16[:, oc % 2, cc * 128:(cc + 1) * 128]),
                            f"T{oc % 2}", 16)
                d_tt[key] = val(f"T{oc % 2}")

        segT = [(d_tt[(g, 6)], d_tt[(g, 7)]) for g in range(3)]
        dn_tn, dn_ev, o_ld, o_vv, o_gd = [], [], [], [], []
        for tc in range(NT):
            @blk.scalar
            def _(e, tc=tc):
                if tc >= 2:
                    wait(e, f"G{tc % 2}", o_gd[tc - 2])
                wait(e, f"G{tc % 2}", x2_gd[tc])
                o_ld.append(inc(e.dma_start(out=s_xt[:, tc % 2, :],
                                            in_=xmid_hbm[tc, :, :]), f"A{tc % 2}", 16))

            @blk.tensor
            def _(e, tc=tc):
                if tc == 0:
                    wait(e, "vv", oi_vv[-1])
                if tc >= 4:
                    wait(e, "sc", dn_ev[tc - 4])
                banks = [psum[2 * (tc % 4)], psum[2 * (tc % 4) + 1]]
                mm = None
                for g in range(3):
                    if tc == 0:
                        wait(e, "T0", segT[g][0])
                        wait(e, "T1", segT[g][1])
                    for ic in range(8 * g, 8 * g + 8):
                        for h in range(2):
                            mm = e.matmul(banks[h][:, :],
                                          s_gate[:, ic, tc * 128:(tc + 1) * 128],
                                          s_dwt2[:, ic, h * 512:(h + 1) * 512],
                                          start=(ic == 0), stop=(ic == NIC - 1))
                dn_tn.append(inc(mm, "tn"))

            @blk.scalar
            def _(e, tc=tc):
                wait(e, "tn", dn_tn[tc])
                if tc >= 2:
                    wait(e, "vv", o_vv[tc - 2])
                banks = [psum[2 * (tc % 4)], psum[2 * (tc % 4) + 1]]
                e.activation(out=s_dot16[:, tc % 2, 0:512], in_=banks[0][:, :],
                             func=AF.Copy, bias=0.0, scale=scal[:, 9:10])
                dn_ev.append(inc(e.activation(out=s_dot16[:, tc % 2, 512:1024],
                                              in_=banks[1][:, :], func=AF.Copy,
                                              bias=0.0, scale=scal[:, 9:10]), "sc"))

            @blk.vector
            def _(e, tc=tc):
                wait(e, "sc", dn_ev[tc])
                wait(e, f"A{tc % 2}", o_ld[tc])
                o_vv.append(inc(e.tensor_add(s_xt[:, tc % 2, :], s_xt[:, tc % 2, :],
                                             s_dot16[:, tc % 2, :]), "vv"))

            @blk.gpsimd
            def _(e, tc=tc):
                wait(e, "vv", o_vv[tc])
                o_gd.append(inc(e.dma_start(out=out_h[tc * 128:(tc + 1) * 128, :],
                                            in_=s_xt[:, tc % 2, :]), f"G{tc % 2}", 16))

        @blk.gpsimd
        def _(e):
            wait(e, "G0", val("G0"))
            wait(e, "G1", val("G1"))

        @blk.sync
        def _(e):
            wait(e, "G0", val("G0"))
            wait(e, "G1", val("G1"))

        freesb("s_dot16", "s_dwt2", "s_gate",
               "s_frs", "s_fs", "s_fq16", "s_fraw", "s_xt")
      except _Stop:
        pass

    return nc


def _consts():
    hsel = np.zeros((16, D), dtype=np.float16)
    for g in range(16):
        hsel[g, g * 64:(g + 1) * 64] = 1.0
    return hsel


def kernel(**inputs):
    nc = build_program()
    hsel = _consts()
    x = np.ascontiguousarray(np.asarray(inputs["L0_attn_x"], dtype=np.float32))
    base = {
        "c_hsel": hsel,
        "w_up": np.ascontiguousarray(np.asarray(inputs["L0_ffn_w_up"], dtype=np.float32)),
        "w_gate": np.ascontiguousarray(np.asarray(inputs["L0_ffn_w_gate"], dtype=np.float32)),
        "w_down": np.ascontiguousarray(np.asarray(inputs["L0_ffn_w_down"], dtype=np.float32)),
    }
    for nm in ("Wq", "Wk", "Wv", "Wo"):
        base[nm + "_w"] = np.ascontiguousarray(np.asarray(inputs[f"L0_attn_{nm}_w"], dtype=np.float32))
        base[nm + "_dW"] = np.ascontiguousarray(np.asarray(inputs[f"L0_attn_{nm}_dW"], dtype=np.float32))
        base[nm + "_dB"] = np.ascontiguousarray(np.asarray(inputs[f"L0_attn_{nm}_dB"], dtype=np.float32))
    in_maps = [dict(base, x=np.ascontiguousarray(x[i])) for i in range(NCORE)]
    res = run_bass_kernel_spmd(nc, in_maps, core_ids=list(range(NCORE)))
    return np.stack([res.results[i]["out"] for i in range(NCORE)], axis=0)
